# version 16
# baseline (speedup 1.0000x reference)
import numpy as np
from contextlib import ExitStack

import concourse.bass as bass
import concourse.mybir as mybir
from concourse.bass_utils import run_bass_kernel_spmd

F32 = mybir.dt.float32
BF16 = mybir.dt.bfloat16
AF = mybir.ActivationFunctionType
ALU = mybir.AluOpType
AX = mybir.AxisListType

B, L, D = 2, 8192, 1024
H, HD, C = 16, 64, 64
LAT, RD = 256, 32
NE, EI = 8, 2048
TB = 512
GROUPS = [[0, 1, 2, 3], [4, 5, 6, 7]]

ENGS = ("tensor", "vector", "scalar", "gpsimd", "sync")
_DMA_Q = {"sync": "dma_sy", "gpsimd": "dma_gp", "scalar": "dma_sc"}


class Op:
    __slots__ = ("i", "eng", "emit", "reads", "writes", "kind", "needs_inc",
                 "inc_no", "waits", "drain_before", "tag")

    def __init__(self, i, eng, emit, reads, writes, kind, tag):
        self.i, self.eng, self.emit = i, eng, emit
        self.reads, self.writes = reads, writes
        self.kind, self.tag = kind, tag
        self.needs_inc = False
        self.inc_no = None
        self.waits = []
        self.drain_before = False


class Flow:

    NQ = 8
    NCQ = 4

    def __init__(self, nc, ctx):
        self.nc = nc
        self.ops = []
        self.sems = {}
        for e in ENGS:
            self.sems["e_" + e] = ctx.enter_context(nc.semaphore("e_" + e))
        for q in ("dma_sy", "dma_gp", "dma_sc"):
            for i in range(self.NQ):
                self.sems[f"{q}{i}"] = ctx.enter_context(nc.semaphore(f"{q}{i}"))
        for i in range(self.NCQ):
            self.sems[f"coll{i}"] = ctx.enter_context(nc.semaphore(f"coll{i}"))
        self.sems["bar"] = ctx.enter_context(nc.semaphore("bar"))
        self.nbar = 0

    def op(self, eng, emit, reads=(), writes=(), kind="compute", tag=""):
        o = Op(len(self.ops), eng, emit, tuple(reads), tuple(writes), kind, tag)
        self.ops.append(o)
        return o

    def dma(self, eng, out, in_, reads=(), writes=(), tag="", **kw):
        def emit(e):
            return e.dma_start(out=out, in_=in_, **kw)
        return self.op(eng, emit, reads, writes, kind="dma", tag=tag)

    def dma_percore(self, eng, fn, reads=(), writes=(), tag=""):
        return self.op(eng, fn, reads, writes, kind="dma_mc", tag=tag)

    def coll(self, kind, alu, ins, outs, reads, writes, tag=""):
        def emit(g):
            return g.collective_compute(kind, alu, replica_groups=GROUPS,
                                        ins=ins, outs=outs)
        return self.op("gpsimd", emit, reads, writes, kind="coll", tag=tag)

    def barrier(self, tag=""):
        self.nbar += 1
        for e in ENGS:
            self.op(e, None, kind="barrier", tag=f"bar{self.nbar}:{tag}")

    def finalize(self, blk):
        ops = self.ops
        edges = [()] * len(ops)
        lw, rd = {}, {}
        seen_bar = 0
        for o in ops:
            if o.kind == "barrier":
                seen_bar += 1
                if seen_bar % len(ENGS) == 0:
                    lw, rd = {}, {}
                continue
            deps = set()
            for r in o.reads:
                if r in lw:
                    deps.add(lw[r])
            for w in o.writes:
                if w in lw:
                    deps.add(lw[w])
                deps.update(rd.get(w, ()))
            deps.discard(o.i)
            edges[o.i] = tuple(sorted(deps))
            for r in o.reads:
                rd.setdefault(r, []).append(o.i)
            for w in o.writes:
                lw[w] = o.i
                rd[w] = []
        for o in ops:
            for pi in edges[o.i]:
                pr = ops[pi]
                if pr.kind == "compute" and pr.eng != o.eng:
                    pr.needs_inc = True
        eng_inc = {e: 0 for e in ENGS}
        q_cnt = {"dma_sy": 0, "dma_gp": 0, "dma_sc": 0, "coll": 0}
        for o in ops:
            if o.kind == "compute" and o.needs_inc:
                eng_inc[o.eng] += 1
                o.inc_no = eng_inc[o.eng]
            elif o.kind in ("dma", "dma_mc"):
                q = _DMA_Q[o.eng]
                j = q_cnt[q]; q_cnt[q] += 1
                o.inc_no = (f"{q}{j % self.NQ}", 16 * (j // self.NQ + 1))
            elif o.kind == "coll":
                j = q_cnt["coll"]; q_cnt["coll"] += 1
                o.inc_no = (f"coll{j % self.NCQ}", j // self.NCQ + 1)
        vpos = {}
        _vp = 0
        for o in ops:
            if o.eng == "vector" and o.kind == "compute":
                vpos[o.i] = _vp
                _vp += 1
        wm = {e: {} for e in ENGS}
        last_drain = {e: -1 for e in ENGS}
        qmax = {e: {} for e in ENGS}
        for o in ops:
            if o.kind == "barrier":
                w = wm[o.eng]
                for s, v in sorted(qmax[o.eng].items()):
                    if w.get(s, 0) < v:
                        o.waits.append((s, v))
                        w[s] = v
                continue
            need = {}
            for pi in edges[o.i]:
                pr = ops[pi]
                if pr.kind == "compute":
                    if pr.eng == o.eng:
                        if pr.eng == "vector":
                            if (pr.i > last_drain[o.eng]
                                    and vpos.get(o.i, 0) - vpos.get(pr.i, 0) < 2):
                                o.drain_before = True
                        elif pr.eng != "tensor" and pr.i > last_drain[o.eng]:
                            o.drain_before = True
                    else:
                        need["e_" + pr.eng] = max(need.get("e_" + pr.eng, 0), pr.inc_no)
                elif pr.kind in ("dma", "dma_mc", "coll"):
                    s, v = pr.inc_no
                    need[s] = max(need.get(s, 0), v)
            if o.kind in ("dma", "dma_mc", "coll"):
                s, v = o.inc_no
                prev = v - (16 if o.kind != "coll" else 1)
                if prev > 0:
                    need[s] = max(need.get(s, 0), prev)
                qmax[o.eng][s] = v
            if o.drain_before:
                last_drain[o.eng] = o.i - 1
            w = wm[o.eng]
            for s, v in sorted(need.items()):
                if w.get(s, 0) < v:
                    o.waits.append((s, v))
                    w[s] = v
        per = {e: [o for o in ops if o.eng == e] for e in ENGS}
        flow = self

        def run(eng_obj, ename):
            bar_idx = 0
            for o in per[ename]:
                if o.kind == "barrier":
                    bar_idx += 1
                    for s, v in o.waits:
                        eng_obj.wait_ge(flow.sems[s], v)
                    if ename in ("vector", "scalar"):
                        eng_obj.drain()
                    if ename == "tensor":
                        eng_obj.nop()
                    eng_obj.sem_inc(flow.sems["bar"], 1)
                    eng_obj.wait_ge(flow.sems["bar"], bar_idx * len(ENGS))
                    continue
                for s, v in o.waits:
                    eng_obj.wait_ge(flow.sems[s], v)
                if o.drain_before:
                    eng_obj.drain()
                if o.kind == "dma_mc":
                    for arm in self.nc.sync.switch_core_id(n=128):
                        try:
                            core = arm.logical
                        except Exception:
                            core = int(arm) % 8
                        instr = o.emit(eng_obj, core)
                        instr.then_inc(flow.sems[o.inc_no[0]], 16)
                    continue
                instr = o.emit(eng_obj)
                if o.kind == "compute" and o.needs_inc:
                    instr.then_inc(flow.sems["e_" + ename], 1)
                elif o.kind == "dma":
                    instr.then_inc(flow.sems[o.inc_no[0]], 16)
                elif o.kind == "coll":
                    instr.then_inc(flow.sems[o.inc_no[0]], 1)

        @blk.tensor
        def _(t):
            run(t, "tensor")

        @blk.vector
        def _(v):
            run(v, "vector")

        @blk.scalar
        def _(s):
            run(s, "scalar")

        @blk.gpsimd
        def _(g):
            run(g, "gpsimd")

        @blk.sync
        def _(sy):
            run(sy, "sync")


def fm3(ap):
    return ap.rearrange("(kc p) t -> p kc t", p=128)


def build(seq=L, dbg=(), stop_after="p6", scan_steps=None, scan_groups=None):
    dbg = set(dbg)
    nto = seq // 4
    nch = seq // C
    ntb_o = nto // TB
    ntb_s = seq // TB
    nc = bass.Bass("TRN2", target_bir_lowering=False, debug=False)
    T = {}

    def din(name, shape, dt=F32):
        ap = nc.dram_tensor(name, shape, dt, kind="ExternalInput").ap()
        T[name] = ap
        return ap

    def dint(name, shape, dt=BF16):
        ap = nc.dram_tensor(name, shape, dt, kind="Internal").ap()
        T[name] = ap
        return ap

    din("x_own", [D, nto])
    din("wqT", [D, 256]); din("wkT", [D, 256]); din("wvT", [D, 256])
    din("qkv_b", [128, 2, 3])
    din("lrwT", [D, 4]); din("lr_b", [4, 1])
    din("ttt_g2", [128, 2]); din("ttt_b2", [128, 2])
    din("ttt_g_tm", [64, 256]); din("ttt_b_tm", [64, 256])
    din("W2_0", [128, 128]); din("W2_1", [128, 128]); din("b2_bc", [64, 256])
    din("post_nw", [128, 8]); din("post_nb", [128, 8])
    din("woT", [D, D]); din("wo_b", [128, 8])
    din("ln1_w", [128, 8]); din("ln2_w", [128, 8])
    din("kv_dT", [D, LAT]); din("q_dT", [D, LAT])
    din("k_uT", [LAT, 128]); din("q_uT", [LAT, 128]); din("v_uT", [LAT, 256])
    din("rope_qT", [LAT, 128]); din("rope_kT", [D, 128])
    din("o_wT", [D, D])
    din("cos_t", [128, seq]); din("sin_t", [128, seq])
    din("r_wT", [D, 8]); din("r_b", [1, 8])
    din("moe_gT", [NE, D, EI]); din("moe_uT", [NE, D, EI]); din("moe_dT", [NE, EI, D])
    din("ident_f", [128, 128])
    din("ident_b", [128, 128], BF16)
    din("triuT_b", [64, 64], BF16)
    din("i64z", [128, 64], BF16)
    din("triuTz", [128, 64], BF16)
    din("ones64", [64, 64], BF16)
    din("trimask4", [64, 256])
    din("e2_b", [128, 2], BF16)
    din("eb2_b", [2, 128], BF16)
    din("onesr_b", [1, 128], BF16)
    din("onesr_f", [1, 128])
    din("onescol_b", [128, 1], BF16)
    din("wmask", [128, 128])
    din("amask", [128, 4, TB], BF16)
    din("r4T", [128, 128])
    din("iota7", [128, 8])
    din("esel", [8, 1024], BF16)
    din("epsv", [64, 8])

    T["out"] = nc.dram_tensor("out", [D, nto], F32, kind="ExternalOutput").ap()

    dbg_out = {}

    def dout(name, shape, dt=F32):
        if name in dbg:
            dbg_out[name] = nc.dram_tensor("dbg_" + name, shape, dt,
                                           kind="ExternalOutput").ap()
        return dbg_out.get(name)
    T["dout"] = dout

    NCB_O = nto // TB
    NCB_T = seq // 2048 if seq >= 2048 else 1
    CW = seq // NCB_T
    for c in range(NCB_O):
        dint(f"h1_pub{c}", [D, TB]); dint(f"h1_gat{c}", [4 * D, TB])
        dint(f"a_pub{c}", [D, TB]); dint(f"a_gat{c}", [4 * D, TB])
    for c in range(NCB_T):
        dint(f"ymix_pub{c}", [256, CW]); dint(f"ymix_gat{c}", [1024, CW])
        dint(f"yat_pub{c}", [256, CW]); dint(f"yat_gat{c}", [1024, CW])
    T.update(NCB_O=NCB_O, NCB_T=NCB_T, CW=CW)
    dint("x2_spill", [D, nto], F32); dint("y0_spill", [D, nto], F32)
    dint("y1_spill", [D, nto], F32)
    dint("rot_sp", [256, seq])

    T.update(seq=seq, nto=nto, nch=nch, ntb_o=ntb_o, ntb_s=ntb_s)

    with ExitStack() as root:
        f = Flow(nc, root)
        T["f"] = f
        cst = {}
        for nm, shape, dt in [("ident_f", [128, 128], F32), ("ident_b", [128, 128], BF16),
                              ("triuT_b", [64, 64], BF16), ("i64z", [128, 64], BF16), ("triuTz", [128, 64], BF16), ("ones64", [64, 64], BF16),
                              ("trimask4", [64, 256], F32), ("e2_b", [128, 2], BF16),
                              ("eb2_b", [2, 128], BF16), ("onesr_b", [1, 128], BF16),
                              ("onesr_f", [1, 128], F32), ("onescol_b", [128, 1], BF16),
                              ("wmask", [128, 128], F32), ("qkv_b", [128, 2, 3], F32),
                              ("ttt_g2", [128, 2], F32), ("ttt_b2", [128, 2], F32),
                              ("ttt_g_tm", [64, 256], F32), ("ttt_b_tm", [64, 256], F32),
                              ("lr_b", [4, 1], F32), ("r4T", [128, 128], F32),
                              ("iota7", [128, 8], F32), ("r_b", [1, 8], F32), ("esel", [8, 1024], BF16),
                              ("epsv", [64, 8], F32), ("amask", [128, 4, TB], BF16),
                              ("post_nw", [128, 8], F32),
                              ("post_nb", [128, 8], F32), ("wo_b", [128, 8], F32),
                              ("ln1_w", [128, 8], F32), ("ln2_w", [128, 8], F32)]:
            cst[nm] = root.enter_context(nc.sbuf_tensor("c_" + nm, shape, dt))
            f.dma("sync", cst[nm][:], T[nm], writes=("c_" + nm,), tag="ldc:" + nm)
        T["cst"] = cst
        cst["lrbh"] = root.enter_context(nc.sbuf_tensor("c_lrbh", [4, 1], F32))
        f.op("scalar", lambda s: s.mul(cst["lrbh"][:], cst["lr_b"][:], 0.5),
             reads=("c_lr_b",), writes=("c_lrbh",), tag="lrbh")
        f.barrier("consts")

        with ExitStack() as ph0:
            emit_p0(f, nc, T, ph0)
        if stop_after != "p0":
            with ExitStack() as ph:
                T["XQb"] = ph.enter_context(nc.sbuf_tensor("XQb", [128, 2, seq], BF16))
                T["XKb"] = ph.enter_context(nc.sbuf_tensor("XKb", [128, 2, seq], BF16))
                T["tgt_fm"] = ph.enter_context(nc.sbuf_tensor("tgt_fm", [128, 2, seq], F32))
                T["etaT"] = ph.enter_context(nc.sbuf_tensor("etaT", [64, nch * 4], F32))
                with ExitStack() as ph2:
                    emit_p1a(f, nc, T, ph2)
                f.barrier("p1a")
                if stop_after != "p1a":
                    T["scan_steps"] = scan_steps
                    T["scan_groups"] = scan_groups
                    with ExitStack() as ph2:
                        emit_p1b(f, nc, T, ph2)
                    f.barrier("p1b")
        if stop_after not in ("p0", "p1a", "p1b"):
            with ExitStack() as ph:
                emit_p2(f, nc, T, ph)
            f.barrier("p2")
        if stop_after not in ("p0", "p1a", "p1b", "p2"):
            with ExitStack() as ph:
                emit_p3(f, nc, T, ph)
            f.barrier("p3")
        if stop_after not in ("p0", "p1a", "p1b", "p2", "p3"):
            with ExitStack() as ph:
                T["xm"] = ph.enter_context(nc.sbuf_tensor("xm", [128, 8, nto], BF16))
                with ExitStack() as ph2:
                    emit_p4(f, nc, T, ph2)
                f.barrier("p4")
                T["accum"] = ph.enter_context(nc.sbuf_tensor("accum", [128, 8, nto], F32))
                with ExitStack() as ph2:
                    emit_p5(f, nc, T, ph2)
                f.barrier("p5")
                with ExitStack() as ph2:
                    emit_p6(f, nc, T, ph2)
        f.barrier("final")
        f.finalize(blk=root.enter_context(nc.Block()))
    return nc, dbg_out


def emit_p0(f, nc, T, ctx):
    cst = T["cst"]; nto = T["nto"]; ntb_o = T["ntb_o"]
    dout = T["dout"]
    sb = lambda n, s, dt=F32: ctx.enter_context(nc.sbuf_tensor(n, s, dt))
    ps = lambda n, s, dt=F32: ctx.enter_context(nc.psum_tensor(n, s, dt))

    xf = sb("xf", [128, 8, nto])
    hb = sb("hb", [128, 8, nto], BF16)
    T["xf_p0"] = xf
    xb = sb("xb0", [128, 8, TB], BF16)
    sqb = sb("sqb0", [128, 8, TB], BF16)
    srow = sb("srow0", [1, 4 * TB])
    srow_b = sb("srowb0", [1, 2 * TB], BF16)
    scf = sb("scf0", [128, 8, TB])
    p_sum = ps("p_sum0", [1, TB])
    p_ss = ps("p_ss0", [1, TB])
    p_bc = ps("p_bc0", [128, 2 * TB])

    f.dma("sync", xf[:], fm3(T["x_own"]), writes=("xf",), tag="p0:ldx")

    for tb in range(ntb_o):
        sl = slice(tb * TB, (tb + 1) * TB)
        k = f"p0.{tb}"
        f.op("vector", lambda v, sl=sl: v.tensor_copy(xb[:], xf[:, :, sl]),
             reads=("xf",), writes=("xb",), tag=k + ":cast")
        f.op("scalar", lambda s: s.activation(sqb[:], xb[:], AF.Square),
             reads=("xb",), writes=("sqb",), tag=k + ":sq")

        def mm_sum(t):
            last = None
            for kc in range(8):
                last = t.matmul(p_sum[0:1, :], cst["onescol_b"][:], xb[:, kc, :],
                                start=(kc == 0), stop=(kc == 7))
            return last
        f.op("tensor", mm_sum, reads=("xb",), writes=("p_sum",), tag=k + ":mms")

        def mm_ss(t):
            last = None
            for kc in range(8):
                last = t.matmul(p_ss[0:1, :], cst["onescol_b"][:], sqb[:, kc, :],
                                start=(kc == 0), stop=(kc == 7))
            return last
        f.op("tensor", mm_ss, reads=("sqb",), writes=("p_ss",), tag=k + ":mmss")

        def stats(v):
            v.tensor_scalar(out=srow[0:1, 0:TB], in0=p_sum[0:1, :], scalar1=1.0 / D,
                            scalar2=None, op0=ALU.mult)
            v.tensor_scalar(out=srow[0:1, 3 * TB:4 * TB], in0=p_ss[0:1, :],
                            scalar1=1.0 / D, scalar2=None, op0=ALU.mult)
            v.drain()
            v.tensor_tensor(srow[0:1, TB:2 * TB], srow[0:1, 0:TB], srow[0:1, 0:TB], ALU.mult)
            v.drain()
            v.tensor_tensor(srow[0:1, TB:2 * TB], srow[0:1, 3 * TB:4 * TB],
                            srow[0:1, TB:2 * TB], ALU.subtract)
            v.drain()
            return v.nop()
        f.op("vector", stats, reads=("p_sum", "p_ss"), writes=("srow",), tag=k + ":st")

        def rstd(s):
            s.activation(srow[0:1, 2 * TB:3 * TB], srow[0:1, TB:2 * TB], AF.Ln,
                         bias=cst['epsv'][0:1, 0:1])
            s.drain()
            s.activation(srow[0:1, 2 * TB:3 * TB], srow[0:1, 2 * TB:3 * TB], AF.Exp, scale=-0.5)
            s.drain()
            return s.nop()
        f.op("scalar", rstd, reads=("srow",), writes=("srow.r",), tag=k + ":rstd")

        def tobf(v):
            v.tensor_copy(srow_b[0:1, 0:TB], srow[0:1, 0:TB])
            v.tensor_copy(srow_b[0:1, TB:2 * TB], srow[0:1, 2 * TB:3 * TB])
            v.drain()
            return v.nop()
        f.op("vector", tobf, reads=("srow", "srow.r"), writes=("srow_b",), tag=k + ":bf")

        def bcast(t):
            t.matmul(p_bc[:, 0:TB], cst["onesr_b"][:], srow_b[0:1, 0:TB], start=True, stop=True)
            return t.matmul(p_bc[:, TB:2 * TB], cst["onesr_b"][:], srow_b[0:1, TB:2 * TB],
                            start=True, stop=True)
        f.op("tensor", bcast, reads=("srow_b",), writes=("p_bc",), tag=k + ":bc")

        def norm(v, sl=sl):
            v.tensor_tensor(scf[:], xf[:, :, sl],
                            p_bc[:, 0:TB].unsqueeze(1).broadcast_to([128, 8, TB]), ALU.subtract)
            v.drain()
            v.tensor_tensor(hb[:, :, sl], scf[:],
                            p_bc[:, TB:2 * TB].unsqueeze(1).broadcast_to([128, 8, TB]), ALU.mult)
            v.drain()
            return v.nop()
        f.op("vector", norm, reads=("xf", "p_bc"), writes=("hb", "scf", "xb"),
             tag=k + ":norm")

    for tb in range(ntb_o):
        sl = slice(tb * TB, (tb + 1) * TB)
        f.dma("sync", fm3(T[f"h1_pub{tb}"]), hb[:, :, sl], reads=("hb",),
              writes=(f"h1_pub{tb}",), tag=f"p0:pub{tb}")

    d = dout("h1gat", [4 * D, nto], BF16)
    if d is not None:
        for c in range(T["NCB_O"]):
            f.coll("AllGather", ALU.bypass, [T[f"h1_pub{c}"].opt()],
                   [T[f"h1_gat{c}"].opt()], reads=(f"h1_pub{c}",),
                   writes=(f"h1_gat{c}",), tag=f"p0:ag{c}")
            f.dma("sync", d[:, c * TB:(c + 1) * TB], T[f"h1_gat{c}"],
                  reads=(f"h1_gat{c}",), tag=f"p0:dbg{c}")
        T["_h1_coll_done"] = True
    f.barrier("p0")


def emit_p1a(f, nc, T, ctx):
    cst = T["cst"]; seq = T["seq"]; nto = T["nto"]; ntb = T["ntb_s"]
    sb = lambda n, s, dt=F32: ctx.enter_context(nc.sbuf_tensor(n, s, dt))
    ps = lambda n, s, dt=F32: ctx.enter_context(nc.psum_tensor(n, s, dt))
    XQb, XKb, tgt_fm, etaT = T["XQb"], T["XKb"], T["tgt_fm"], T["etaT"]

    wq_sb = sb("wq_sb", [128, 8, 256], BF16)
    wk_sb = sb("wk_sb", [128, 8, 256], BF16)
    wv_sb = sb("wv_sb", [128, 8, 256], BF16)
    lrw_sb = sb("lrw_sb", [128, 8, 4], BF16)
    _hg0 = sb("hg0", [128, 8, TB], BF16)
    hg = [_hg0, _hg0]
    praw = sb("praw", [128, 2, TB])
    xkf = sb("xkf", [128, 2, TB])
    sqb = sb("sqb1", [128, 2, TB], BF16)
    dvec = sb("dvec", [128, 2, TB])
    dvb = sb("dvb", [128, 2, TB], BF16)
    st = sb("st_p1", [2, 6 * TB])
    stb = sb("stb_p1", [2, 6 * TB], BF16)
    ssc = sb("ssc_p1", [2, 2 * TB])
    etrow = sb("etrow", [4, TB])
    p_q = ps("p_q", [128, TB])
    p_b1 = ps("p_b1", [128, 2 * TB])
    p_mix = ps("p_mix", [64, 4 * TB])
    p_eta = ps("p_eta", [4, TB])

    r3 = lambda ap: ap.rearrange("(kc p) m -> p kc m", p=128)
    f.dma("gpsimd", wq_sb[:], r3(T["wqT"]), writes=("wq_sb",), tag="p1:ldwq")
    f.dma("gpsimd", wk_sb[:], r3(T["wkT"]), writes=("wk_sb",), tag="p1:ldwk")
    f.dma("gpsimd", wv_sb[:], r3(T["wvT"]), writes=("wv_sb",), tag="p1:ldwv")
    f.dma("gpsimd", lrw_sb[:], r3(T["lrwT"]), writes=("lrw_sb",), tag="p1:ldlr")

    if not T.get("_h1_coll_done"):
        for c in range(T["NCB_O"]):
            f.coll("AllGather", ALU.bypass, [T[f"h1_pub{c}"].opt()],
                   [T[f"h1_gat{c}"].opt()], reads=(f"h1_pub{c}",),
                   writes=(f"h1_gat{c}",), tag=f"p1:ag{c}")

    RN_K = 0; MU_D = 2; SD_D = 4; RN_Q = 0

    def proj(t, wsb, mc, par, extra_sq_bias=None):
        last = None
        for kc in range(8):
            last = t.matmul(p_q[:], wsb[:, kc, 128 * mc:128 * mc + 128],
                            hg[par][:, kc, :], start=(kc == 0), stop=(kc == 7))
        return last

    for gtb in range(ntb):
        par = 0
        sl = slice(gtb * TB, (gtb + 1) * TB)
        k = f"p1.{gtb}"
        t0 = gtb * TB
        r = t0 // nto
        loc = t0 - r * nto
        ch = loc // TB
        src = T[f"h1_gat{ch}"][r * 1024:(r + 1) * 1024, :] \
            .rearrange("(kc p) t -> p kc t", p=128)
        f.dma("sync", hg[par][:], src, reads=(f"h1_gat{ch}",), writes=(f"hg{par}",),
              tag=k + ":ld")

        for mc in range(2):
            f.op("tensor", lambda t, mc=mc, par=par: proj(t, wk_sb, mc, par),
                 reads=(f"hg{par}", "wk_sb"), writes=("p_q",), tag=k + f":kmm{mc}")

            f.op("scalar", lambda s, mc=mc: s.activation(
                praw[:, mc, :], p_q[:], AF.Identity, bias=cst["qkv_b"][:, mc, 1:2]),
                reads=("p_q",), writes=(f"praw{mc}",), tag=k + f":kraw{mc}")
            f.op("vector", lambda v, mc=mc: v.tensor_tensor(
                sqb[:, mc, :], praw[:, mc, :], praw[:, mc, :], ALU.mult),
                reads=(f"praw{mc}", f"sqb{mc}"), writes=(f"sqb{mc}",),
                tag=k + f":krsq{mc}")
            f.op("tensor", lambda t, mc=mc: t.matmul(
                p_mix[0:2, mc * TB:mc * TB + TB], cst["e2_b"][:], sqb[:, mc, :],
                start=True, stop=True),
                reads=(f"sqb{mc}",), writes=(f"p_mix{mc}",), tag=k + f":kss{mc}")

        def k_lnexp(s):
            s.activation(st[0:2, RN_K * TB:(RN_K + 2) * TB], p_mix[0:2, 0:2 * TB],
                         AF.Ln, bias=cst["epsv"][0:2, 2:3])
            s.drain()
            s.activation(stb[0:2, RN_K * TB:(RN_K + 2) * TB],
                         st[0:2, RN_K * TB:(RN_K + 2) * TB], AF.Exp, scale=-0.5)
            s.drain()
            return s.nop()
        f.op("scalar", k_lnexp, reads=("p_mix0", "p_mix1", "stb_k"),
             writes=("st_k", "stb_k"), tag=k + ":klnexp")

        f.op("tensor", lambda t: (
            t.matmul(p_b1[:, 0:TB], cst["eb2_b"][:], stb[0:2, RN_K * TB:(RN_K + 1) * TB],
                     start=True, stop=True),
            t.matmul(p_b1[:, TB:2 * TB], cst["eb2_b"][:],
                     stb[0:2, (RN_K + 1) * TB:(RN_K + 2) * TB], start=True, stop=True))[-1],
            reads=("stb_k",), writes=("p_b1",), tag=k + ":kbc")

        def k_norm(v, sl=sl):
            for mc in range(2):
                v.tensor_tensor(xkf[:, mc, :], praw[:, mc, :],
                                p_b1[:, mc * TB:mc * TB + TB], ALU.mult)
            v.drain()
            v.tensor_copy(XKb[:, :, sl], xkf[:])
            v.drain()
            return v.nop()
        f.op("vector", k_norm, reads=("praw0", "praw1", "p_b1"),
             writes=("xkf", "XKb"), tag=k + ":knorm")

        for mc in range(2):
            f.op("tensor", lambda t, mc=mc, par=par: proj(t, wv_sb, mc, par),
                 reads=(f"hg{par}", "wv_sb", f"praw{mc}"), writes=("p_q",),
                 tag=k + f":vmm{mc}")
            f.op("scalar", lambda s, mc=mc: s.activation(
                praw[:, mc, :], p_q[:], AF.Identity, bias=cst["qkv_b"][:, mc, 2:3]),
                reads=("p_q",), writes=(f"praw{mc}",), tag=k + f":vraw{mc}")

        def dcalc(v):
            for mc in range(2):
                v.tensor_tensor(dvec[:, mc, :], praw[:, mc, :], xkf[:, mc, :], ALU.subtract)
            v.drain()
            v.tensor_copy(dvb[:], dvec[:])
            v.drain()
            return v.nop()
        f.op("vector", dcalc, reads=("praw0", "praw1", "xkf"),
             writes=("dvec", "dvb"), tag=k + ":dcalc")

        f.op("scalar", lambda s: s.activation(sqb[:], dvec[:], AF.Square),
             reads=("dvec",), writes=("sqb0", "sqb1"), tag=k + ":dsq")

        def dsums(t):
            last = None
            for mc in range(2):
                t.matmul(p_mix[0:2, mc * TB:mc * TB + TB], cst["e2_b"][:],
                         dvb[:, mc, :], start=True, stop=True)
            for mc in range(2):
                last = t.matmul(p_mix[0:2, (2 + mc) * TB:(2 + mc) * TB + TB],
                                cst["e2_b"][:], sqb[:, mc, :], start=True, stop=True)
            return last
        f.op("tensor", dsums, reads=("dvb", "sqb0", "sqb1"),
             writes=("p_mix0", "p_mix1", "p_mix2", "p_mix3"), tag=k + ":dsums")

        def dstat(v):
            v.tensor_scalar(out=st[0:2, MU_D * TB:(MU_D + 2) * TB],
                            in0=p_mix[0:2, 0:2 * TB], scalar1=1.0 / HD,
                            scalar2=None, op0=ALU.mult)
            v.drain()
            v.tensor_tensor(ssc[0:2, :], p_mix[0:2, 0:2 * TB],
                            st[0:2, MU_D * TB:(MU_D + 2) * TB], ALU.mult)
            v.drain()
            v.tensor_tensor(ssc[0:2, :], p_mix[0:2, 2 * TB:4 * TB], ssc[0:2, :],
                            ALU.subtract)
            v.drain()
            v.tensor_scalar(out=st[0:2, SD_D * TB:(SD_D + 2) * TB], in0=ssc[0:2, :],
                            scalar1=1.0 / (HD - 1), scalar2=None, op0=ALU.mult)
            v.drain()
            v.tensor_copy(stb[0:2, MU_D * TB:(MU_D + 2) * TB],
                          st[0:2, MU_D * TB:(MU_D + 2) * TB])
            v.drain()
            return v.nop()
        f.op("vector", dstat, reads=("p_mix0", "p_mix1", "p_mix2", "p_mix3"),
             writes=("st_mu", "stb_mu", "ssc"), tag=k + ":dstat")

        def dsd(s):
            rng = slice(SD_D * TB, (SD_D + 2) * TB)
            s.activation(st[0:2, rng], st[0:2, rng], AF.Ln)
            s.drain()
            s.activation(st[0:2, rng], st[0:2, rng], AF.Exp, scale=0.5)
            s.drain()
            return s.nop()
        f.op("scalar", dsd, reads=("st_mu",), writes=("st_sd",), tag=k + ":dsd")

        def dr2s(s):
            rng = slice(SD_D * TB, (SD_D + 2) * TB)
            s.activation(st[0:2, rng], st[0:2, rng], AF.Ln,
                         bias=cst["epsv"][0:2, 0:1])
            s.drain()
            s.activation(stb[0:2, rng], st[0:2, rng], AF.Exp, scale=-1.0)
            s.drain()
            return s.nop()
        f.op("scalar", dr2s, reads=("st_sd", "stb_r2"), writes=("st_sd", "stb_r2"),
             tag=k + ":dr2s")

        f.op("tensor", lambda t: (
            t.matmul(p_b1[:, 0:TB], cst["eb2_b"][:], stb[0:2, MU_D * TB:(MU_D + 1) * TB],
                     start=True, stop=True),
            t.matmul(p_b1[:, TB:2 * TB], cst["eb2_b"][:],
                     stb[0:2, (MU_D + 1) * TB:(MU_D + 2) * TB], start=True, stop=True))[-1],
            reads=("stb_mu", "p_b1"), writes=("p_b1",), tag=k + ":mubc")

        def dsub(v):
            for mc in range(2):
                v.tensor_tensor(dvec[:, mc, :], dvec[:, mc, :],
                                p_b1[:, mc * TB:mc * TB + TB], ALU.subtract)
            v.drain()
            return v.nop()
        f.op("vector", dsub, reads=("dvec", "p_b1"), writes=("dvec",), tag=k + ":dsub")

        f.op("tensor", lambda t: (
            t.matmul(p_b1[:, 0:TB], cst["eb2_b"][:], stb[0:2, SD_D * TB:(SD_D + 1) * TB],
                     start=True, stop=True),
            t.matmul(p_b1[:, TB:2 * TB], cst["eb2_b"][:],
                     stb[0:2, (SD_D + 1) * TB:(SD_D + 2) * TB], start=True, stop=True))[-1],
            reads=("stb_r2", "dvec", "p_b1"), writes=("p_b1",), tag=k + ":r2bc")

        def dmul(v, sl=sl):
            for mc in range(2):
                v.tensor_tensor(tgt_fm[:, mc, sl], dvec[:, mc, :],
                                p_b1[:, mc * TB:mc * TB + TB], ALU.mult)
            v.drain()
            return v.nop()
        f.op("vector", dmul, reads=("dvec", "p_b1"), writes=("tgt_fm", "dvec"),
             tag=k + ":dmul")

        for mc in range(2):
            f.op("tensor", lambda t, mc=mc, par=par: proj(t, wq_sb, mc, par),
                 reads=(f"hg{par}", "wq_sb", f"praw{mc}", "dvec"), writes=("p_q",),
                 tag=k + f":qmm{mc}")

            f.op("scalar", lambda s, mc=mc: s.activation(
                praw[:, mc, :], p_q[:], AF.Identity, bias=cst["qkv_b"][:, mc, 0:1]),
                reads=("p_q",), writes=(f"praw{mc}",), tag=k + f":qraw{mc}")
            f.op("vector", lambda v, mc=mc: v.tensor_tensor(
                sqb[:, mc, :], praw[:, mc, :], praw[:, mc, :], ALU.mult),
                reads=(f"praw{mc}", f"sqb{mc}"), writes=(f"sqb{mc}",),
                tag=k + f":qrsq{mc}")
            f.op("tensor", lambda t, mc=mc: t.matmul(
                p_mix[0:2, mc * TB:mc * TB + TB], cst["e2_b"][:], sqb[:, mc, :],
                start=True, stop=True),
                reads=(f"sqb{mc}", "st_mu"), writes=(f"p_mix{mc}",), tag=k + f":qss{mc}")

        def q_lnexp(s):
            s.activation(st[0:2, RN_Q * TB:(RN_Q + 2) * TB], p_mix[0:2, 0:2 * TB],
                         AF.Ln, bias=cst["epsv"][0:2, 2:3])
            s.drain()
            s.activation(stb[0:2, RN_Q * TB:(RN_Q + 2) * TB],
                         st[0:2, RN_Q * TB:(RN_Q + 2) * TB], AF.Exp, scale=-0.5)
            s.drain()
            return s.nop()
        f.op("scalar", q_lnexp, reads=("p_mix0", "p_mix1", "stb_q"),
             writes=("st_q", "stb_q"), tag=k + ":qlnexp")

        f.op("tensor", lambda t: (
            t.matmul(p_b1[:, 0:TB], cst["eb2_b"][:], stb[0:2, RN_Q * TB:(RN_Q + 1) * TB],
                     start=True, stop=True),
            t.matmul(p_b1[:, TB:2 * TB], cst["eb2_b"][:],
                     stb[0:2, (RN_Q + 1) * TB:(RN_Q + 2) * TB], start=True, stop=True))[-1],
            reads=("stb_q", "dvec", "p_b1"), writes=("p_b1",), tag=k + ":qbc")

        def q_norm(v, sl=sl):
            for mc in range(2):
                v.tensor_tensor(XQb[:, mc, sl], praw[:, mc, :],
                                p_b1[:, mc * TB:mc * TB + TB], ALU.mult)
            v.drain()
            return v.nop()
        f.op("vector", q_norm, reads=("praw0", "praw1", "p_b1"),
             writes=("XQb",), tag=k + ":qnorm")

        def etamm(t, par=par):
            last = None
            for kc in range(8):
                last = t.matmul(p_eta[:], lrw_sb[:, kc, :], hg[par][:, kc, :],
                                start=(kc == 0), stop=(kc == 7))
            return last
        f.op("tensor", etamm, reads=(f"hg{par}", "lrw_sb"), writes=("p_eta",),
             tag=k + ":etamm")
        f.op("scalar", lambda s: s.activation(etrow[:], p_eta[:], AF.Tanh, scale=0.5,
                                              bias=cst["lrbh"][:]),
             reads=("p_eta", "c_lrbh"), writes=("etrow",), tag=k + ":etatanh")
        f.op("vector", lambda v: v.tensor_scalar(
            out=etrow[:], in0=etrow[:], scalar1=-1.0 / 8192.0, scalar2=-1.0 / 8192.0,
            op0=ALU.mult, op1=ALU.add),
            reads=("etrow",), writes=("etrow",), tag=k + ":etasc")

        def etat(t):
            last = None
            for ch in range(8):
                last = t.transpose(p_mix[0:64, ch * 4:ch * 4 + 4],
                                   etrow[:, ch * 64:ch * 64 + 64], cst["ident_f"][0:4, 0:4])
            return last
        f.op("tensor", etat, reads=("etrow", "p_mix0"), writes=("p_mix0",),
             tag=k + ":etat")
        f.op("vector", lambda v, gtb=gtb: v.tensor_copy(
            etaT[:, gtb * 32:gtb * 32 + 32], p_mix[0:64, 0:32]),
            reads=("p_mix0",), writes=("etaT", f"hg{par}"), tag=k + ":etaev")

    d = T["dout"]("xqb", [128, 2, seq], BF16)
    if d is not None:
        f.dma("sync", d, XQb[:], reads=("XQb",), tag="dbg:xqb")
    d = T["dout"]("xkb", [128, 2, seq], BF16)
    if d is not None:
        f.dma("sync", d, XKb[:], reads=("XKb",), tag="dbg:xkb")
    d = T["dout"]("tgtfm", [128, 2, seq])
    if d is not None:
        f.dma("sync", d, tgt_fm[:], reads=("tgt_fm",), tag="dbg:tgt")
    d = T["dout"]("etat", [64, T["nch"] * 4])
    if d is not None:
        f.dma("sync", d, etaT[:], reads=("etaT",), tag="dbg:eta")


def emit_p1b(f, nc, T, ctx):
    cst = T["cst"]; seq = T["seq"]; nch = T["nch"]
    if T.get("scan_steps"):
        nch = min(nch, T["scan_steps"])
    sb = lambda n, s, dt=F32: ctx.enter_context(nc.sbuf_tensor(n, s, dt))
    ps = lambda n, s, dt=F32: ctx.enter_context(nc.psum_tensor(n, s, dt))
    XQb, XKb, tgt_fm, etaT = T["XQb"], T["XKb"], T["tgt_fm"], T["etaT"]

    W2 = sb("W2", [128, 256]); W2b = sb("W2b", [128, 256], BF16)
    b2 = sb("b2", [64, 256]); b2b = sb("b2b", [128, 256], BF16)
    gneg = sb("gneg", [128, 256], BF16)
    at_m = sb("at_m", [128, 256], BF16)
    xk_tm = sb("xk_tm", [64, 256], BF16)
    sq_tm = sb("sq_tm", [64, 256])
    xh = sb("xh", [64, 256])
    t1 = sb("t1s", [64, 256])
    t2 = sb("t2s", [64, 256])
    o_ln = sb("o_ln", [64, 256])
    wsc = sb("wsc", [128, 256])
    stt = sb("stt", [64, 32])
    XO16 = sb("XO16", [128, 2, 1024], BF16)
    z1f = sb("z1f", [64, 256])
    zbf = sb("zbf", [64, 256])
    p_tm = ps("p_tm", [64, 256], BF16)
    psA = ps("psA", [64, 384])
    psB = ps("psB", [64, 384])
    p_atA = ps("p_atA", [64, 128])
    p_atB = ps("p_atB", [64, 128])
    p_wd = ps("p_wd", [128, 256])
    p_ot = ps("p_ot", [128, 384])
    p_z = psA
    p_tg = psB

    f.dma("sync", W2[:, 0:128], T["W2_0"], writes=("W2:0",), tag="sc:ldW0")
    f.dma("sync", W2[:, 128:256], T["W2_1"], writes=("W2:1",), tag="sc:ldW1")
    f.dma("sync", b2[:], T["b2_bc"], writes=("b2",), tag="sc:ldb")

    g2_tm = sb("g2_tm", [64, 256])

    def initcast(v):
        v.tensor_tensor(g2_tm[:], cst["ttt_g_tm"][:], cst["ttt_g_tm"][:], ALU.mult)
        v.drain()
        v.tensor_copy(W2b[:], W2[:])
        v.memset(b2b[64:128, :], 0.0)
        v.memset(gneg[64:128, :], 0.0)
        v.memset(at_m[64:128, :], 0.0)
        v.drain()
        v.tensor_copy(b2b[0:64, :], b2[:])
        v.drain()
        return v.nop()
    f.op("vector", initcast, reads=("W2:0", "W2:1", "b2"),
         writes=("W2b", "b2b", "gneg", "at_m"), tag="sc:initcast")

    z3 = lambda ap: ap.rearrange("p (h e) -> p h e", e=64)
    bc3 = lambda col: stt[:, col:col + 4].unsqueeze(2).broadcast_to([64, 4, 64])
    S_SUM, S_SS, S_MU, S_RSTD, S_S1, S_S2, S_SCL, S_TMP = 0, 4, 8, 12, 16, 20, 24, 28

    GG = T.get("scan_groups")
    realop = f.op
    realdma = f.dma
    import re as _re
    def gate(tag):
        m = _re.search(r":([a-z0-9]+)$", tag)
        grp = m.group(1) if m else tag
        return GG is None or grp in GG
    def gop(eng, emit, reads=(), writes=(), kind="compute", tag=""):
        if gate(tag):
            return realop(eng, emit, reads, writes, kind=kind, tag=tag)
    fop_save = f.op

    class _G:
        def op(self, *a, **k):
            if gate(k.get("tag", a[-1] if a else "")):
                return realop(*a, **k)
            return None
        def dma(self, *a, **k):
            if gate(k.get("tag", "")):
                return realdma(*a, **k)
            return None
        def __getattr__(self, n):
            return getattr(f, n)
    fg = _G()

    stt2 = sb("stt2", [64, 32])
    sqo = sb("sqo", [64, 256])
    zbf2 = [zbf, sb("zbfB", [64, 256])]
    bc32 = lambda col: stt2[:, col:col + 4].unsqueeze(2).broadcast_to([64, 4, 64])

    def zrstd2(s):
        s.activation(stt2[:, S_RSTD:S_RSTD + 4], stt2[:, S_TMP:S_TMP + 4], AF.Ln,
                     bias=cst["epsv"][:, 1:2])
        s.drain()
        s.activation(stt2[:, S_RSTD:S_RSTD + 4], stt2[:, S_RSTD:S_RSTD + 4],
                     AF.Exp, scale=-0.5)
        s.drain()
        return s.nop()

    def mk_tail(n):
        k = f"sc{n}"
        sl = slice(n * C, (n + 1) * C)
        blkidx = n % 16
        zp = n % 2
        zbb = zbf2[zp]
        ZR = f"zbf{zp}"
        th = {}

        th["osq"] = lambda: fg.op("scalar", lambda s: s.activation(
            sqo[:], zbb[:], AF.Square), reads=(ZR, "sqo"), writes=("sqo",),
            tag=k + ":osq")
        th["os1"] = lambda: fg.op("vector", lambda v: v.tensor_reduce(
            stt2[:, S_SUM:S_SUM + 4], z3(zbb[:]), axis=AX.X, op=ALU.add),
            reads=(ZR, "stt2"), writes=("stt2",), tag=k + ":os1")
        th["os2"] = lambda: fg.op("vector", lambda v: v.tensor_reduce(
            stt2[:, S_SS:S_SS + 4], z3(sqo[:]), axis=AX.X, op=ALU.add),
            reads=("sqo", "stt2"), writes=("stt2",), tag=k + ":os2")
        th["os3"] = lambda: fg.op("vector", lambda v: v.tensor_scalar(
            out=stt2[:, S_MU:S_MU + 8], in0=stt2[:, S_SUM:S_SUM + 8],
            scalar1=1.0 / HD, scalar2=None, op0=ALU.mult),
            reads=("stt2",), writes=("stt2",), tag=k + ":os3")
        th["os4"] = lambda: fg.op("vector", lambda v: v.tensor_tensor(
            stt2[:, S_S1:S_S1 + 4], stt2[:, S_MU:S_MU + 4],
            stt2[:, S_MU:S_MU + 4], ALU.mult),
            reads=("stt2",), writes=("stt2",), tag=k + ":os4")
        th["os5"] = lambda: fg.op("vector", lambda v: v.tensor_tensor(
            stt2[:, S_TMP:S_TMP + 4], stt2[:, S_MU + 4:S_MU + 8],
            stt2[:, S_S1:S_S1 + 4], ALU.subtract),
            reads=("stt2",), writes=("stt2",), tag=k + ":os5")
        th["orstd"] = lambda: fg.op("scalar", zrstd2, reads=("stt2",),
                                    writes=("stt2",), tag=k + ":orstd")
        th["on1"] = lambda: fg.op("vector", lambda v: v.tensor_tensor(
            z3(o_ln[:]), z3(zbb[:]), bc32(S_MU), ALU.subtract),
            reads=(ZR, "stt2", "o_ln"), writes=("o_ln",), tag=k + ":on1")
        th["on2"] = lambda: fg.op("vector", lambda v: v.tensor_tensor(
            z3(o_ln[:]), z3(o_ln[:]), bc32(S_RSTD), ALU.mult),
            reads=("stt2", "o_ln"), writes=("o_ln",), tag=k + ":on2")
        th["on3"] = lambda: fg.op("vector", lambda v: v.tensor_tensor(
            o_ln[:], o_ln[:], cst["ttt_g_tm"][:], ALU.mult),
            reads=("o_ln",), writes=("o_ln",), tag=k + ":on3")
        th["on4"] = lambda: fg.op("vector", lambda v: v.tensor_tensor(
            o_ln[:], o_ln[:], cst["ttt_b_tm"][:], ALU.add),
            reads=("o_ln",), writes=("o_ln",), tag=k + ":on4")

        def _tout():
            def t_out(t):
                t.transpose(p_ot[:, 0:64], o_ln[0:64, 0:128],
                            cst["ident_f"][0:64, 0:64])
                return t.transpose(p_ot[:, 64:128], o_ln[0:64, 128:256],
                                   cst["ident_f"][0:64, 0:64])
            fg.op("tensor", t_out, reads=("o_ln", "p_ot"), writes=("p_ot",),
                 tag=k + ":tout")
        th["tout"] = _tout
        th["oadd"] = lambda: fg.op("vector", lambda v, sl=sl, blkidx=blkidx:
            v.tensor_tensor(
                XO16[:, :, 64 * blkidx:64 * blkidx + 64],
                p_ot[:, 0:128].rearrange("p (a b) -> p a b", b=64),
                XQb[:, :, sl], ALU.add),
            reads=("p_ot", "XQb"), writes=(f"XO16:{blkidx}",), tag=k + ":oadd")

        def _pub():
            if blkidx == 15 or n == nch - 1:
                nblk = n // 16
                lo = 1024 * nblk
                w = 64 * (blkidx + 1)
                CW = T["CW"]
                cc = lo // CW
                off = lo - cc * CW
                fg.dma("sync", T[f"ymix_pub{cc}"].rearrange("(mc p) t -> p mc t", p=128)
                      [:, :, off:off + w], XO16[:, :, 0:w],
                      reads=tuple(f"XO16:{i}" for i in range(blkidx + 1)),
                      writes=(f"ymix_pub{cc}",), tag=k + ":pub")
        th["pub"] = _pub
        return th

    TAIL_ORDER = ("osq", "os1", "os2", "os3", "os4", "os5", "orstd",
                  "on1", "on2", "on3", "on4", "tout", "oadd", "pub")

    prev = None
    P = (lambda nm: prev[nm]() if prev is not None else None)
    for n in range(nch):
        k = f"sc{n}"
        sl = slice(n * C, (n + 1) * C)
        zp = n % 2
        zbb = zbf2[zp]
        ZW = f"zbf{zp}"

        def t_xk(t, sl=sl):
            t.transpose(p_tm[0:64, 0:128], XKb[:, 0, sl], cst["ident_b"][:])
            return t.transpose(p_tm[0:64, 128:256], XKb[:, 1, sl], cst["ident_b"][:])
        fg.op("tensor", t_xk, reads=("XKb",), writes=("p_tm",), tag=k + ":txk")

        def t_tg(t, sl=sl):
            t.transpose(p_tg[0:64, 0:128], tgt_fm[:, 0, sl], cst["ident_f"][:])
            return t.transpose(p_tg[0:64, 128:256], tgt_fm[:, 1, sl], cst["ident_f"][:])
        fg.op("tensor", t_tg, reads=("tgt_fm", "psB"), writes=("psB",), tag=k + ":ttg")

        def atmm(t, sl=sl):
            last = None
            for h in range(4):
                p, h2 = h // 2, h % 2
                dst = p_atA if h2 == 0 else p_atB
                last = t.matmul(dst[0:64, 64 * p:64 * p + 64],
                                XKb[64 * h2:64 * h2 + 64, p, sl],
                                XQb[64 * h2:64 * h2 + 64, p, sl], start=True, stop=True)
            return last
        fg.op("tensor", atmm, reads=("XKb", "XQb"), writes=("p_at",), tag=k + ":at")

        fg.op("vector", lambda v: v.tensor_copy(xk_tm[:], p_tm[0:64, :]),
             reads=("p_tm",), writes=("xk_tm",), tag=k + ":evxk")
        fg.op("vector", lambda v: v.tensor_tensor(at_m[0:64, 0:128], p_atA[0:64, :],
                                                  cst["trimask4"][:, 0:128], ALU.mult),
             reads=("p_at", "at_m"), writes=("at_m",), tag=k + ":evat")
        fg.op("vector", lambda v: v.tensor_tensor(at_m[0:64, 128:256], p_atB[0:64, :],
                                                  cst["trimask4"][:, 0:128], ALU.mult),
             reads=("p_at", "at_m"), writes=("at_m",), tag=k + ":evat2")

        P("osq")
        P("os1")
        P("os2")

        def z1mm(t, sl=sl):
            last = None
            for p in range(2):
                last = t.matmul(p_z[0:64, 128 * p:128 * p + 128], XKb[:, p, sl],
                                W2b[:, 128 * p:128 * p + 128], start=True, stop=True)
            return last
        fg.op("tensor", z1mm, reads=("XKb", "W2b", "psA"), writes=("psA",), tag=k + ":z1")
        fg.op("vector", lambda v: v.tensor_tensor(z1f[:], p_z[0:64, 0:256], b2[:], ALU.add),
             reads=("psA", "b2"), writes=("z1f",), tag=k + ":z1f")
        P("os3")
        fg.op("vector", lambda v: v.tensor_tensor(sq_tm[:], z1f[:], z1f[:], ALU.mult),
             reads=("z1f", "sq_tm"), writes=("sq_tm",), tag=k + ":zsq")

        fg.op("vector", lambda v: v.tensor_reduce(
            stt[:, S_SUM:S_SUM + 4], z3(z1f[:]), axis=AX.X, op=ALU.add),
            reads=("z1f", "stt"), writes=("stt",), tag=k + ":zst1")
        P("os4")
        fg.op("vector", lambda v: v.tensor_reduce(
            stt[:, S_SS:S_SS + 4], z3(sq_tm[:]), axis=AX.X, op=ALU.add),
            reads=("sq_tm", "stt"), writes=("stt",), tag=k + ":zst2")
        P("os5")
        P("orstd")
        fg.op("vector", lambda v: v.tensor_scalar(
            out=stt[:, S_MU:S_MU + 8], in0=stt[:, S_SUM:S_SUM + 8],
            scalar1=1.0 / HD, scalar2=None, op0=ALU.mult),
            reads=("stt",), writes=("stt",), tag=k + ":zst3")
        P("on1")
        fg.op("vector", lambda v: v.tensor_tensor(
            stt[:, S_S1:S_S1 + 4], stt[:, S_MU:S_MU + 4], stt[:, S_MU:S_MU + 4],
            ALU.mult),
            reads=("stt",), writes=("stt",), tag=k + ":zst4")
        P("on2")
        fg.op("vector", lambda v: v.tensor_tensor(
            stt[:, S_TMP:S_TMP + 4], stt[:, S_MU + 4:S_MU + 8],
            stt[:, S_S1:S_S1 + 4], ALU.subtract),
            reads=("stt",), writes=("stt",), tag=k + ":zst5")
        P("on3")

        def zrstd(s):
            s.activation(stt[:, S_RSTD:S_RSTD + 4], stt[:, S_TMP:S_TMP + 4], AF.Ln,
                         bias=cst["epsv"][:, 1:2])
            s.drain()
            s.activation(stt[:, S_RSTD:S_RSTD + 4], stt[:, S_RSTD:S_RSTD + 4],
                         AF.Exp, scale=-0.5)
            s.drain()
            return s.nop()
        fg.op("scalar", zrstd, reads=("stt",), writes=("stt",), tag=k + ":zrstd")

        fg.op("vector", lambda v: v.tensor_tensor(
            z3(xh[:]), z3(z1f[:]), bc3(S_MU), ALU.subtract),
            reads=("z1f", "stt", "xh"), writes=("xh",), tag=k + ":xh1")
        P("on4")
        P("tout")
        fg.op("vector", lambda v: v.tensor_tensor(
            z3(xh[:]), z3(xh[:]), bc3(S_RSTD), ALU.mult),
            reads=("stt", "xh"), writes=("xh",), tag=k + ":xh2")
        fg.op("vector", lambda v, n=n: v.scalar_tensor_tensor(
            out=stt[:, S_SCL:S_SCL + 4], in0=stt[:, S_RSTD:S_RSTD + 4],
            scalar=1.0 / HD, in1=etaT[:, 4 * n:4 * n + 4],
            op0=ALU.mult, op1=ALU.mult),
            reads=("stt", "etaT"), writes=("stt",), tag=k + ":scl1")
        fg.op("vector", lambda v: v.tensor_tensor(
            t1[:], xh[:], p_tg[0:64, 0:256], ALU.subtract),
            reads=("xh", "psB", "t1"), writes=("t1",), tag=k + ":go1")
        P("oadd")
        fg.op("vector", lambda v: v.tensor_tensor(t1[:], t1[:], g2_tm[:], ALU.mult),
             reads=("t1",), writes=("t1",), tag=k + ":go2")
        fg.op("vector", lambda v: v.tensor_tensor(t2[:], t1[:], xh[:], ALU.mult),
             reads=("t1", "xh", "t2"), writes=("t2",), tag=k + ":gs2")
        fg.op("vector", lambda v: v.tensor_reduce(
            stt[:, S_S1:S_S1 + 4], z3(t1[:]), axis=AX.X, op=ALU.add),
            reads=("t1", "stt"), writes=("stt",), tag=k + ":gs1")
        fg.op("vector", lambda v: v.tensor_reduce(
            stt[:, S_S2:S_S2 + 4], z3(t2[:]), axis=AX.X, op=ALU.add),
            reads=("t2", "stt"), writes=("stt",), tag=k + ":gs3")
        fg.op("vector", lambda v: v.scalar_tensor_tensor(
            out=z3(t2[:]), in0=z3(t1[:]), scalar=float(HD), in1=bc3(S_S1),
            op0=ALU.mult, op1=ALU.subtract),
            reads=("t1", "stt", "t2"), writes=("t2",), tag=k + ":gn1")
        fg.op("vector", lambda v: v.tensor_tensor(
            z3(xh[:]), z3(xh[:]), bc3(S_S2), ALU.mult),
            reads=("stt", "xh"), writes=("xh",), tag=k + ":gn2")
        fg.op("vector", lambda v: v.tensor_tensor(t2[:], t2[:], xh[:], ALU.subtract),
             reads=("t2", "xh"), writes=("t2",), tag=k + ":gn3")
        fg.op("vector", lambda v: v.tensor_tensor(
            z3(gneg[0:64, :]), z3(t2[:]), bc3(S_SCL), ALU.mult),
            reads=("t2", "stt", "gneg"), writes=("gneg",), tag=k + ":gn4")

        def zbmm(t, sl=sl):
            last = None
            for p in range(2):
                dst = psB if p == 0 else psA
                t.matmul(dst[0:64, 256:384], cst["triuTz"][:],
                         gneg[:, 128 * p:128 * p + 128], start=True, stop=False)
                t.matmul(dst[0:64, 256:384], XQb[:, p, sl],
                         W2b[:, 128 * p:128 * p + 128], start=False, stop=True)
                for h2 in range(2):
                    hh = 2 * p + h2
                    atoff = {0: 0, 2: 64, 1: 128, 3: 192}[hh]
                    last = t.matmul(dst[0:64, 256 + 64 * h2:256 + 64 * h2 + 64],
                                    at_m[:, atoff:atoff + 64],
                                    gneg[:, 64 * hh:64 * hh + 64],
                                    start=False, stop=True, skip_group_check=True)
            return last
        fg.op("tensor", zbmm, reads=("gneg", "XQb", "W2b", "at_m", "psA", "psB", "z1f"),
             writes=("psA", "psB"), tag=k + ":zb")

        fg.op("vector", lambda v, zbb=zbb: v.tensor_tensor(
            zbb[:, 0:128], psB[0:64, 256:384], b2[:, 0:128], ALU.add),
            reads=("psA", "psB", "b2", ZW), writes=(ZW,), tag=k + ":zbf")
        fg.op("vector", lambda v, zbb=zbb: v.tensor_tensor(
            zbb[:, 128:256], psA[0:64, 256:384], b2[:, 128:256], ALU.add),
            reads=("psA", "psB", "b2", ZW), writes=(ZW,), tag=k + ":zbf2")
        P("pub")

        def bnew(t):
            t.matmul(p_ot[0:64, 128:256], cst["ones64"][:], gneg[0:64, 0:128],
                     start=True, stop=True)
            return t.matmul(p_ot[0:64, 256:384], cst["ones64"][:],
                            gneg[0:64, 128:256], start=True, stop=True)
        fg.op("tensor", bnew, reads=("gneg", "p_ot"), writes=("p_bn",), tag=k + ":bn")

        def wdmm(t):
            t.matmul(p_wd[:, 0:128], xk_tm[0:64, 0:128], gneg[0:64, 0:128],
                     start=True, stop=True)
            return t.matmul(p_wd[:, 128:256], xk_tm[0:64, 128:256],
                            gneg[0:64, 128:256], start=True, stop=True)
        fg.op("tensor", wdmm, reads=("xk_tm", "gneg"), writes=("p_wd",), tag=k + ":wd")

        fg.op("vector", lambda v: v.tensor_tensor(
            wsc[:, 0:128], p_wd[:, 0:128], cst["wmask"][:], ALU.mult),
            reads=("p_wd", "wsc"), writes=("wsc",), tag=k + ":wu1")
        fg.op("vector", lambda v: v.tensor_tensor(
            wsc[:, 128:256], p_wd[:, 128:256], cst["wmask"][:], ALU.mult),
            reads=("p_wd", "wsc"), writes=("wsc",), tag=k + ":wu2")
        fg.op("vector", lambda v: v.tensor_tensor(
            b2[:], b2[:], p_ot[0:64, 128:384], ALU.add),
            reads=("p_bn", "b2"), writes=("b2",), tag=k + ":wu5")
        fg.op("vector", lambda v: v.tensor_tensor(W2[:], W2[:], wsc[:], ALU.add),
             reads=("wsc", "W2:0", "W2:1"), writes=("W2:0", "W2:1"), tag=k + ":wu3")
        fg.op("vector", lambda v: v.tensor_copy(b2b[0:64, :], b2[:]),
             reads=("b2", "b2b"), writes=("b2b",), tag=k + ":wu6")
        fg.op("vector", lambda v: v.tensor_copy(W2b[:], W2[:]),
             reads=("W2:0", "W2:1", "W2b"), writes=("W2b",), tag=k + ":wu4")

        prev = mk_tail(n)

    if prev is not None:
        for nm in TAIL_ORDER:
            prev[nm]()

    for cc in range(T["NCB_T"]):
        f.coll("AllGather", ALU.bypass, [T[f"ymix_pub{cc}"].opt()],
               [T[f"ymix_gat{cc}"].opt()], reads=(f"ymix_pub{cc}",),
               writes=(f"ymix_gat{cc}",), tag=f"sc:ag{cc}")
    d = T["dout"]("ymixg", [1024, seq], BF16)
    if d is not None:
        for cc in range(T["NCB_T"]):
            f.dma("sync", d[:, cc * T["CW"]:(cc + 1) * T["CW"]], T[f"ymix_gat{cc}"],
                  reads=(f"ymix_gat{cc}",), tag=f"dbg:ymix{cc}")


class LNH:

    def __init__(self, f, nc, cst, ctx, pfx):
        self.f, self.nc, self.cst, self.pfx = f, nc, cst, pfx
        sb = lambda n, s, dt=F32: ctx.enter_context(nc.sbuf_tensor(pfx + n, s, dt))
        ps = lambda n, s, dt=F32: ctx.enter_context(nc.psum_tensor(pfx + n, s, dt))
        self.xb = sb("xb", [128, 8, TB], BF16)
        self.sqb = sb("sqb", [128, 8, TB], BF16)
        self.srow = sb("srow", [1, 4 * TB])
        self.srow_b = sb("srowb", [1, 2 * TB], BF16)
        self.p_sum = ps("p_sum", [1, TB])
        self.p_ss = ps("p_ss", [1, TB])
        self.p_bc = ps("p_bc", [128, 2 * TB])

    def stats(self, tag, src_ap, src_res, src_bf16, eps_col, extra_reads=()):
        f, cst = self.f, self.cst
        xb, sqb, srow, srow_b = self.xb, self.sqb, self.srow, self.srow_b
        p_sum, p_ss, p_bc = self.p_sum, self.p_ss, self.p_bc
        P = self.pfx
        if not src_bf16:
            f.op("vector", lambda v: v.tensor_copy(xb[:], src_ap),
                 reads=(src_res,) + tuple(extra_reads), writes=(P + "xb",), tag=tag + ":cast")
            sums_src, sums_res = xb, P + "xb"
        else:
            sums_src, sums_res = src_ap, src_res
        f.op("scalar", lambda s: s.activation(sqb[:], (xb[:] if not src_bf16 else src_ap), AF.Square),
             reads=(sums_res if not src_bf16 else src_res,), writes=(P + "sqb",), tag=tag + ":sq")

        def mm_sum(t):
            last = None
            for kc in range(8):
                aa = sums_src[:, kc, :] if not src_bf16 else src_ap[:, kc, :]
                last = t.matmul(p_sum[0:1, :], cst["onescol_b"][:], aa,
                                start=(kc == 0), stop=(kc == 7))
            return last
        f.op("tensor", mm_sum, reads=(sums_res if not src_bf16 else src_res,),
             writes=(P + "p_sum",), tag=tag + ":mms")

        def mm_ss(t):
            last = None
            for kc in range(8):
                last = t.matmul(p_ss[0:1, :], cst["onescol_b"][:], sqb[:, kc, :],
                                start=(kc == 0), stop=(kc == 7))
            return last
        f.op("tensor", mm_ss, reads=(P + "sqb",), writes=(P + "p_ss",), tag=tag + ":mmss")

        def stats_v(v):
            v.tensor_scalar(out=srow[0:1, 0:TB], in0=p_sum[0:1, :], scalar1=1.0 / D,
                            scalar2=None, op0=ALU.mult)
            v.tensor_scalar(out=srow[0:1, 3 * TB:4 * TB], in0=p_ss[0:1, :],
                            scalar1=1.0 / D, scalar2=None, op0=ALU.mult)
            v.drain()
            v.tensor_tensor(srow[0:1, TB:2 * TB], srow[0:1, 0:TB], srow[0:1, 0:TB], ALU.mult)
            v.drain()
            v.tensor_tensor(srow[0:1, TB:2 * TB], srow[0:1, 3 * TB:4 * TB],
                            srow[0:1, TB:2 * TB], ALU.subtract)
            v.drain()
            return v.nop()
        f.op("vector", stats_v, reads=(P + "p_sum", P + "p_ss"), writes=(P + "srow",),
             tag=tag + ":st")

        def rstd_s(s):
            s.activation(srow[0:1, 2 * TB:3 * TB], srow[0:1, TB:2 * TB], AF.Ln,
                         bias=cst["epsv"][0:1, eps_col:eps_col + 1])
            s.drain()
            s.activation(srow[0:1, 2 * TB:3 * TB], srow[0:1, 2 * TB:3 * TB],
                         AF.Exp, scale=-0.5)
            s.drain()
            return s.nop()
        f.op("scalar", rstd_s, reads=(P + "srow",), writes=(P + "srow",), tag=tag + ":rstd")

        def tobf(v):
            v.tensor_copy(srow_b[0:1, 0:TB], srow[0:1, 0:TB])
            v.tensor_copy(srow_b[0:1, TB:2 * TB], srow[0:1, 2 * TB:3 * TB])
            v.drain()
            return v.nop()
        f.op("vector", tobf, reads=(P + "srow",), writes=(P + "srowb",), tag=tag + ":bf")

        def bcast(t):
            t.matmul(p_bc[:, 0:TB], cst["onesr_b"][:], srow_b[0:1, 0:TB],
                     start=True, stop=True)
            return t.matmul(p_bc[:, TB:2 * TB], cst["onesr_b"][:],
                            srow_b[0:1, TB:2 * TB], start=True, stop=True)
        f.op("tensor", bcast, reads=(P + "srowb", P + "p_bc"), writes=(P + "p_bc",),
             tag=tag + ":bc")
        return p_bc


def emit_p2(f, nc, T, ctx):
    cst = T["cst"]; nto = T["nto"]; ntb = T["ntb_o"]; seq = T["seq"]
    sb = lambda n, s, dt=F32: ctx.enter_context(nc.sbuf_tensor(n, s, dt))
    ps = lambda n, s, dt=F32: ctx.enter_context(nc.psum_tensor(n, s, dt))

    ym = sb("ym", [128, 8, nto], BF16)
    yn = sb("yn", [128, 8, TB], BF16)
    ao = sb("ao", [128, 8, TB])
    xf = sb("xf2", [128, 8, TB])
    x2 = sb("x2t", [128, 8, TB])
    ab = sb("abuf", [128, 8, TB], BF16)
    wo_sb = sb("wo_sb", [128, 8, 1024], BF16)
    scf = sb("scf2", [128, 8, TB])
    p_w = [ps("p_wA", [128, TB]), ps("p_wB", [128, TB])]
    ln = LNH(f, nc, cst, ctx, "l2_")

    f.dma("gpsimd", wo_sb[:], T["woT"].rearrange("(kc p) m -> p kc m", p=128),
          writes=("wo_sb",), tag="p2:ldwo")

    def gat(e, core):
        q = core % 4
        cc = (nto * q) // T["CW"]
        off = nto * q - cc * T["CW"]
        return e.dma_start(out=ym[:], in_=T[f"ymix_gat{cc}"]
                           .rearrange("(kc p) t -> p kc t", p=128)
                           [:, :, off:off + nto])
    f.dma_percore("sync", gat,
                  reads=tuple(f"ymix_gat{cc}" for cc in range(T["NCB_T"])),
                  writes=("ym",), tag="p2:gat")

    for tb in range(ntb):
        sl = slice(tb * TB, (tb + 1) * TB)
        k = f"p2.{tb}"
        p_bc = ln.stats(k + ":pn", ym[:, :, sl], "ym", True, 1)

        def pnorm(v, sl=sl):
            v.tensor_tensor(scf[:], ym[:, :, sl],
                            p_bc[:, 0:TB].unsqueeze(1).broadcast_to([128, 8, TB]), ALU.subtract)
            v.drain()
            v.tensor_tensor(scf[:], scf[:],
                            p_bc[:, TB:2 * TB].unsqueeze(1).broadcast_to([128, 8, TB]), ALU.mult)
            v.drain()
            v.tensor_tensor(scf[:], scf[:],
                            cst["post_nw"][:].unsqueeze(2).broadcast_to([128, 8, TB]), ALU.mult)
            v.drain()
            v.tensor_tensor(yn[:], scf[:],
                            cst["post_nb"][:].unsqueeze(2).broadcast_to([128, 8, TB]), ALU.add)
            v.drain()
            return v.nop()
        f.op("vector", pnorm, reads=("ym", "l2_p_bc"), writes=("yn", "scf2"),
             tag=k + ":pn")

        for mc in range(8):
            wp = mc % 2
            def womm(t, mc=mc, wp=wp):
                last = None
                for kc in range(8):
                    last = t.matmul(p_w[wp][:], wo_sb[:, kc, 128 * mc:128 * mc + 128],
                                    yn[:, kc, :], start=(kc == 0), stop=(kc == 7))
                return last
            f.op("tensor", womm, reads=("yn", "wo_sb"), writes=(f"p_w{wp}",), tag=k + f":wo{mc}")
            f.op("scalar", lambda s, mc=mc, wp=wp: s.activation(
                ao[:, mc, :], p_w[wp][:], AF.Identity, bias=cst["wo_b"][:, mc:mc + 1]),
                reads=(f"p_w{wp}",), writes=("ao",), tag=k + f":woev{mc}")

        f.dma("sync", xf[:], fm3(T["x_own"])[:, :, sl], writes=("xf2",), tag=k + ":ldx")
        p_bc = ln.stats(k + ":ol", ao[:], "ao", False, 0)

        def onorm(v):
            v.tensor_tensor(scf[:], ao[:],
                            p_bc[:, 0:TB].unsqueeze(1).broadcast_to([128, 8, TB]), ALU.subtract)
            v.drain()
            v.tensor_tensor(scf[:], scf[:],
                            p_bc[:, TB:2 * TB].unsqueeze(1).broadcast_to([128, 8, TB]), ALU.mult)
            v.drain()
            v.tensor_tensor(x2[:], scf[:], xf[:], ALU.add)
            v.drain()
            return v.nop()
        f.op("vector", onorm, reads=("ao", "l2_p_bc", "xf2"), writes=("x2t", "scf2"),
             tag=k + ":on")
        f.dma("sync", fm3(T["x2_spill"])[:, :, sl], x2[:], reads=("x2t",),
              writes=("x2_spill",), tag=k + ":spx2")

        p_bc2 = ln.stats(k + ":y0", x2[:], "x2t", False, 0)

        def y0norm(v):
            v.tensor_tensor(scf[:], x2[:],
                            p_bc2[:, 0:TB].unsqueeze(1).broadcast_to([128, 8, TB]), ALU.subtract)
            v.drain()
            v.tensor_tensor(xf[:], scf[:],
                            p_bc2[:, TB:2 * TB].unsqueeze(1).broadcast_to([128, 8, TB]), ALU.mult)
            v.drain()
            return v.nop()
        f.op("vector", y0norm, reads=("x2t", "l2_p_bc"), writes=("xf2", "scf2"),
             tag=k + ":y0n")
        f.dma("sync", fm3(T["y0_spill"])[:, :, sl], xf[:], reads=("xf2",),
              writes=("y0_spill",), tag=k + ":spy0")

        p_bc3 = ln.stats(k + ":ln1", xf[:], "xf2", False, 0)

        def anorm(v):
            v.tensor_tensor(scf[:], xf[:],
                            p_bc3[:, 0:TB].unsqueeze(1).broadcast_to([128, 8, TB]), ALU.subtract)
            v.drain()
            v.tensor_tensor(scf[:], scf[:],
                            p_bc3[:, TB:2 * TB].unsqueeze(1).broadcast_to([128, 8, TB]), ALU.mult)
            v.drain()
            v.tensor_tensor(ab[:], scf[:],
                            cst["ln1_w"][:].unsqueeze(2).broadcast_to([128, 8, TB]), ALU.mult)
            v.drain()
            return v.nop()
        f.op("vector", anorm, reads=("xf2", "l2_p_bc"), writes=("abuf", "scf2"),
             tag=k + ":an")
        f.dma("sync", fm3(T[f"a_pub{tb}"]), ab[:], reads=("abuf",),
              writes=(f"a_pub{tb}",), tag=k + ":pub")

    d = T["dout"]("x2sp", [D, nto])
    if d is not None:
        f.dma("sync", d, T["x2_spill"], reads=("x2_spill",), tag="dbg:x2")
    d = T["dout"]("agat", [4 * D, nto], BF16)
    if d is not None:
        for c in range(T["NCB_O"]):
            f.coll("AllGather", ALU.bypass, [T[f"a_pub{c}"].opt()],
                   [T[f"a_gat{c}"].opt()], reads=(f"a_pub{c}",),
                   writes=(f"a_gat{c}",), tag=f"p2:ag{c}")
            f.dma("sync", d[:, c * TB:(c + 1) * TB], T[f"a_gat{c}"],
                  reads=(f"a_gat{c}",), tag=f"dbg:agat{c}")
        T["_a_coll_done"] = True


def emit_p3(f, nc, T, ctx):
    cst = T["cst"]; nto = T["nto"]; seq = T["seq"]; ntb = T["ntb_s"]
    sb = lambda n, s, dt=F32: ctx.enter_context(nc.sbuf_tensor(n, s, dt))
    ps = lambda n, s, dt=F32: ctx.enter_context(nc.psum_tensor(n, s, dt))

    kvl = sb("kvl", [128, 2, seq], BF16)
    ql = sb("ql", [128, 2, seq], BF16)
    ku_sb = sb("ku_sb", [128, 2, 128], BF16)
    qu_sb = sb("qu_sb", [128, 2, 128], BF16)
    vu_sb = sb("vu_sb", [128, 2, 256], BF16)
    p_l = ps("p_l", [128, TB])
    p_rh = ps("p_rh", [128, TB])
    krb_d = nc.dram_tensor("krb_d", [128, seq], BF16, kind="Internal").ap()
    qrb_d = nc.dram_tensor("qrb_d", [128, seq], BF16, kind="Internal").ap()

    if not T.get("_a_coll_done"):
        for c in range(T["NCB_O"]):
            f.coll("AllGather", ALU.bypass, [T[f"a_pub{c}"].opt()],
                   [T[f"a_gat{c}"].opt()], reads=(f"a_pub{c}",),
                   writes=(f"a_gat{c}",), tag=f"p3:agc{c}")

    r3 = lambda ap: ap.rearrange("(kc p) m -> p kc m", p=128)
    for nm, dst in [("k_uT", ku_sb), ("q_uT", qu_sb), ("v_uT", vu_sb)]:
        f.dma("gpsimd", dst[:], r3(T[nm]), writes=(nm + "_sb",), tag="p3:ld" + nm)

    ctxa = ctx.enter_context(ExitStack())
    sba = lambda n, s, dt=F32: ctxa.enter_context(nc.sbuf_tensor(n, s, dt))
    krch = [sba("krch0", [128, TB], BF16), sba("krch1", [128, TB], BF16)]
    qrch = [sba("qrch0", [128, TB], BF16), sba("qrch1", [128, TB], BF16)]
    _ag0 = sba("ag0", [128, 8, TB], BF16)
    ag = [_ag0, _ag0]
    cosb = sba("cosb", [128, TB]); sinb = sba("sinb", [128, TB])
    rr1 = sba("rr1", [128, TB]); rr2 = sba("rr2", [128, TB]); rr3 = sba("rr3", [128, TB])
    kvd_sb = sba("kvd_sb", [128, 8, 256], BF16)
    qd_sb = sba("qd_sb", [128, 8, 256], BF16)
    rk_sb = sba("rk_sb", [128, 8, 128], BF16)
    rq_sb = sba("rq_sb", [128, 2, 128], BF16)
    for nm, dst in [("kv_dT", kvd_sb), ("q_dT", qd_sb), ("rope_kT", rk_sb),
                    ("rope_qT", rq_sb)]:
        f.dma("gpsimd", dst[:], r3(T[nm]), writes=(nm + "_sb",), tag="p3:ld" + nm)

    for gtb in range(ntb):
        par = 0
        cpar = gtb % 2
        sl = slice(gtb * TB, (gtb + 1) * TB)
        k = f"p3a.{gtb}"
        t0 = gtb * TB
        r = t0 // nto
        loc = t0 - r * nto
        ch = loc // TB
        src_ap = T[f"a_gat{ch}"][r * 1024:(r + 1) * 1024, :] \
            .rearrange("(kc p) t -> p kc t", p=128)
        f.dma("sync", ag[par][:], src_ap, reads=(f"a_gat{ch}",), writes=(f"ag{par}",),
              tag=k + ":ld")
        f.dma("sync", cosb[:], T["cos_t"][:, sl], writes=("cosb",), tag=k + ":ldc")
        f.dma("sync", sinb[:], T["sin_t"][:, sl], writes=("sinb",), tag=k + ":lds")

        for nm, wsb, dst, dres in [("kv", kvd_sb, kvl, "kvl"), ("q", qd_sb, ql, "ql")]:
            for mc in range(2):
                pl = p_l if mc == 0 else p_rh
                PL = "p_l" if mc == 0 else "p_rh"
                def latmm(t, wsb=wsb, mc=mc, par=par, pl=pl):
                    last = None
                    for kc in range(8):
                        last = t.matmul(pl[:], wsb[:, kc, 128 * mc:128 * mc + 128],
                                        ag[par][:, kc, :], start=(kc == 0), stop=(kc == 7))
                    return last
                f.op("tensor", latmm, reads=(f"ag{par}", ("kv_dT" if nm == "kv" else "q_dT") + "_sb", PL),
                     writes=(PL,), tag=k + f":{nm}mm{mc}")
                f.op("vector", lambda v, dst=dst, mc=mc, sl=sl, pl=pl: v.tensor_copy(
                    dst[:, mc, sl], pl[:]),
                    reads=(PL,), writes=(dres,), tag=k + f":{nm}ev{mc}")

        def krmm(t, par=par):
            last = None
            for kc in range(8):
                last = t.matmul(p_l[:], rk_sb[:, kc, :], ag[par][:, kc, :],
                                start=(kc == 0), stop=(kc == 7))
            return last
        f.op("tensor", krmm, reads=(f"ag{par}", "rope_kT_sb", "kvl", "ql", "p_l"),
             writes=("p_l",), tag=k + ":krmm")
        f.op("vector", lambda v: v.tensor_copy(rr1[:], p_l[:]),
             reads=("p_l", "rr1"), writes=("rr1",), tag=k + ":krev")
        f.op("tensor", lambda t: t.matmul(p_rh[:], cst["r4T"][:], rr1[:],
                                          start=True, stop=True),
             reads=("rr1",), writes=("p_rh",), tag=k + ":krrh")

        def krope(v, cpar=cpar):
            v.tensor_tensor(rr2[:], rr1[:], cosb[:], ALU.mult)
            v.drain()
            v.tensor_tensor(rr3[:], p_rh[:], sinb[:], ALU.mult)
            v.drain()
            v.tensor_tensor(krch[cpar][:], rr2[:], rr3[:], ALU.add)
            v.drain()
            return v.nop()
        f.op("vector", krope, reads=("rr1", "p_rh", "cosb", "sinb", f"krch{cpar}"),
             writes=(f"krch{cpar}", "rr2", "rr3"), tag=k + ":krope")
        f.dma("sync", krb_d[:, sl], krch[cpar][:], reads=(f"krch{cpar}",),
              writes=("krb_d",), tag=k + ":krsp")

        def qrmm(t, sl=sl):
            last = None
            for mc in range(2):
                last = t.matmul(p_l[:], rq_sb[:, mc, :], ql[:, mc, sl],
                                start=(mc == 0), stop=(mc == 1))
            return last
        f.op("tensor", qrmm, reads=("ql", "rope_qT_sb", "krb_d", "p_l"), writes=("p_l",),
             tag=k + ":qrmm")
        f.op("vector", lambda v: v.tensor_copy(rr1[:], p_l[:]),
             reads=("p_l", "rr1"), writes=("rr1",), tag=k + ":qrev")
        f.op("tensor", lambda t: t.matmul(p_rh[:], cst["r4T"][:], rr1[:],
                                          start=True, stop=True),
             reads=("rr1", "p_rh"), writes=("p_rh",), tag=k + ":qrrh")

        def qrope(v, cpar=cpar):
            v.tensor_tensor(rr2[:], rr1[:], cosb[:], ALU.mult)
            v.drain()
            v.tensor_tensor(rr3[:], p_rh[:], sinb[:], ALU.mult)
            v.drain()
            v.tensor_tensor(rr2[:], rr2[:], rr3[:], ALU.add)
            v.drain()
            v.tensor_scalar(out=qrch[cpar][:], in0=rr2[:], scalar1=0.125, scalar2=None,
                            op0=ALU.mult)
            v.drain()
            return v.nop()
        f.op("vector", qrope, reads=("rr1", "p_rh", "cosb", "sinb", f"qrch{cpar}"),
             writes=(f"qrch{cpar}", "rr2", "rr3", f"ag{par}", "cosb", "sinb"),
             tag=k + ":qrope")
        f.dma("sync", qrb_d[:, sl], qrch[cpar][:], reads=(f"qrch{cpar}",),
              writes=("qrb_d",), tag=k + ":qrsp")

    f.barrier("p3a")
    ctxa.close()

    kaug = [sb("kaugA", [65, seq], BF16), sb("kaugB", [65, seq], BF16)]
    qaug = [sb("qaugA", [65, seq], BF16), sb("qaugB", [65, seq], BF16)]
    vaug = [sb("vaugA", [128, seq // 128, 65], BF16),
            sb("vaugB", [128, seq // 128, 65], BF16)]
    sqsK = [sb("sqsKA", [64, TB], BF16), sb("sqsKB", [64, TB], BF16)]
    sqsQ = [sb("sqsQA", [64, TB], BF16), sb("sqsQB", [64, TB], BF16)]
    mkr = [sb("mkrA", [1, 8]), sb("mkrB", [1, 8])]
    rowb = sb("rowb", [1, seq], BF16)
    ybuf = sb("ybuf", [64, TB], BF16)
    osum = [sb("osumA", [65, TB]), sb("osumB", [65, TB])]
    rrow = [sb("rrowA", [1, TB]), sb("rrowB", [1, TB])]
    rrow_b = [sb("rrow_bA", [1, TB], BF16), sb("rrow_bB", [1, TB], BF16)]
    ppb = sb("ppb", [128, 16, TB], BF16)
    p_bc2 = ps("p_bc2", [128, TB])
    p_s = [ps("p_s0", [128, TB]), ps("p_s1", [128, TB]), p_l, p_bc2]
    PS_RES = ("p_s0", "p_s1", "p_l", "p_bc2")
    p_os = [ps("p_os0", [65, TB]), ps("p_os1", [65, TB])]
    p_kq = ps("p_kq", [128, TB])

    njc = seq // 128

    def make_prologue(h):
        hp = h % 2
        ka, qa, va, mk = kaug[hp], qaug[hp], vaug[hp], mkr[hp]
        sqK, sqQ = sqsK[hp], sqsQ[hp]
        KA, QA, VA, MK = (f"kaug{hp}", f"qaug{hp}", f"vaug{hp}", f"mkr{hp}")
        SK, SQ = f"sqsK{hp}", f"sqsQ{hp}"
        kh = f"p3b{h}"
        ops = []
        E = ops.append
        E(lambda: f.dma("sync", ka[0:32, :], krb_d[32 * h:32 * h + 32, :],
                        reads=("krb_d", KA), writes=(KA,), tag=kh + ":kr"))
        E(lambda: f.dma("sync", qa[0:32, :], qrb_d[32 * h:32 * h + 32, :],
                        reads=("qrb_d", QA), writes=(QA,), tag=kh + ":qr"))
        E(lambda: f.op("gpsimd", lambda g: g.memset(ka[64:65, :], 1.0),
                       reads=(KA,), writes=(KA,), tag=kh + ":k1"))
        E(lambda: f.op("gpsimd", lambda g: g.memset(va[:], 1.0),
                       reads=(VA,), writes=(VA,), tag=kh + ":v1"))
        for gtb in range(ntb):
            sl = slice(gtb * TB, (gtb + 1) * TB)
            def kumm(t, sl=sl):
                last = None
                for mc in range(2):
                    last = t.matmul(p_kq[0:32, :], ku_sb[:, mc, 32 * h:32 * h + 32],
                                    kvl[:, mc, sl], start=(mc == 0), stop=(mc == 1))
                return last
            E(lambda kumm=kumm, gtb=gtb: f.op("tensor", kumm,
                                              reads=("kvl", "k_uT_sb", "p_kq"),
                                              writes=("p_kq",), tag=kh + f":ku{gtb}"))
            E(lambda sl=sl, gtb=gtb: f.op(
                "vector", lambda v, sl=sl: v.tensor_copy(ka[32:64, sl], p_kq[0:32, :]),
                reads=("p_kq",), writes=(KA,), tag=kh + f":kuev{gtb}"))
            def qumm(t, sl=sl):
                last = None
                for mc in range(2):
                    last = t.matmul(p_kq[0:32, :], qu_sb[:, mc, 32 * h:32 * h + 32],
                                    ql[:, mc, sl], start=(mc == 0), stop=(mc == 1))
                return last
            E(lambda qumm=qumm, gtb=gtb: f.op("tensor", qumm,
                                              reads=("ql", "q_uT_sb", "p_kq"),
                                              writes=("p_kq",), tag=kh + f":qu{gtb}"))
            E(lambda sl=sl, gtb=gtb: f.op(
                "vector", lambda v, sl=sl: v.tensor_scalar(
                    out=qa[32:64, sl], in0=p_kq[0:32, :], scalar1=0.125, scalar2=None,
                    op0=ALU.mult),
                reads=("p_kq",), writes=(QA,), tag=kh + f":quev{gtb}"))
        for jc in range(njc):
            def vmm2(t, jc=jc):
                last = None
                for mc in range(2):
                    last = t.matmul(p_rh[:, 0:64], kvl[:, mc, 128 * jc:128 * jc + 128],
                                    vu_sb[:, mc, 64 * h:64 * h + 64],
                                    start=(mc == 0), stop=(mc == 1))
                return last
            E(lambda vmm2=vmm2, jc=jc: f.op("tensor", vmm2,
                                            reads=("kvl", "v_uT_sb", "p_rh"),
                                            writes=("p_rh",), tag=kh + f":v{jc}"))
            E(lambda jc=jc: f.op(
                "vector", lambda v, jc=jc: v.tensor_copy(va[:, jc, 0:64],
                                                         p_rh[:, 0:64]),
                reads=("p_rh", VA), writes=(VA,), tag=kh + f":vev{jc}"))
        sqd = [sqsK[0], sqsQ[0], sqsK[1], sqsQ[1]]
        SQD = ("sqsK0", "sqsQ0", "sqsK1", "sqsQ1")

        def em_sq(src_ap, SRC, gtb, nm):
            sl = slice(gtb * TB, (gtb + 1) * TB)
            d = gtb % 4
            E(lambda sl=sl, d=d, gtb=gtb: f.op(
                "vector", lambda v, sl=sl, d=d: v.tensor_tensor(
                    sqd[d][:], src_ap[0:64, sl], src_ap[0:64, sl], ALU.mult),
                reads=(SRC, SQD[d]), writes=(SQD[d],), tag=kh + f":sq{nm}{gtb}"))

        def em_ss(gtb, nm):
            sl = slice(gtb * TB, (gtb + 1) * TB)
            d = gtb % 4
            E(lambda d=d, gtb=gtb: f.op(
                "tensor", lambda t, d=d: t.matmul(
                    p_kq[0:1, :], cst["onescol_b"][0:64, 0:1], sqd[d][:],
                    start=True, stop=True),
                reads=(SQD[d], "p_kq"), writes=("p_kq",), tag=kh + f":ss{nm}{gtb}"))
            E(lambda sl=sl, gtb=gtb: f.op(
                "vector", lambda v, sl=sl: v.tensor_copy(rowb[0:1, sl], p_kq[0:1, :]),
                reads=("p_kq", "rowb"), writes=("rowb",), tag=kh + f":ev{nm}{gtb}"))

        LEAD = 3
        for gtb in range(min(LEAD, ntb)):
            em_sq(ka, KA, gtb, "k")
        for gtb in range(ntb):
            if gtb + LEAD < ntb:
                em_sq(ka, KA, gtb + LEAD, "k")
            em_ss(gtb, "k")
        E(lambda: f.op("vector", lambda v: v.tensor_reduce(
            mk[0:1, 0:1], rowb[0:1, :], axis=AX.X, op=ALU.max),
            reads=("rowb", MK), writes=(MK,), tag=kh + ":mkred"))
        E(lambda: f.op("scalar", lambda s: s.activation(mk[0:1, 1:2], mk[0:1, 0:1],
                                                        AF.Ln),
                       reads=(MK,), writes=(MK,), tag=kh + ":mkln"))
        for gtb in range(min(LEAD, ntb)):
            em_sq(qa, QA, gtb, "q")
        for gtb in range(ntb):
            if gtb + LEAD < ntb:
                em_sq(qa, QA, gtb + LEAD, "q")
            em_ss(gtb, "q")
        E(lambda: f.op("scalar", lambda s: s.activation(rowb[0:1, :], rowb[0:1, :],
                                                        AF.Ln),
                       reads=("rowb",), writes=("rowb",), tag=kh + ":bln"))
        E(lambda: f.op("vector", lambda v: v.tensor_scalar(
            out=rowb[0:1, :], in0=rowb[0:1, :], scalar1=mk[0:1, 1:2], scalar2=None,
            op0=ALU.add),
            reads=("rowb", MK), writes=("rowb",), tag=kh + ":badd"))
        E(lambda: f.op("scalar", lambda s: s.activation(rowb[0:1, :], rowb[0:1, :],
                                                        AF.Exp, scale=0.5),
                       reads=("rowb",), writes=("rowb",), tag=kh + ":bexp"))
        E(lambda: f.op("vector", lambda v: v.tensor_scalar(
            out=qa[64:65, :], in0=rowb[0:1, :], scalar1=-1.0, scalar2=None,
            op0=ALU.mult),
            reads=("rowb", QA), writes=(QA,), tag=kh + ":bneg"))
        return ops

    GN = 8
    nst = seq // TB

    def emit_head(h, pend):
        hp = h % 2
        ka, qa, va = kaug[hp], qaug[hp], vaug[hp]
        KA, QA, VA = f"kaug{hp}", f"qaug{hp}", f"vaug{hp}"
        kh = f"p3b{h}"
        chunks = []
        for st in range(nst):
            njt = 4 * (st + 1)
            nchk = (njt + GN - 1) // GN
            for ck in range(nchk):
                chunks.append((st, ck, ck * GN, min(GN, njt - ck * GN),
                               ck == nchk - 1))
        NCH = len(chunks)

        def s_exp(ci):
            st, ck, j0, jn, is_last = chunks[ci]
            isl = slice(st * TB, (st + 1) * TB)
            sbase = (ci % 2) * 8
            for jj in range(jn):
                jt = j0 + jj
                par = (ci * GN + jj) % 4
                slot = sbase + jj
                def smm(t, jt=jt, par=par, isl=isl):
                    return t.matmul(p_s[par][:], ka[:, 128 * jt:128 * jt + 128],
                                    qa[:, isl], start=True, stop=True)
                f.op("tensor", smm, reads=(KA, QA), writes=(PS_RES[par],),
                     tag=kh + f":s{st}.{jt}")

                def expf(s, slot=slot, par=par):
                    return s.activation(ppb[:, slot, :], p_s[par][:], AF.Exp)
                f.op("scalar", expf, reads=(PS_RES[par],), writes=(f"ppb{slot}",),
                     tag=kh + f":e{st}.{jt}")
                if jt >= 4 * st:
                    off = jt - 4 * st
                    f.op("vector", lambda v, slot=slot, off=off: v.tensor_tensor(
                        ppb[:, slot, :], ppb[:, slot, :], cst["amask"][:, off, :],
                        ALU.mult),
                        reads=(f"ppb{slot}",), writes=(f"ppb{slot}",),
                        tag=kh + f":m{st}.{jt}")

        def av(ci):
            st, ck, j0, jn, is_last = chunks[ci]
            opar = ci % 2
            sbase = (ci % 2) * 8
            osp = st % 2

            def avg(t, j0=j0, jn=jn, opar=opar, sbase=sbase):
                last = None
                for jj in range(jn):
                    last = t.matmul(p_os[opar][:], va[:, j0 + jj, :],
                                    ppb[:, sbase + jj, :], start=(jj == 0),
                                    stop=(jj == jn - 1))
                return last
            f.op("tensor", avg,
                 reads=tuple(f"ppb{sbase + jj}" for jj in range(jn))
                 + (VA, f"p_os{opar}"),
                 writes=(f"p_os{opar}",), tag=kh + f":av{st}.{ck}")
            if ck == 0:
                f.op("vector", lambda v, opar=opar, osp=osp: v.tensor_copy(
                    osum[osp][:], p_os[opar][:]),
                    reads=(f"p_os{opar}", f"osum{osp}"), writes=(f"osum{osp}",),
                    tag=kh + f":oi{st}")
            else:
                f.op("vector", lambda v, opar=opar, osp=osp: v.tensor_tensor(
                    osum[osp][:], osum[osp][:], p_os[opar][:], ALU.add),
                    reads=(f"p_os{opar}", f"osum{osp}"), writes=(f"osum{osp}",),
                    tag=kh + f":oa{st}.{ck}")
            if is_last:
                f.op("vector", lambda v, osp=osp: v.reciprocal(
                    rrow[osp][0:1, :], osum[osp][64:65, :]),
                    reads=(f"osum{osp}", f"rrow{osp}"), writes=(f"rrow{osp}",),
                    tag=kh + f":rc{st}")
                f.op("vector", lambda v, osp=osp: v.tensor_copy(
                    rrow_b[osp][0:1, :], rrow[osp][0:1, :]),
                    reads=(f"rrow{osp}", f"rrowb{osp}"), writes=(f"rrowb{osp}",),
                    tag=kh + f":rcb{st}")

        def tail(ci):
            st, ck, j0, jn, is_last = chunks[ci]
            if not is_last:
                return
            osp = st % 2
            isl = slice(st * TB, (st + 1) * TB)
            f.op("tensor", lambda t, osp=osp: t.matmul(
                p_bc2[0:64, :], cst["onesr_b"][0:1, 0:64], rrow_b[osp][0:1, :],
                start=True, stop=True),
                 reads=(f"rrowb{osp}", "p_bc2"), writes=("p_bc2",), tag=kh + f":rb{st}")
            f.op("vector", lambda v, osp=osp: v.tensor_tensor(
                ybuf[:], osum[osp][0:64, :], p_bc2[0:64, :], ALU.mult),
                reads=(f"osum{osp}", "p_bc2", "ybuf"), writes=("ybuf",),
                tag=kh + f":ymul{st}")
            cc = (st * TB) // T["CW"]
            off = st * TB - cc * T["CW"]
            f.dma("sync", T[f"yat_pub{cc}"][64 * h:64 * h + 64, off:off + TB],
                  ybuf[:], reads=("ybuf",), writes=(f"yat_pub{cc}",),
                  tag=kh + f":pub{st}")

        NPOP = 5
        for ci in range(NCH):
            s_exp(ci)
            if ci >= 1:
                av(ci - 1)
            if ci >= 2:
                tail(ci - 2)
            for _ in range(NPOP):
                if pend:
                    pend.pop(0)()
        av(NCH - 1)
        tail(NCH - 2)
        tail(NCH - 1)
        while pend:
            pend.pop(0)()

    for t in make_prologue(0):
        t()
    for h in range(4):
        pend = make_prologue(h + 1) if h < 3 else []
        emit_head(h, pend)

    for cc in range(T["NCB_T"]):
        f.coll("AllGather", ALU.bypass, [T[f"yat_pub{cc}"].opt()],
               [T[f"yat_gat{cc}"].opt()], reads=(f"yat_pub{cc}",),
               writes=(f"yat_gat{cc}",), tag=f"p3:ag{cc}")
    d = T["dout"]("yatg", [1024, seq], BF16)
    if d is not None:
        for cc in range(T["NCB_T"]):
            f.dma("sync", d[:, cc * T["CW"]:(cc + 1) * T["CW"]], T[f"yat_gat{cc}"],
                  reads=(f"yat_gat{cc}",), tag=f"dbg:yat{cc}")


def emit_p4(f, nc, T, ctx):
    cst = T["cst"]; nto = T["nto"]; ntb = T["ntb_o"]
    sb = lambda n, s, dt=F32: ctx.enter_context(nc.sbuf_tensor(n, s, dt))
    ps = lambda n, s, dt=F32: ctx.enter_context(nc.psum_tensor(n, s, dt))
    xm = T["xm"]

    ya = sb("ya", [128, 8, nto], BF16)
    ow_sb = sb("ow_sb", [128, 8, 1024], BF16)
    y0t = sb("y0t", [128, 8, TB])
    y1t = sb("y1t", [128, 8, TB])
    scf = sb("scf4", [128, 8, TB])
    p_w = [ps("p_w4A", [128, TB]), ps("p_w4B", [128, TB])]
    ln = LNH(f, nc, cst, ctx, "l4_")

    f.dma("gpsimd", ow_sb[:], T["o_wT"].rearrange("(kc p) m -> p kc m", p=128),
          writes=("ow_sb",), tag="p4:ldow")

    def gat(e, core):
        q = core % 4
        cc = (nto * q) // T["CW"]
        off = nto * q - cc * T["CW"]
        return e.dma_start(out=ya[:], in_=T[f"yat_gat{cc}"]
                           .rearrange("(kc p) t -> p kc t", p=128)
                           [:, :, off:off + nto])
    f.dma_percore("sync", gat,
                  reads=tuple(f"yat_gat{cc}" for cc in range(T["NCB_T"])),
                  writes=("ya",), tag="p4:gat")

    for tb in range(ntb):
        sl = slice(tb * TB, (tb + 1) * TB)
        k = f"p4.{tb}"
        f.dma("sync", y0t[:], fm3(T["y0_spill"])[:, :, sl], reads=("y0_spill",),
              writes=("y0t",), tag=k + ":ldy0")
        for mc in range(8):
            wp = mc % 2
            def owmm(t, mc=mc, sl=sl, wp=wp):
                last = None
                for kc in range(8):
                    last = t.matmul(p_w[wp][:], ow_sb[:, kc, 128 * mc:128 * mc + 128],
                                    ya[:, kc, sl], start=(kc == 0), stop=(kc == 7))
                return last
            f.op("tensor", owmm, reads=("ya", "ow_sb"), writes=(f"p_w4{wp}",), tag=k + f":ow{mc}")

            def y1ev(v, mc=mc, wp=wp):
                v.tensor_tensor(y1t[:, mc, :], p_w[wp][:], y0t[:, mc, :], ALU.add)
                v.drain()
                return v.nop()
            f.op("vector", y1ev, reads=(f"p_w4{wp}", "y0t"), writes=("y1t",), tag=k + f":y1{mc}")
        f.dma("sync", fm3(T["y1_spill"])[:, :, sl], y1t[:], reads=("y1t",),
              writes=("y1_spill",), tag=k + ":spy1")

        p_bc = ln.stats(k + ":ln2", y1t[:], "y1t", False, 0)

        def xnorm(v, sl=sl):
            v.tensor_tensor(scf[:], y1t[:],
                            p_bc[:, 0:TB].unsqueeze(1).broadcast_to([128, 8, TB]), ALU.subtract)
            v.drain()
            v.tensor_tensor(scf[:], scf[:],
                            p_bc[:, TB:2 * TB].unsqueeze(1).broadcast_to([128, 8, TB]), ALU.mult)
            v.drain()
            v.tensor_tensor(xm[:, :, sl], scf[:],
                            cst["ln2_w"][:].unsqueeze(2).broadcast_to([128, 8, TB]), ALU.mult)
            v.drain()
            return v.nop()
        f.op("vector", xnorm, reads=("y1t", "l4_p_bc"), writes=("xm", "scf4"),
             tag=k + ":xn")

    d = T["dout"]("xmo", [128, 8, nto], BF16)
    if d is not None:
        f.dma("sync", d, xm[:], reads=("xm",), tag="dbg:xm")


def emit_p5(f, nc, T, ctx):
    cst = T["cst"]; nto = T["nto"]; ntb = T["ntb_o"]
    sb = lambda n, s, dt=F32: ctx.enter_context(nc.sbuf_tensor(n, s, dt))
    ps = lambda n, s, dt=F32: ctx.enter_context(nc.psum_tensor(n, s, dt))
    xm = T["xm"]
    BIG = 1.0e30

    accum = T["accum"]
    rw_sb = sb("rw_sb", [128, 8, 8], BF16)
    rbb = sb("rbb", [128, 8])
    lg = sb("lg", [128, nto // 128, 8])
    g1 = sb("g1", [128, nto // 128, 8])
    g2 = sb("g2", [128, nto // 128, 8])
    q3 = sb("q3", [128, nto // 128, 8])
    stt5 = sb("stt5", [128, 8 * (nto // 128)])
    wgt_fmb = sb("wgt_fmb", [8, nto], BF16)
    act_sb = sb("act_sb", [128, 16, 2 * TB], BF16)
    wg_sb = sb("wg_sb", [128, 8, TB], BF16)
    wu_sb = sb("wu_sb", [128, 8, TB], BF16)
    dn_sb = sb("dn_sb", [128, 16, 1024], BF16)
    sg = sb("sg", [128, TB], BF16)
    sgd = sb("sgd", [128, TB])
    wbt = sb("wbt", [128, 2 * TB], BF16)
    p_g = ps("p_g", [128, TB])
    p_u = ps("p_u", [128, TB])
    p_d = [ps("p_dA", [128, TB]), ps("p_dB", [128, TB])]
    p_wb = ps("p_wb", [128, 2 * TB])
    p_r = ps("p_r", [128, 8])
    ntt = nto // 128

    f.dma("gpsimd", rw_sb[:], T["r_wT"].rearrange("(kc p) m -> p kc m", p=128),
          writes=("rw_sb",), tag="p5:ldrw")
    f.op("tensor", lambda t: t.matmul(p_r[:], cst["onesr_f"][:], cst["r_b"][:],
                                      start=True, stop=True),
         reads=(), writes=("p_r",), tag="p5:rbb")
    f.op("vector", lambda v: v.tensor_copy(rbb[:], p_r[:]),
         reads=("p_r",), writes=("rbb",), tag="p5:rbbev")

    for tt in range(ntt):
        tsl = slice(tt * 128, tt * 128 + 128)
        def rmm(t, tsl=tsl):
            last = None
            for kc in range(8):
                last = t.matmul(p_r[:], xm[:, kc, tsl], rw_sb[:, kc, :],
                                start=(kc == 0), stop=(kc == 7))
            return last
        f.op("tensor", rmm, reads=("xm", "rw_sb", "rbb"), writes=("p_r",),
             tag=f"p5:rmm{tt}")
        f.op("vector", lambda v, tt=tt: v.tensor_tensor(lg[:, tt, :], p_r[:], rbb[:],
                                                        ALU.add),
             reads=("p_r", "rbb"), writes=("lg",), tag=f"p5:lgev{tt}")

    f.op("scalar", lambda s: s.activation(lg[:], lg[:], AF.Tanh, scale=0.5),
         reads=("lg",), writes=("lg",), tag="p5:tanh")
    f.op("vector", lambda v: v.tensor_scalar(out=lg[:], in0=lg[:], scalar1=0.5,
                                             scalar2=0.5, op0=ALU.mult, op1=ALU.add),
         reads=("lg",), writes=("lg",), tag="p5:probs")

    OF_M1, OF_I1, OF_M2, OF_I2, OF_W1, OF_W2, OF_T = (i * ntt for i in range(7))
    iob = cst["iota7"][:, 0:8].unsqueeze(1).broadcast_to([128, ntt, 8])
    bcc = lambda off: stt5[:, off:off + ntt].unsqueeze(2).broadcast_to([128, ntt, 8])

    def top2(v):
        v.tensor_reduce(stt5[:, OF_M1:OF_M1 + ntt], lg[:], axis=AX.X, op=ALU.max)
        v.drain()
        v.tensor_tensor(g1[:], lg[:], bcc(OF_M1), ALU.is_ge)
        v.drain()
        v.tensor_scalar(out=q3[:], in0=g1[:], scalar1=-BIG, scalar2=BIG,
                        op0=ALU.mult, op1=ALU.add)
        v.drain()
        v.tensor_tensor(q3[:], q3[:], iob, ALU.add)
        v.drain()
        v.tensor_reduce(stt5[:, OF_I1:OF_I1 + ntt], q3[:], axis=AX.X, op=ALU.min)
        v.drain()
        v.tensor_tensor(g1[:], iob, bcc(OF_I1), ALU.is_equal)
        v.drain()
        v.scalar_tensor_tensor(out=q3[:], in0=g1[:], scalar=-BIG, in1=lg[:],
                               op0=ALU.mult, op1=ALU.add)
        v.drain()
        v.tensor_reduce(stt5[:, OF_M2:OF_M2 + ntt], q3[:], axis=AX.X, op=ALU.max)
        v.drain()
        v.tensor_tensor(g2[:], q3[:], bcc(OF_M2), ALU.is_ge)
        v.drain()
        v.tensor_scalar(out=q3[:], in0=g2[:], scalar1=-BIG, scalar2=BIG,
                        op0=ALU.mult, op1=ALU.add)
        v.drain()
        v.tensor_tensor(q3[:], q3[:], iob, ALU.add)
        v.drain()
        v.tensor_reduce(stt5[:, OF_I2:OF_I2 + ntt], q3[:], axis=AX.X, op=ALU.min)
        v.drain()
        v.tensor_tensor(g2[:], iob, bcc(OF_I2), ALU.is_equal)
        v.drain()
        v.tensor_tensor(stt5[:, OF_T:OF_T + ntt], stt5[:, OF_M1:OF_M1 + ntt],
                        stt5[:, OF_M2:OF_M2 + ntt], ALU.add)
        v.drain()
        v.reciprocal(stt5[:, OF_T:OF_T + ntt], stt5[:, OF_T:OF_T + ntt])
        v.drain()
        v.tensor_tensor(stt5[:, OF_W1:OF_W1 + ntt], stt5[:, OF_M1:OF_M1 + ntt],
                        stt5[:, OF_T:OF_T + ntt], ALU.mult)
        v.tensor_tensor(stt5[:, OF_W2:OF_W2 + ntt], stt5[:, OF_M2:OF_M2 + ntt],
                        stt5[:, OF_T:OF_T + ntt], ALU.mult)
        v.drain()
        v.tensor_tensor(g1[:], g1[:], bcc(OF_W1), ALU.mult)
        v.tensor_tensor(g2[:], g2[:], bcc(OF_W2), ALU.mult)
        v.drain()
        v.tensor_tensor(g1[:], g1[:], g2[:], ALU.add)
        v.drain()
        return v.nop()
    f.op("vector", top2, reads=("lg",), writes=("g1", "g2", "q3", "stt5"),
         tag="p5:top2")

    for half in range((ntt + 7) // 8):
        tts = range(half * 8, min(ntt, half * 8 + 8))
        def wtr(t, tts=tts, half=half):
            last = None
            for tt in tts:
                last = t.transpose(p_wb[0:8, (tt - half * 8) * 128:(tt - half * 8) * 128 + 128],
                                   g1[:, tt, 0:8], cst["ident_f"][:])
            return last
        f.op("tensor", wtr, reads=("g1", "p_wb"), writes=("p_wb",), tag=f"p5:wtr{half}")
        n0 = half * 1024
        n1 = min(nto, n0 + 1024)
        f.op("vector", lambda v, n0=n0, n1=n1: v.tensor_copy(
            wgt_fmb[0:8, n0:n1], p_wb[0:8, 0:n1 - n0]),
            reads=("p_wb",), writes=("wgt_fmb",), tag=f"p5:wev{half}")

    NH2 = nto // (2 * TB) if nto >= 2 * TB else 1
    HLEN = min(2 * TB, nto)
    ntc2 = HLEN // TB
    for th in range(NH2):
        h0 = th * HLEN
        for e in range(NE):
            ke = f"p5.{th}.{e}"
            for ib in range(4):
                io = ib * TB
                for sub in range(4):
                    f.dma("gpsimd", wg_sb[:, :, 128 * sub:128 * sub + 128],
                          T["moe_gT"][e].rearrange("(kc p) m -> p kc m", p=128)
                          [:, :, io + 128 * sub:io + 128 * sub + 128],
                          writes=(f"wg_sb{sub}",), tag=ke + f":ldg{ib}.{sub}")
                for sub in range(4):
                    f.dma("gpsimd", wu_sb[:, :, 128 * sub:128 * sub + 128],
                          T["moe_uT"][e].rearrange("(kc p) m -> p kc m", p=128)
                          [:, :, io + 128 * sub:io + 128 * sub + 128],
                          writes=(f"wu_sb{sub}",), tag=ke + f":ldu{ib}.{sub}")
                for tc2 in range(ntc2):
                    tsl = slice(h0 + tc2 * TB, h0 + tc2 * TB + TB)
                    for sub in range(4):
                        ic = 4 * ib + sub
                        def gmm(t, sub=sub, tsl=tsl):
                            last = None
                            for kc in range(8):
                                last = t.matmul(p_g[:],
                                                wg_sb[:, kc, 128 * sub:128 * sub + 128],
                                                xm[:, kc, tsl],
                                                start=(kc == 0), stop=(kc == 7))
                            return last
                        f.op("tensor", gmm, reads=(f"wg_sb{sub}", "xm"), writes=("p_g",),
                             tag=ke + f":g{ib}.{tc2}.{sub}")
                        def umm(t, sub=sub, tsl=tsl):
                            last = None
                            for kc in range(8):
                                last = t.matmul(p_u[:],
                                                wu_sb[:, kc, 128 * sub:128 * sub + 128],
                                                xm[:, kc, tsl],
                                                start=(kc == 0), stop=(kc == 7))
                            return last
                        f.op("tensor", umm, reads=(f"wu_sb{sub}", "xm"), writes=("p_u",),
                             tag=ke + f":u{ib}.{tc2}.{sub}")
                        f.op("scalar", lambda s: s.activation(sg[:], p_g[:], AF.Silu),
                             reads=("p_g",), writes=("sg",), tag=ke + f":si{ib}.{tc2}.{sub}")
                        f.op("vector", lambda v, ic=ic, tc2=tc2: v.tensor_tensor(
                            act_sb[:, ic, tc2 * TB:tc2 * TB + TB], p_u[:], sg[:], ALU.mult),
                            reads=("p_u", "sg"), writes=("act_sb",),
                            tag=ke + f":am{ib}.{tc2}.{sub}")
            for icg in range(4):
                f.dma("gpsimd", dn_sb[:, 4 * icg:4 * icg + 4, :],
                      T["moe_dT"][e].rearrange("(ic p) m -> p ic m", p=128)
                      [:, 4 * icg:4 * icg + 4, :],
                      writes=(f"dn_sb{icg}",), tag=ke + f":ldd{icg}")
            if e > 0:
                def wbc(t, e=e, h0=h0):
                    last = None
                    for tc2 in range(ntc2):
                        last = t.matmul(p_wb[:, tc2 * TB:tc2 * TB + TB],
                                        cst["esel"][:, 128 * (e - 1):128 * (e - 1) + 128],
                                        wgt_fmb[0:8, h0 + tc2 * TB:h0 + tc2 * TB + TB],
                                        start=True, stop=True)
                    return last
                f.op("tensor", wbc, reads=("wgt_fmb", "p_wb"), writes=("p_wb",),
                     tag=ke + ":wbc")
                f.op("vector", lambda v: v.tensor_copy(wbt[:, 0:ntc2 * TB],
                                                       p_wb[:, 0:ntc2 * TB]),
                     reads=("p_wb",), writes=("wbt",), tag=ke + ":wbev")
            for tc2 in range(ntc2):
                tsl = slice(h0 + tc2 * TB, h0 + tc2 * TB + TB)
                for hc in range(8):
                    dp = hc % 2
                    def dmm(t, hc=hc, tc2=tc2, dp=dp):
                        last = None
                        for ic in range(16):
                            last = t.matmul(p_d[dp][:], dn_sb[:, ic, 128 * hc:128 * hc + 128],
                                            act_sb[:, ic, tc2 * TB:tc2 * TB + TB],
                                            start=(ic == 0), stop=(ic == 15))
                        return last
                    f.op("tensor", dmm,
                         reads=("dn_sb0", "dn_sb1", "dn_sb2", "dn_sb3", "act_sb"),
                         writes=(f"p_d{dp}",), tag=ke + f":d{tc2}.{hc}")
                    if e == 0:
                        f.op("vector", lambda v, hc=hc, tsl=tsl, dp=dp: v.tensor_copy(
                            accum[:, hc, tsl], p_d[dp][:]),
                            reads=(f"p_d{dp}",), writes=("accum",), tag=ke + f":ac{tc2}.{hc}")
                    else:
                        def acc(v, hc=hc, tsl=tsl, tc2=tc2, dp=dp):
                            v.tensor_tensor(sgd[:], p_d[dp][:], wbt[:, tc2 * TB:tc2 * TB + TB],
                                            ALU.mult)
                            v.drain()
                            v.tensor_tensor(accum[:, hc, tsl], accum[:, hc, tsl],
                                            sgd[:], ALU.add)
                            v.drain()
                            return v.nop()
                        f.op("vector", acc, reads=(f"p_d{dp}", "wbt", "accum"),
                             writes=("accum", "sgd"), tag=ke + f":ac{tc2}.{hc}")

    d = T["dout"]("moeo", [128, 8, nto])
    if d is not None:
        f.dma("sync", d, accum[:], reads=("accum",), tag="dbg:moe")


def emit_p6(f, nc, T, ctx):
    cst = T["cst"]; nto = T["nto"]; ntb = T["ntb_o"]
    sb = lambda n, s, dt=F32: ctx.enter_context(nc.sbuf_tensor(n, s, dt))
    accum = T["accum"]
    y1t = sb("y1t6", [128, 8, TB])
    x2t = sb("x2t6", [128, 8, TB])
    scf = sb("scf6", [128, 8, TB])
    outt = sb("outt", [128, 8, TB])
    ln = LNH(f, nc, cst, ctx, "l6_")

    for tb in range(ntb):
        sl = slice(tb * TB, (tb + 1) * TB)
        k = f"p6.{tb}"
        f.dma("sync", y1t[:], fm3(T["y1_spill"])[:, :, sl], reads=("y1_spill",),
              writes=("y1t6",), tag=k + ":ldy1")
        f.dma("sync", x2t[:], fm3(T["x2_spill"])[:, :, sl], reads=("x2_spill",),
              writes=("x2t6",), tag=k + ":ldx2")

        def y2c(v, sl=sl):
            v.tensor_tensor(y1t[:], y1t[:], accum[:, :, sl], ALU.add)
            v.drain()
            return v.nop()
        f.op("vector", y2c, reads=("y1t6", "accum"), writes=("y1t6",), tag=k + ":y2")

        p_bc = ln.stats(k + ":fl", y1t[:], "y1t6", False, 0)

        def onorm(v):
            v.tensor_tensor(scf[:], y1t[:],
                            p_bc[:, 0:TB].unsqueeze(1).broadcast_to([128, 8, TB]), ALU.subtract)
            v.drain()
            v.tensor_tensor(scf[:], scf[:],
                            p_bc[:, TB:2 * TB].unsqueeze(1).broadcast_to([128, 8, TB]), ALU.mult)
            v.drain()
            v.tensor_tensor(outt[:], scf[:], x2t[:], ALU.add)
            v.drain()
            return v.nop()
        f.op("vector", onorm, reads=("y1t6", "l6_p_bc", "x2t6"),
             writes=("outt", "scf6"), tag=k + ":on")
        f.dma("sync", fm3(T["out"])[:, :, sl], outt[:], reads=("outt",),
              writes=("out",), tag=k + ":st")


def _cc(a):
    return np.ascontiguousarray(a)


def prep_in_maps(inputs, seq=L):
    ii = {k: np.asarray(v) for k, v in inputs.items()}
    x = ii["L0_attn_hidden_states"].astype(np.float32, copy=False)
    nto = seq // 4
    nch = seq // C

    def A(name):
        return ii[name]

    ident_f = np.eye(128, dtype=np.float32)
    import ml_dtypes as _mld
    ident_b = np.eye(128, dtype=_mld.bfloat16)
    triuT_b = np.triu(np.ones((64, 64), np.float32)).astype("bfloat16_np", copy=False) \
        if False else np.triu(np.ones((64, 64))).astype(np.float32)
    import ml_dtypes
    bf = ml_dtypes.bfloat16
    triuT_b = np.triu(np.ones((64, 64))).astype(bf)
    i64z = np.zeros((128, 64), bf); i64z[0:64] = np.eye(64)
    triuTz = np.zeros((128, 64), bf); triuTz[0:64] = np.triu(np.ones((64, 64)))
    ones64 = np.ones((64, 64), bf)
    trimask4 = np.tile(np.triu(np.ones((64, 64), np.float32)), (1, 4))
    e2_b = np.zeros((128, 2), bf); e2_b[0:64, 0] = 1; e2_b[64:128, 1] = 1
    eb2_b = np.zeros((2, 128), bf); eb2_b[0, 0:64] = 1; eb2_b[1, 64:128] = 1
    onesr_b = np.ones((1, 128), bf)
    onesr_f = np.ones((1, 128), np.float32)
    onescol_b = np.ones((128, 1), bf)
    wmask = np.zeros((128, 128), np.float32)
    wmask[0:64, 0:64] = 1; wmask[64:128, 64:128] = 1
    amask = np.zeros((128, 4, TB), bf)
    for o in range(4):
        jj = 128 * o + np.arange(128)[:, None]
        amask[:, o, :] = (jj <= np.arange(TB)[None, :]).astype(bf)
    r4 = np.zeros((32, 32), np.float32)
    for dp in range(16):
        r4[dp + 16, dp] = -1.0
        r4[dp, dp + 16] = 1.0
    r4T = np.zeros((128, 128), np.float32)
    for blkk in range(4):
        r4T[32 * blkk:32 * blkk + 32, 32 * blkk:32 * blkk + 32] = r4
    iota7 = np.broadcast_to(np.arange(8, dtype=np.float32), (128, 8)).copy()
    epsv = np.zeros((64, 8), np.float32)
    epsv[:, 0] = 1e-5; epsv[:, 1] = 1e-6; epsv[:, 2] = 1e-24
    esel = np.zeros((8, 1024), bf)
    for e in range(8):
        esel[e, 128 * e:128 * e + 128] = 1
    inv = 1.0 / 10000.0 ** (np.arange(0, RD, 2, dtype=np.float32) / RD)
    tpos = np.arange(seq, dtype=np.float32)
    fr = tpos[:, None] * inv[None, :]
    emb = np.concatenate([fr, fr], -1)
    cos1 = np.cos(emb).T.astype(np.float32)
    sin1 = np.sin(emb).T.astype(np.float32)
    cos_t = np.tile(cos1, (4, 1)); sin_t = np.tile(sin1, (4, 1))

    woT = _cc(A("L0_attn_wo_w").T)
    wo_b = _cc(A("L0_attn_wo_b").reshape(8, 128).T)
    post_nw = _cc(A("L0_attn_post_norm_w").reshape(8, 128).T)
    post_nb = _cc(A("L0_attn_post_norm_b").reshape(8, 128).T)
    ln1_w = _cc(A("L0_ffn_ln1_w").reshape(8, 128).T)
    ln2_w = _cc(A("L0_ffn_ln2_w").reshape(8, 128).T)
    kv_dT = _cc(A("L0_ffn_kv_d").T); q_dT = _cc(A("L0_ffn_q_d").T)
    v_u = A("L0_ffn_v_u"); k_u = A("L0_ffn_k_u"); q_u = A("L0_ffn_q_u")
    rope_q = A("L0_ffn_rope_q_w"); rope_k = A("L0_ffn_rope_k_w")
    o_wT = _cc(A("L0_ffn_o_w").T)
    r_wT = np.zeros((D, 8), np.float32)
    r_wT[:, 0:7] = A("L0_ffn_router_w").T
    r_b = np.zeros((1, 8), np.float32)
    r_b[0, 0:7] = A("L0_ffn_routing_bias")
    r_b[0, 7] = -1.0e9
    moe_gT = np.zeros((NE, D, EI), np.float32)
    moe_uT = np.zeros((NE, D, EI), np.float32)
    moe_dT = np.zeros((NE, EI, D), np.float32)
    moe_gT[0] = A("L0_ffn_shared_gate")[0].T
    moe_uT[0] = A("L0_ffn_shared_up")[0].T
    moe_dT[0] = A("L0_ffn_shared_down")[0].T
    for e in range(7):
        moe_gT[e + 1] = A("L0_ffn_routed_gate")[e].T
        moe_uT[e + 1] = A("L0_ffn_routed_up")[e].T
        moe_dT[e + 1] = A("L0_ffn_routed_down")[e].T

    wq, wk, wv = A("L0_attn_wq_w"), A("L0_attn_wk_w"), A("L0_attn_wv_w")
    wqb, wkb, wvb = A("L0_attn_wq_b"), A("L0_attn_wk_b"), A("L0_attn_wv_b")
    gw = A("L0_attn_ttt_norm_weight"); gb = A("L0_attn_ttt_norm_bias")
    lrw = A("L0_attn_ttt_lr_weight"); lrb = A("L0_attn_ttt_lr_bias")
    W1 = A("L0_attn_W1"); b1 = A("L0_attn_b1")

    xfm = np.ascontiguousarray(x.transpose(0, 2, 1))

    in_maps = []
    for c in range(8):
        b, q = c // 4, c % 4
        hsl = slice(256 * q, 256 * q + 256)
        hh = slice(4 * q, 4 * q + 4)
        m = dict(
            x_own=_cc(xfm[b][:, nto * q:nto * q + nto]),
            wqT=_cc(wq[hsl].T), wkT=_cc(wk[hsl].T), wvT=_cc(wv[hsl].T),
            qkv_b=_cc(np.stack([wqb[hsl].reshape(2, 128).T,
                                wkb[hsl].reshape(2, 128).T,
                                wvb[hsl].reshape(2, 128).T], axis=-1)),
            lrwT=_cc(lrw[hh, 0, :].T), lr_b=_cc(lrb[hh].reshape(4, 1)),
            ttt_g2=_cc(gw[hh].reshape(256).reshape(2, 128).T),
            ttt_b2=_cc(gb[hh].reshape(256).reshape(2, 128).T),
            ttt_g_tm=np.broadcast_to(gw[hh].reshape(256), (64, 256)).copy(),
            ttt_b_tm=np.broadcast_to(gb[hh].reshape(256), (64, 256)).copy(),
            W2_0=_blkdiag(W1[4 * q], W1[4 * q + 1]),
            W2_1=_blkdiag(W1[4 * q + 2], W1[4 * q + 3]),
            b2_bc=np.broadcast_to(b1[hh].reshape(256), (64, 256)).copy(),
            post_nw=post_nw, post_nb=post_nb, woT=woT, wo_b=wo_b,
            ln1_w=ln1_w, ln2_w=ln2_w, kv_dT=kv_dT, q_dT=q_dT,
            k_uT=_cc(k_u.reshape(H, HD, 256)[hh, 32:64].reshape(128, 256).T),
            q_uT=_cc(q_u.reshape(H, HD, 256)[hh, 32:64].reshape(128, 256).T),
            v_uT=_cc(v_u.reshape(H, HD, 256)[hh].reshape(256, 256).T),
            rope_qT=_cc(rope_q.reshape(H, RD, 256)[hh].reshape(128, 256).T),
            rope_kT=_cc(rope_k.reshape(H, RD, D)[hh].reshape(128, D).T),
            o_wT=o_wT, cos_t=cos_t, sin_t=sin_t, r_wT=r_wT, r_b=r_b,
            moe_gT=moe_gT, moe_uT=moe_uT, moe_dT=moe_dT,
            ident_f=ident_f, ident_b=ident_b, triuT_b=triuT_b, i64z=i64z,
            triuTz=triuTz, ones64=ones64, trimask4=trimask4,
            e2_b=e2_b, eb2_b=eb2_b, onesr_b=onesr_b, onesr_f=onesr_f,
            onescol_b=onescol_b, wmask=wmask, amask=amask, r4T=r4T, iota7=iota7,
            epsv=epsv, esel=esel,
        )
        in_maps.append(m)
    return in_maps


def _blkdiag(a, bmat):
    W = np.zeros((128, 128), np.float32)
    W[0:64, 0:64] = a
    W[64:128, 64:128] = bmat
    return W


_BUILT = {}


def kernel(**inputs):
    seq = int(np.asarray(inputs["L0_attn_hidden_states"]).shape[1])
    key = seq
    if key not in _BUILT:
        _BUILT[key] = build(seq=seq)
    nc, _ = _BUILT[key]
    in_maps = prep_in_maps(inputs, seq=seq)
    res = run_bass_kernel_spmd(nc, in_maps, core_ids=list(range(8)))
    nto = seq // 4
    out = np.zeros((B, seq, D), np.float32)
    for c in range(8):
        b, q = c // 4, c % 4
        out[b, nto * q:nto * q + nto, :] = np.asarray(res.results[c]["out"]).T
    return out


# version 17
# speedup vs baseline: 1.0146x; 1.0146x over previous
import numpy as np
from contextlib import ExitStack

import concourse.bass as bass
import concourse.mybir as mybir
from concourse.bass_utils import run_bass_kernel_spmd

F32 = mybir.dt.float32
BF16 = mybir.dt.bfloat16
AF = mybir.ActivationFunctionType
ALU = mybir.AluOpType
AX = mybir.AxisListType

B, L, D = 2, 8192, 1024
H, HD, C = 16, 64, 64
LAT, RD = 256, 32
NE, EI = 8, 2048
TB = 512
GROUPS = [[0, 1, 2, 3], [4, 5, 6, 7]]

ENGS = ("tensor", "vector", "scalar", "gpsimd", "sync")
_DMA_Q = {"sync": "dma_sy", "gpsimd": "dma_gp", "scalar": "dma_sc"}


class Op:
    __slots__ = ("i", "eng", "emit", "reads", "writes", "kind", "needs_inc",
                 "inc_no", "waits", "drain_before", "tag")

    def __init__(self, i, eng, emit, reads, writes, kind, tag):
        self.i, self.eng, self.emit = i, eng, emit
        self.reads, self.writes = reads, writes
        self.kind, self.tag = kind, tag
        self.needs_inc = False
        self.inc_no = None
        self.waits = []
        self.drain_before = False


class Flow:

    NQ = 8
    NCQ = 4

    def __init__(self, nc, ctx):
        self.nc = nc
        self.ops = []
        self.sems = {}
        for e in ENGS:
            self.sems["e_" + e] = ctx.enter_context(nc.semaphore("e_" + e))
        for q in ("dma_sy", "dma_gp", "dma_sc"):
            for i in range(self.NQ):
                self.sems[f"{q}{i}"] = ctx.enter_context(nc.semaphore(f"{q}{i}"))
        for i in range(self.NCQ):
            self.sems[f"coll{i}"] = ctx.enter_context(nc.semaphore(f"coll{i}"))
        self.sems["bar"] = ctx.enter_context(nc.semaphore("bar"))
        self.nbar = 0

    def op(self, eng, emit, reads=(), writes=(), kind="compute", tag=""):
        o = Op(len(self.ops), eng, emit, tuple(reads), tuple(writes), kind, tag)
        self.ops.append(o)
        return o

    def dma(self, eng, out, in_, reads=(), writes=(), tag="", **kw):
        def emit(e):
            return e.dma_start(out=out, in_=in_, **kw)
        return self.op(eng, emit, reads, writes, kind="dma", tag=tag)

    def dma_percore(self, eng, fn, reads=(), writes=(), tag=""):
        return self.op(eng, fn, reads, writes, kind="dma_mc", tag=tag)

    def coll(self, kind, alu, ins, outs, reads, writes, tag=""):
        def emit(g):
            return g.collective_compute(kind, alu, replica_groups=GROUPS,
                                        ins=ins, outs=outs)
        return self.op("gpsimd", emit, reads, writes, kind="coll", tag=tag)

    def barrier(self, tag=""):
        self.nbar += 1
        for e in ENGS:
            self.op(e, None, kind="barrier", tag=f"bar{self.nbar}:{tag}")

    def finalize(self, blk):
        ops = self.ops
        edges = [()] * len(ops)
        lw, rd = {}, {}
        seen_bar = 0
        for o in ops:
            if o.kind == "barrier":
                seen_bar += 1
                if seen_bar % len(ENGS) == 0:
                    lw, rd = {}, {}
                continue
            deps = set()
            for r in o.reads:
                if r in lw:
                    deps.add(lw[r])
            for w in o.writes:
                if w in lw:
                    deps.add(lw[w])
                deps.update(rd.get(w, ()))
            deps.discard(o.i)
            edges[o.i] = tuple(sorted(deps))
            for r in o.reads:
                rd.setdefault(r, []).append(o.i)
            for w in o.writes:
                lw[w] = o.i
                rd[w] = []
        for o in ops:
            for pi in edges[o.i]:
                pr = ops[pi]
                if pr.kind == "compute" and pr.eng != o.eng:
                    pr.needs_inc = True
        eng_inc = {e: 0 for e in ENGS}
        q_cnt = {"dma_sy": 0, "dma_gp": 0, "dma_sc": 0, "coll": 0}
        for o in ops:
            if o.kind == "compute" and o.needs_inc:
                eng_inc[o.eng] += 1
                o.inc_no = eng_inc[o.eng]
            elif o.kind in ("dma", "dma_mc"):
                q = _DMA_Q[o.eng]
                j = q_cnt[q]; q_cnt[q] += 1
                o.inc_no = (f"{q}{j % self.NQ}", 16 * (j // self.NQ + 1))
            elif o.kind == "coll":
                j = q_cnt["coll"]; q_cnt["coll"] += 1
                o.inc_no = (f"coll{j % self.NCQ}", j // self.NCQ + 1)
        vpos = {}
        _vp = 0
        for o in ops:
            if o.eng == "vector" and o.kind == "compute":
                vpos[o.i] = _vp
                _vp += 1
        wm = {e: {} for e in ENGS}
        last_drain = {e: -1 for e in ENGS}
        qmax = {e: {} for e in ENGS}
        for o in ops:
            if o.kind == "barrier":
                w = wm[o.eng]
                for s, v in sorted(qmax[o.eng].items()):
                    if w.get(s, 0) < v:
                        o.waits.append((s, v))
                        w[s] = v
                continue
            need = {}
            for pi in edges[o.i]:
                pr = ops[pi]
                if pr.kind == "compute":
                    if pr.eng == o.eng:
                        if pr.eng == "vector":
                            if (pr.i > last_drain[o.eng]
                                    and vpos.get(o.i, 0) - vpos.get(pr.i, 0) < 2):
                                o.drain_before = True
                        elif pr.eng != "tensor" and pr.i > last_drain[o.eng]:
                            o.drain_before = True
                    else:
                        need["e_" + pr.eng] = max(need.get("e_" + pr.eng, 0), pr.inc_no)
                elif pr.kind in ("dma", "dma_mc", "coll"):
                    s, v = pr.inc_no
                    need[s] = max(need.get(s, 0), v)
            if o.kind in ("dma", "dma_mc", "coll"):
                s, v = o.inc_no
                prev = v - (16 if o.kind != "coll" else 1)
                if prev > 0:
                    need[s] = max(need.get(s, 0), prev)
                qmax[o.eng][s] = v
            if o.drain_before:
                last_drain[o.eng] = o.i - 1
            w = wm[o.eng]
            for s, v in sorted(need.items()):
                if w.get(s, 0) < v:
                    o.waits.append((s, v))
                    w[s] = v
        per = {e: [o for o in ops if o.eng == e] for e in ENGS}
        flow = self

        def run(eng_obj, ename):
            bar_idx = 0
            for o in per[ename]:
                if o.kind == "barrier":
                    bar_idx += 1
                    for s, v in o.waits:
                        eng_obj.wait_ge(flow.sems[s], v)
                    if ename in ("vector", "scalar"):
                        eng_obj.drain()
                    if ename == "tensor":
                        eng_obj.nop()
                    eng_obj.sem_inc(flow.sems["bar"], 1)
                    eng_obj.wait_ge(flow.sems["bar"], bar_idx * len(ENGS))
                    continue
                for s, v in o.waits:
                    eng_obj.wait_ge(flow.sems[s], v)
                if o.drain_before:
                    eng_obj.drain()
                if o.kind == "dma_mc":
                    for arm in self.nc.sync.switch_core_id(n=128):
                        try:
                            core = arm.logical
                        except Exception:
                            core = int(arm) % 8
                        instr = o.emit(eng_obj, core)
                        instr.then_inc(flow.sems[o.inc_no[0]], 16)
                    continue
                instr = o.emit(eng_obj)
                if o.kind == "compute" and o.needs_inc:
                    instr.then_inc(flow.sems["e_" + ename], 1)
                elif o.kind == "dma":
                    instr.then_inc(flow.sems[o.inc_no[0]], 16)
                elif o.kind == "coll":
                    instr.then_inc(flow.sems[o.inc_no[0]], 1)

        @blk.tensor
        def _(t):
            run(t, "tensor")

        @blk.vector
        def _(v):
            run(v, "vector")

        @blk.scalar
        def _(s):
            run(s, "scalar")

        @blk.gpsimd
        def _(g):
            run(g, "gpsimd")

        @blk.sync
        def _(sy):
            run(sy, "sync")


def fm3(ap):
    return ap.rearrange("(kc p) t -> p kc t", p=128)


def build(seq=L, dbg=(), stop_after="p6", scan_steps=None, scan_groups=None):
    dbg = set(dbg)
    nto = seq // 4
    nch = seq // C
    ntb_o = nto // TB
    ntb_s = seq // TB
    nc = bass.Bass("TRN2", target_bir_lowering=False, debug=False)
    T = {}

    def din(name, shape, dt=F32):
        ap = nc.dram_tensor(name, shape, dt, kind="ExternalInput").ap()
        T[name] = ap
        return ap

    def dint(name, shape, dt=BF16):
        ap = nc.dram_tensor(name, shape, dt, kind="Internal").ap()
        T[name] = ap
        return ap

    din("x_own", [D, nto])
    din("wqT", [D, 256]); din("wkT", [D, 256]); din("wvT", [D, 256])
    din("qkv_b", [128, 2, 3])
    din("lrwT", [D, 4]); din("lr_b", [4, 1])
    din("ttt_g2", [128, 2]); din("ttt_b2", [128, 2])
    din("ttt_g_tm", [64, 256]); din("ttt_b_tm", [64, 256])
    din("W2_0", [128, 128]); din("W2_1", [128, 128]); din("b2_bc", [64, 256])
    din("post_nw", [128, 8]); din("post_nb", [128, 8])
    din("woT", [D, D]); din("wo_b", [128, 8])
    din("ln1_w", [128, 8]); din("ln2_w", [128, 8])
    din("kv_dT", [D, LAT]); din("q_dT", [D, LAT])
    din("k_uT", [LAT, 128]); din("q_uT", [LAT, 128]); din("v_uT", [LAT, 256])
    din("rope_qT", [LAT, 128]); din("rope_kT", [D, 128])
    din("o_wT", [D, D])
    din("cos_t", [128, seq]); din("sin_t", [128, seq])
    din("r_wT", [D, 8]); din("r_b", [1, 8])
    din("moe_gT", [NE, D, EI]); din("moe_uT", [NE, D, EI]); din("moe_dT", [NE, EI, D])
    din("ident_f", [128, 128])
    din("ident_b", [128, 128], BF16)
    din("triuT_b", [64, 64], BF16)
    din("i64z", [128, 64], BF16)
    din("triuTz", [128, 64], BF16)
    din("ones64", [64, 64], BF16)
    din("trimask4", [64, 256])
    din("e2_b", [128, 2], BF16)
    din("eb2_b", [2, 128], BF16)
    din("onesr_b", [1, 128], BF16)
    din("onesr_f", [1, 128])
    din("onescol_b", [128, 1], BF16)
    din("wmask", [128, 128])
    din("amask", [128, 4, TB], BF16)
    din("r4T", [128, 128])
    din("iota7", [128, 8])
    din("esel", [8, 1024], BF16)
    din("epsv", [64, 8])

    T["out"] = nc.dram_tensor("out", [D, nto], F32, kind="ExternalOutput").ap()

    dbg_out = {}

    def dout(name, shape, dt=F32):
        if name in dbg:
            dbg_out[name] = nc.dram_tensor("dbg_" + name, shape, dt,
                                           kind="ExternalOutput").ap()
        return dbg_out.get(name)
    T["dout"] = dout

    NCB_O = nto // TB
    NCB_T = seq // 2048 if seq >= 2048 else 1
    CW = seq // NCB_T
    for c in range(NCB_O):
        dint(f"h1_pub{c}", [D, TB]); dint(f"h1_gat{c}", [4 * D, TB])
        dint(f"a_pub{c}", [D, TB]); dint(f"a_gat{c}", [4 * D, TB])
    for c in range(NCB_T):
        dint(f"ymix_pub{c}", [256, CW]); dint(f"ymix_gat{c}", [1024, CW])
        dint(f"yat_pub{c}", [256, CW]); dint(f"yat_gat{c}", [1024, CW])
    T.update(NCB_O=NCB_O, NCB_T=NCB_T, CW=CW)
    dint("x2_spill", [D, nto], F32); dint("y0_spill", [D, nto], F32)
    dint("y1_spill", [D, nto], F32)
    dint("rot_sp", [256, seq])

    T.update(seq=seq, nto=nto, nch=nch, ntb_o=ntb_o, ntb_s=ntb_s)

    with ExitStack() as root:
        f = Flow(nc, root)
        T["f"] = f
        cst = {}
        for nm, shape, dt in [("ident_f", [128, 128], F32), ("ident_b", [128, 128], BF16),
                              ("triuT_b", [64, 64], BF16), ("i64z", [128, 64], BF16), ("triuTz", [128, 64], BF16), ("ones64", [64, 64], BF16),
                              ("trimask4", [64, 256], F32), ("e2_b", [128, 2], BF16),
                              ("eb2_b", [2, 128], BF16), ("onesr_b", [1, 128], BF16),
                              ("onesr_f", [1, 128], F32), ("onescol_b", [128, 1], BF16),
                              ("wmask", [128, 128], F32), ("qkv_b", [128, 2, 3], F32),
                              ("ttt_g2", [128, 2], F32), ("ttt_b2", [128, 2], F32),
                              ("ttt_g_tm", [64, 256], F32), ("ttt_b_tm", [64, 256], F32),
                              ("lr_b", [4, 1], F32), ("r4T", [128, 128], F32),
                              ("iota7", [128, 8], F32), ("r_b", [1, 8], F32), ("esel", [8, 1024], BF16),
                              ("epsv", [64, 8], F32), ("amask", [128, 4, TB], BF16),
                              ("post_nw", [128, 8], F32),
                              ("post_nb", [128, 8], F32), ("wo_b", [128, 8], F32),
                              ("ln1_w", [128, 8], F32), ("ln2_w", [128, 8], F32)]:
            cst[nm] = root.enter_context(nc.sbuf_tensor("c_" + nm, shape, dt))
            f.dma("sync", cst[nm][:], T[nm], writes=("c_" + nm,), tag="ldc:" + nm)
        T["cst"] = cst
        cst["lrbh"] = root.enter_context(nc.sbuf_tensor("c_lrbh", [4, 1], F32))
        f.op("scalar", lambda s: s.mul(cst["lrbh"][:], cst["lr_b"][:], 0.5),
             reads=("c_lr_b",), writes=("c_lrbh",), tag="lrbh")
        f.barrier("consts")

        with ExitStack() as ph0:
            emit_p0(f, nc, T, ph0)
        if stop_after != "p0":
            with ExitStack() as ph:
                T["XQb"] = ph.enter_context(nc.sbuf_tensor("XQb", [128, 2, seq], BF16))
                T["XKb"] = ph.enter_context(nc.sbuf_tensor("XKb", [128, 2, seq], BF16))
                T["tgt_fm"] = ph.enter_context(nc.sbuf_tensor("tgt_fm", [128, 2, seq], F32))
                T["etaT"] = ph.enter_context(nc.sbuf_tensor("etaT", [64, nch * 4], F32))
                with ExitStack() as ph2:
                    emit_p1a(f, nc, T, ph2)
                f.barrier("p1a")
                if stop_after != "p1a":
                    T["scan_steps"] = scan_steps
                    T["scan_groups"] = scan_groups
                    with ExitStack() as ph2:
                        emit_p1b(f, nc, T, ph2)
                    f.barrier("p1b")
        if stop_after not in ("p0", "p1a", "p1b"):
            with ExitStack() as ph:
                emit_p2(f, nc, T, ph)
            f.barrier("p2")
        if stop_after not in ("p0", "p1a", "p1b", "p2"):
            with ExitStack() as ph:
                emit_p3(f, nc, T, ph)
            f.barrier("p3")
        if stop_after not in ("p0", "p1a", "p1b", "p2", "p3"):
            with ExitStack() as ph:
                T["xm"] = ph.enter_context(nc.sbuf_tensor("xm", [128, 8, nto], BF16))
                with ExitStack() as ph2:
                    emit_p4(f, nc, T, ph2)
                f.barrier("p4")
                T["accum"] = ph.enter_context(nc.sbuf_tensor("accum", [128, 8, nto], F32))
                with ExitStack() as ph2:
                    emit_p5(f, nc, T, ph2)
                f.barrier("p5")
                with ExitStack() as ph2:
                    emit_p6(f, nc, T, ph2)
        f.barrier("final")
        f.finalize(blk=root.enter_context(nc.Block()))
    return nc, dbg_out


def emit_p0(f, nc, T, ctx):
    cst = T["cst"]; nto = T["nto"]; ntb_o = T["ntb_o"]
    dout = T["dout"]
    sb = lambda n, s, dt=F32: ctx.enter_context(nc.sbuf_tensor(n, s, dt))
    ps = lambda n, s, dt=F32: ctx.enter_context(nc.psum_tensor(n, s, dt))

    xf = sb("xf", [128, 8, nto])
    hb = sb("hb", [128, 8, nto], BF16)
    T["xf_p0"] = xf
    xb = sb("xb0", [128, 8, TB], BF16)
    sqb = sb("sqb0", [128, 8, TB], BF16)
    srow = sb("srow0", [1, 4 * TB])
    srow_b = sb("srowb0", [1, 2 * TB], BF16)
    scf = sb("scf0", [128, 8, TB])
    p_sum = ps("p_sum0", [1, TB])
    p_ss = ps("p_ss0", [1, TB])
    p_bc = ps("p_bc0", [128, 2 * TB])

    f.dma("sync", xf[:], fm3(T["x_own"]), writes=("xf",), tag="p0:ldx")

    for tb in range(ntb_o):
        sl = slice(tb * TB, (tb + 1) * TB)
        k = f"p0.{tb}"
        f.op("vector", lambda v, sl=sl: v.tensor_copy(xb[:], xf[:, :, sl]),
             reads=("xf",), writes=("xb",), tag=k + ":cast")
        f.op("scalar", lambda s: s.activation(sqb[:], xb[:], AF.Square),
             reads=("xb",), writes=("sqb",), tag=k + ":sq")

        def mm_sum(t):
            last = None
            for kc in range(8):
                last = t.matmul(p_sum[0:1, :], cst["onescol_b"][:], xb[:, kc, :],
                                start=(kc == 0), stop=(kc == 7))
            return last
        f.op("tensor", mm_sum, reads=("xb",), writes=("p_sum",), tag=k + ":mms")

        def mm_ss(t):
            last = None
            for kc in range(8):
                last = t.matmul(p_ss[0:1, :], cst["onescol_b"][:], sqb[:, kc, :],
                                start=(kc == 0), stop=(kc == 7))
            return last
        f.op("tensor", mm_ss, reads=("sqb",), writes=("p_ss",), tag=k + ":mmss")

        def stats(v):
            v.tensor_scalar(out=srow[0:1, 0:TB], in0=p_sum[0:1, :], scalar1=1.0 / D,
                            scalar2=None, op0=ALU.mult)
            v.tensor_scalar(out=srow[0:1, 3 * TB:4 * TB], in0=p_ss[0:1, :],
                            scalar1=1.0 / D, scalar2=None, op0=ALU.mult)
            v.drain()
            v.tensor_tensor(srow[0:1, TB:2 * TB], srow[0:1, 0:TB], srow[0:1, 0:TB], ALU.mult)
            v.drain()
            v.tensor_tensor(srow[0:1, TB:2 * TB], srow[0:1, 3 * TB:4 * TB],
                            srow[0:1, TB:2 * TB], ALU.subtract)
            v.drain()
            return v.nop()
        f.op("vector", stats, reads=("p_sum", "p_ss"), writes=("srow",), tag=k + ":st")

        def rstd(s):
            s.activation(srow[0:1, 2 * TB:3 * TB], srow[0:1, TB:2 * TB], AF.Ln,
                         bias=cst['epsv'][0:1, 0:1])
            s.drain()
            s.activation(srow[0:1, 2 * TB:3 * TB], srow[0:1, 2 * TB:3 * TB], AF.Exp, scale=-0.5)
            s.drain()
            return s.nop()
        f.op("scalar", rstd, reads=("srow",), writes=("srow.r",), tag=k + ":rstd")

        def tobf(v):
            v.tensor_copy(srow_b[0:1, 0:TB], srow[0:1, 0:TB])
            v.tensor_copy(srow_b[0:1, TB:2 * TB], srow[0:1, 2 * TB:3 * TB])
            v.drain()
            return v.nop()
        f.op("vector", tobf, reads=("srow", "srow.r"), writes=("srow_b",), tag=k + ":bf")

        def bcast(t):
            t.matmul(p_bc[:, 0:TB], cst["onesr_b"][:], srow_b[0:1, 0:TB], start=True, stop=True)
            return t.matmul(p_bc[:, TB:2 * TB], cst["onesr_b"][:], srow_b[0:1, TB:2 * TB],
                            start=True, stop=True)
        f.op("tensor", bcast, reads=("srow_b",), writes=("p_bc",), tag=k + ":bc")

        def norm(v, sl=sl):
            v.tensor_tensor(scf[:], xf[:, :, sl],
                            p_bc[:, 0:TB].unsqueeze(1).broadcast_to([128, 8, TB]), ALU.subtract)
            v.drain()
            v.tensor_tensor(hb[:, :, sl], scf[:],
                            p_bc[:, TB:2 * TB].unsqueeze(1).broadcast_to([128, 8, TB]), ALU.mult)
            v.drain()
            return v.nop()
        f.op("vector", norm, reads=("xf", "p_bc"), writes=("hb", "scf", "xb"),
             tag=k + ":norm")

    for tb in range(ntb_o):
        sl = slice(tb * TB, (tb + 1) * TB)
        f.dma("sync", fm3(T[f"h1_pub{tb}"]), hb[:, :, sl], reads=("hb",),
              writes=(f"h1_pub{tb}",), tag=f"p0:pub{tb}")

    d = dout("h1gat", [4 * D, nto], BF16)
    if d is not None:
        for c in range(T["NCB_O"]):
            f.coll("AllGather", ALU.bypass, [T[f"h1_pub{c}"].opt()],
                   [T[f"h1_gat{c}"].opt()], reads=(f"h1_pub{c}",),
                   writes=(f"h1_gat{c}",), tag=f"p0:ag{c}")
            f.dma("sync", d[:, c * TB:(c + 1) * TB], T[f"h1_gat{c}"],
                  reads=(f"h1_gat{c}",), tag=f"p0:dbg{c}")
        T["_h1_coll_done"] = True
    f.barrier("p0")


def emit_p1a(f, nc, T, ctx):
    cst = T["cst"]; seq = T["seq"]; nto = T["nto"]; ntb = T["ntb_s"]
    sb = lambda n, s, dt=F32: ctx.enter_context(nc.sbuf_tensor(n, s, dt))
    ps = lambda n, s, dt=F32: ctx.enter_context(nc.psum_tensor(n, s, dt))
    XQb, XKb, tgt_fm, etaT = T["XQb"], T["XKb"], T["tgt_fm"], T["etaT"]

    wq_sb = sb("wq_sb", [128, 8, 256], BF16)
    wk_sb = sb("wk_sb", [128, 8, 256], BF16)
    wv_sb = sb("wv_sb", [128, 8, 256], BF16)
    lrw_sb = sb("lrw_sb", [128, 8, 4], BF16)
    hg = [sb("hg0", [128, 8, TB], BF16), sb("hg1", [128, 8, TB], BF16)]
    praw = sb("praw", [128, 2, TB])
    xkf = sb("xkf", [128, 2, TB])
    sqb = sb("sqb1", [128, 2, TB], BF16)
    dvec = sb("dvec", [128, 2, TB])
    dvb = sb("dvb", [128, 2, TB], BF16)
    st = sb("st_p1", [2, 4 * TB])
    stb = sb("stb_p1", [2, 6 * TB], BF16)
    etrow = sb("etrow", [4, TB])
    p_q = ps("p_q", [128, TB])
    p_b1 = ps("p_b1", [128, 2 * TB])
    p_mix = ps("p_mix", [64, 4 * TB])
    p_eta = ps("p_eta", [4, TB])

    r3 = lambda ap: ap.rearrange("(kc p) m -> p kc m", p=128)
    f.dma("gpsimd", wq_sb[:], r3(T["wqT"]), writes=("wq_sb",), tag="p1:ldwq")
    f.dma("gpsimd", wk_sb[:], r3(T["wkT"]), writes=("wk_sb",), tag="p1:ldwk")
    f.dma("gpsimd", wv_sb[:], r3(T["wvT"]), writes=("wv_sb",), tag="p1:ldwv")
    f.dma("gpsimd", lrw_sb[:], r3(T["lrwT"]), writes=("lrw_sb",), tag="p1:ldlr")

    if not T.get("_h1_coll_done"):
        for c in range(T["NCB_O"]):
            f.coll("AllGather", ALU.bypass, [T[f"h1_pub{c}"].opt()],
                   [T[f"h1_gat{c}"].opt()], reads=(f"h1_pub{c}",),
                   writes=(f"h1_gat{c}",), tag=f"p1:ag{c}")

    RN_K = 0; MU_D = 2; SD_D = 4; RN_Q = 0

    def proj(t, wsb, mc, par, extra_sq_bias=None):
        last = None
        for kc in range(8):
            last = t.matmul(p_q[:], wsb[:, kc, 128 * mc:128 * mc + 128],
                            hg[par][:, kc, :], start=(kc == 0), stop=(kc == 7))
        return last

    for gtb in range(ntb):
        par = gtb % 2
        sl = slice(gtb * TB, (gtb + 1) * TB)
        k = f"p1.{gtb}"
        t0 = gtb * TB
        r = t0 // nto
        loc = t0 - r * nto
        ch = loc // TB
        src = T[f"h1_gat{ch}"][r * 1024:(r + 1) * 1024, :] \
            .rearrange("(kc p) t -> p kc t", p=128)
        f.dma("sync", hg[par][:], src, reads=(f"h1_gat{ch}",), writes=(f"hg{par}",),
              tag=k + ":ld")

        for mc in range(2):
            f.op("tensor", lambda t, mc=mc, par=par: proj(t, wk_sb, mc, par),
                 reads=(f"hg{par}", "wk_sb"), writes=("p_q",), tag=k + f":kmm{mc}")

            f.op("scalar", lambda s, mc=mc: s.activation(
                praw[:, mc, :], p_q[:], AF.Identity, bias=cst["qkv_b"][:, mc, 1:2]),
                reads=("p_q",), writes=(f"praw{mc}",), tag=k + f":kraw{mc}")
            f.op("vector", lambda v, mc=mc: v.tensor_tensor(
                sqb[:, mc, :], praw[:, mc, :], praw[:, mc, :], ALU.mult),
                reads=(f"praw{mc}", f"sqb{mc}"), writes=(f"sqb{mc}",),
                tag=k + f":krsq{mc}")
            f.op("tensor", lambda t, mc=mc: t.matmul(
                p_mix[0:2, mc * TB:mc * TB + TB], cst["e2_b"][:], sqb[:, mc, :],
                start=True, stop=True),
                reads=(f"sqb{mc}",), writes=(f"p_mix{mc}",), tag=k + f":kss{mc}")

        def k_lnexp(s):
            s.activation(st[0:2, RN_K * TB:(RN_K + 2) * TB], p_mix[0:2, 0:2 * TB],
                         AF.Ln, bias=cst["epsv"][0:2, 2:3])
            s.drain()
            s.activation(stb[0:2, RN_K * TB:(RN_K + 2) * TB],
                         st[0:2, RN_K * TB:(RN_K + 2) * TB], AF.Exp, scale=-0.5)
            s.drain()
            return s.nop()
        f.op("scalar", k_lnexp, reads=("p_mix0", "p_mix1", "stb_k"),
             writes=("st_k", "stb_k"), tag=k + ":klnexp")

        f.op("tensor", lambda t: (
            t.matmul(p_b1[:, 0:TB], cst["eb2_b"][:], stb[0:2, RN_K * TB:(RN_K + 1) * TB],
                     start=True, stop=True),
            t.matmul(p_b1[:, TB:2 * TB], cst["eb2_b"][:],
                     stb[0:2, (RN_K + 1) * TB:(RN_K + 2) * TB], start=True, stop=True))[-1],
            reads=("stb_k",), writes=("p_b1",), tag=k + ":kbc")

        def k_norm(v, sl=sl):
            for mc in range(2):
                v.tensor_tensor(xkf[:, mc, :], praw[:, mc, :],
                                p_b1[:, mc * TB:mc * TB + TB], ALU.mult)
            v.drain()
            v.tensor_copy(XKb[:, :, sl], xkf[:])
            v.drain()
            return v.nop()
        f.op("vector", k_norm, reads=("praw0", "praw1", "p_b1"),
             writes=("xkf", "XKb"), tag=k + ":knorm")

        for mc in range(2):
            f.op("tensor", lambda t, mc=mc, par=par: proj(t, wv_sb, mc, par),
                 reads=(f"hg{par}", "wv_sb", f"praw{mc}"), writes=("p_q",),
                 tag=k + f":vmm{mc}")
            f.op("scalar", lambda s, mc=mc: s.activation(
                praw[:, mc, :], p_q[:], AF.Identity, bias=cst["qkv_b"][:, mc, 2:3]),
                reads=("p_q",), writes=(f"praw{mc}",), tag=k + f":vraw{mc}")

        def dcalc(v):
            for mc in range(2):
                v.tensor_tensor(dvec[:, mc, :], praw[:, mc, :], xkf[:, mc, :], ALU.subtract)
            v.drain()
            v.tensor_copy(dvb[:], dvec[:])
            v.drain()
            return v.nop()
        f.op("vector", dcalc, reads=("praw0", "praw1", "xkf"),
             writes=("dvec", "dvb"), tag=k + ":dcalc")

        f.op("scalar", lambda s: s.activation(sqb[:], dvec[:], AF.Square),
             reads=("dvec",), writes=("sqb0", "sqb1"), tag=k + ":dsq")

        def dsums(t):
            last = None
            for mc in range(2):
                t.matmul(p_mix[0:2, mc * TB:mc * TB + TB], cst["e2_b"][:],
                         dvb[:, mc, :], start=True, stop=True)
            for mc in range(2):
                last = t.matmul(p_mix[0:2, (2 + mc) * TB:(2 + mc) * TB + TB],
                                cst["e2_b"][:], sqb[:, mc, :], start=True, stop=True)
            return last
        f.op("tensor", dsums, reads=("dvb", "sqb0", "sqb1"),
             writes=("p_mix0", "p_mix1", "p_mix2", "p_mix3"), tag=k + ":dsums")

        def dstat(v):
            v.tensor_scalar(out=st[0:2, MU_D * TB:(MU_D + 2) * TB],
                            in0=p_mix[0:2, 0:2 * TB], scalar1=1.0 / HD,
                            scalar2=None, op0=ALU.mult)
            v.drain()
            v.tensor_tensor(st[0:2, 0:2 * TB], p_mix[0:2, 0:2 * TB],
                            st[0:2, MU_D * TB:(MU_D + 2) * TB], ALU.mult)
            v.drain()
            v.tensor_tensor(st[0:2, 0:2 * TB], p_mix[0:2, 2 * TB:4 * TB],
                            st[0:2, 0:2 * TB], ALU.subtract)
            v.drain()
            v.tensor_scalar(out=st[0:2, 0:2 * TB], in0=st[0:2, 0:2 * TB],
                            scalar1=1.0 / (HD - 1), scalar2=None, op0=ALU.mult)
            v.drain()
            v.tensor_copy(stb[0:2, MU_D * TB:(MU_D + 2) * TB],
                          st[0:2, MU_D * TB:(MU_D + 2) * TB])
            v.drain()
            return v.nop()
        f.op("vector", dstat, reads=("p_mix0", "p_mix1", "p_mix2", "p_mix3", "st_k"),
             writes=("st_mu", "stb_mu", "st_sd0"), tag=k + ":dstat")

        def dsd(s):
            rng = slice(0, 2 * TB)
            s.activation(st[0:2, rng], st[0:2, rng], AF.Ln)
            s.drain()
            s.activation(st[0:2, rng], st[0:2, rng], AF.Exp, scale=0.5)
            s.drain()
            return s.nop()
        f.op("scalar", dsd, reads=("st_mu", "st_sd0"), writes=("st_sd",), tag=k + ":dsd")

        def dr2s(s):
            s.activation(st[0:2, 0:2 * TB], st[0:2, 0:2 * TB], AF.Ln,
                         bias=cst["epsv"][0:2, 0:1])
            s.drain()
            s.activation(stb[0:2, SD_D * TB:(SD_D + 2) * TB], st[0:2, 0:2 * TB],
                         AF.Exp, scale=-1.0)
            s.drain()
            return s.nop()
        f.op("scalar", dr2s, reads=("st_sd", "stb_r2"), writes=("st_sd", "stb_r2"),
             tag=k + ":dr2s")

        f.op("tensor", lambda t: (
            t.matmul(p_b1[:, 0:TB], cst["eb2_b"][:], stb[0:2, MU_D * TB:(MU_D + 1) * TB],
                     start=True, stop=True),
            t.matmul(p_b1[:, TB:2 * TB], cst["eb2_b"][:],
                     stb[0:2, (MU_D + 1) * TB:(MU_D + 2) * TB], start=True, stop=True))[-1],
            reads=("stb_mu", "p_b1"), writes=("p_b1",), tag=k + ":mubc")

        def dsub(v):
            for mc in range(2):
                v.tensor_tensor(dvec[:, mc, :], dvec[:, mc, :],
                                p_b1[:, mc * TB:mc * TB + TB], ALU.subtract)
            v.drain()
            return v.nop()
        f.op("vector", dsub, reads=("dvec", "p_b1"), writes=("dvec",), tag=k + ":dsub")

        f.op("tensor", lambda t: (
            t.matmul(p_b1[:, 0:TB], cst["eb2_b"][:], stb[0:2, SD_D * TB:(SD_D + 1) * TB],
                     start=True, stop=True),
            t.matmul(p_b1[:, TB:2 * TB], cst["eb2_b"][:],
                     stb[0:2, (SD_D + 1) * TB:(SD_D + 2) * TB], start=True, stop=True))[-1],
            reads=("stb_r2", "dvec", "p_b1"), writes=("p_b1",), tag=k + ":r2bc")

        def dmul(v, sl=sl):
            for mc in range(2):
                v.tensor_tensor(tgt_fm[:, mc, sl], dvec[:, mc, :],
                                p_b1[:, mc * TB:mc * TB + TB], ALU.mult)
            v.drain()
            return v.nop()
        f.op("vector", dmul, reads=("dvec", "p_b1"), writes=("tgt_fm", "dvec"),
             tag=k + ":dmul")

        for mc in range(2):
            f.op("tensor", lambda t, mc=mc, par=par: proj(t, wq_sb, mc, par),
                 reads=(f"hg{par}", "wq_sb", f"praw{mc}", "dvec"), writes=("p_q",),
                 tag=k + f":qmm{mc}")

            f.op("scalar", lambda s, mc=mc: s.activation(
                praw[:, mc, :], p_q[:], AF.Identity, bias=cst["qkv_b"][:, mc, 0:1]),
                reads=("p_q",), writes=(f"praw{mc}",), tag=k + f":qraw{mc}")
            f.op("vector", lambda v, mc=mc: v.tensor_tensor(
                sqb[:, mc, :], praw[:, mc, :], praw[:, mc, :], ALU.mult),
                reads=(f"praw{mc}", f"sqb{mc}"), writes=(f"sqb{mc}",),
                tag=k + f":qrsq{mc}")
            f.op("tensor", lambda t, mc=mc: t.matmul(
                p_mix[0:2, mc * TB:mc * TB + TB], cst["e2_b"][:], sqb[:, mc, :],
                start=True, stop=True),
                reads=(f"sqb{mc}", "st_mu"), writes=(f"p_mix{mc}",), tag=k + f":qss{mc}")

        def q_lnexp(s):
            s.activation(st[0:2, RN_Q * TB:(RN_Q + 2) * TB], p_mix[0:2, 0:2 * TB],
                         AF.Ln, bias=cst["epsv"][0:2, 2:3])
            s.drain()
            s.activation(stb[0:2, RN_Q * TB:(RN_Q + 2) * TB],
                         st[0:2, RN_Q * TB:(RN_Q + 2) * TB], AF.Exp, scale=-0.5)
            s.drain()
            return s.nop()
        f.op("scalar", q_lnexp, reads=("p_mix0", "p_mix1", "stb_q", "st_sd"),
             writes=("st_q", "stb_q"), tag=k + ":qlnexp")

        f.op("tensor", lambda t: (
            t.matmul(p_b1[:, 0:TB], cst["eb2_b"][:], stb[0:2, RN_Q * TB:(RN_Q + 1) * TB],
                     start=True, stop=True),
            t.matmul(p_b1[:, TB:2 * TB], cst["eb2_b"][:],
                     stb[0:2, (RN_Q + 1) * TB:(RN_Q + 2) * TB], start=True, stop=True))[-1],
            reads=("stb_q", "dvec", "p_b1"), writes=("p_b1",), tag=k + ":qbc")

        def q_norm(v, sl=sl):
            for mc in range(2):
                v.tensor_tensor(XQb[:, mc, sl], praw[:, mc, :],
                                p_b1[:, mc * TB:mc * TB + TB], ALU.mult)
            v.drain()
            return v.nop()
        f.op("vector", q_norm, reads=("praw0", "praw1", "p_b1"),
             writes=("XQb",), tag=k + ":qnorm")

        def etamm(t, par=par):
            last = None
            for kc in range(8):
                last = t.matmul(p_eta[:], lrw_sb[:, kc, :], hg[par][:, kc, :],
                                start=(kc == 0), stop=(kc == 7))
            return last
        f.op("tensor", etamm, reads=(f"hg{par}", "lrw_sb"), writes=("p_eta",),
             tag=k + ":etamm")
        f.op("scalar", lambda s: s.activation(etrow[:], p_eta[:], AF.Tanh, scale=0.5,
                                              bias=cst["lrbh"][:]),
             reads=("p_eta", "c_lrbh"), writes=("etrow",), tag=k + ":etatanh")
        f.op("vector", lambda v: v.tensor_scalar(
            out=etrow[:], in0=etrow[:], scalar1=-1.0 / 8192.0, scalar2=-1.0 / 8192.0,
            op0=ALU.mult, op1=ALU.add),
            reads=("etrow",), writes=("etrow",), tag=k + ":etasc")

        def etat(t):
            last = None
            for ch in range(8):
                last = t.transpose(p_mix[0:64, ch * 4:ch * 4 + 4],
                                   etrow[:, ch * 64:ch * 64 + 64], cst["ident_f"][0:4, 0:4])
            return last
        f.op("tensor", etat, reads=("etrow", "p_mix0"), writes=("p_mix0",),
             tag=k + ":etat")
        f.op("vector", lambda v, gtb=gtb: v.tensor_copy(
            etaT[:, gtb * 32:gtb * 32 + 32], p_mix[0:64, 0:32]),
            reads=("p_mix0",), writes=("etaT", f"hg{par}"), tag=k + ":etaev")

    d = T["dout"]("xqb", [128, 2, seq], BF16)
    if d is not None:
        f.dma("sync", d, XQb[:], reads=("XQb",), tag="dbg:xqb")
    d = T["dout"]("xkb", [128, 2, seq], BF16)
    if d is not None:
        f.dma("sync", d, XKb[:], reads=("XKb",), tag="dbg:xkb")
    d = T["dout"]("tgtfm", [128, 2, seq])
    if d is not None:
        f.dma("sync", d, tgt_fm[:], reads=("tgt_fm",), tag="dbg:tgt")
    d = T["dout"]("etat", [64, T["nch"] * 4])
    if d is not None:
        f.dma("sync", d, etaT[:], reads=("etaT",), tag="dbg:eta")


def emit_p1b(f, nc, T, ctx):
    cst = T["cst"]; seq = T["seq"]; nch = T["nch"]
    if T.get("scan_steps"):
        nch = min(nch, T["scan_steps"])
    sb = lambda n, s, dt=F32: ctx.enter_context(nc.sbuf_tensor(n, s, dt))
    ps = lambda n, s, dt=F32: ctx.enter_context(nc.psum_tensor(n, s, dt))
    XQb, XKb, tgt_fm, etaT = T["XQb"], T["XKb"], T["tgt_fm"], T["etaT"]

    W2 = sb("W2", [128, 256]); W2b = sb("W2b", [128, 256], BF16)
    b2 = sb("b2", [64, 256]); b2b = sb("b2b", [128, 256], BF16)
    gneg = sb("gneg", [128, 256], BF16)
    at_m = sb("at_m", [128, 256], BF16)
    xk_tm = sb("xk_tm", [64, 256], BF16)
    sq_tm = sb("sq_tm", [64, 256])
    xh = sb("xh", [64, 256])
    t1 = sb("t1s", [64, 256])
    t2 = sb("t2s", [64, 256])
    o_ln = sb("o_ln", [64, 256])
    wsc = sb("wsc", [128, 256])
    stt = sb("stt", [64, 32])
    XO16 = sb("XO16", [128, 2, 1024], BF16)
    z1f = sb("z1f", [64, 256])
    zbf = sb("zbf", [64, 256])
    p_tm = ps("p_tm", [64, 256], BF16)
    psA = ps("psA", [64, 384])
    psB = ps("psB", [64, 384])
    p_atA = ps("p_atA", [64, 128])
    p_atB = ps("p_atB", [64, 128])
    p_wd = ps("p_wd", [128, 256])
    p_ot = ps("p_ot", [128, 384])
    p_z = psA
    p_tg = psB

    f.dma("sync", W2[:, 0:128], T["W2_0"], writes=("W2:0",), tag="sc:ldW0")
    f.dma("sync", W2[:, 128:256], T["W2_1"], writes=("W2:1",), tag="sc:ldW1")
    f.dma("sync", b2[:], T["b2_bc"], writes=("b2",), tag="sc:ldb")

    g2_tm = sb("g2_tm", [64, 256])

    def initcast(v):
        v.tensor_tensor(g2_tm[:], cst["ttt_g_tm"][:], cst["ttt_g_tm"][:], ALU.mult)
        v.drain()
        v.tensor_copy(W2b[:], W2[:])
        v.memset(b2b[64:128, :], 0.0)
        v.memset(gneg[64:128, :], 0.0)
        v.memset(at_m[64:128, :], 0.0)
        v.drain()
        v.tensor_copy(b2b[0:64, :], b2[:])
        v.drain()
        return v.nop()
    f.op("vector", initcast, reads=("W2:0", "W2:1", "b2"),
         writes=("W2b", "b2b", "gneg", "at_m"), tag="sc:initcast")

    z3 = lambda ap: ap.rearrange("p (h e) -> p h e", e=64)
    bc3 = lambda col: stt[:, col:col + 4].unsqueeze(2).broadcast_to([64, 4, 64])
    S_SUM, S_SS, S_MU, S_RSTD, S_S1, S_S2, S_SCL, S_TMP = 0, 4, 8, 12, 16, 20, 24, 28

    GG = T.get("scan_groups")
    realop = f.op
    realdma = f.dma
    import re as _re
    def gate(tag):
        m = _re.search(r":([a-z0-9]+)$", tag)
        grp = m.group(1) if m else tag
        return GG is None or grp in GG
    def gop(eng, emit, reads=(), writes=(), kind="compute", tag=""):
        if gate(tag):
            return realop(eng, emit, reads, writes, kind=kind, tag=tag)
    fop_save = f.op

    class _G:
        def op(self, *a, **k):
            if gate(k.get("tag", a[-1] if a else "")):
                return realop(*a, **k)
            return None
        def dma(self, *a, **k):
            if gate(k.get("tag", "")):
                return realdma(*a, **k)
            return None
        def __getattr__(self, n):
            return getattr(f, n)
    fg = _G()

    stt2 = sb("stt2", [64, 32])
    sqo = sb("sqo", [64, 256])
    zbf2 = [zbf, sb("zbfB", [64, 256])]
    bc32 = lambda col: stt2[:, col:col + 4].unsqueeze(2).broadcast_to([64, 4, 64])

    def zrstd2(s):
        s.activation(stt2[:, S_RSTD:S_RSTD + 4], stt2[:, S_TMP:S_TMP + 4], AF.Ln,
                     bias=cst["epsv"][:, 1:2])
        s.drain()
        s.activation(stt2[:, S_RSTD:S_RSTD + 4], stt2[:, S_RSTD:S_RSTD + 4],
                     AF.Exp, scale=-0.5)
        s.drain()
        return s.nop()

    def mk_tail(n):
        k = f"sc{n}"
        sl = slice(n * C, (n + 1) * C)
        blkidx = n % 16
        zp = n % 2
        zbb = zbf2[zp]
        ZR = f"zbf{zp}"
        th = {}

        th["osq"] = lambda: fg.op("scalar", lambda s: s.activation(
            sqo[:], zbb[:], AF.Square), reads=(ZR, "sqo"), writes=("sqo",),
            tag=k + ":osq")
        th["os1"] = lambda: fg.op("vector", lambda v: v.tensor_reduce(
            stt2[:, S_SUM:S_SUM + 4], z3(zbb[:]), axis=AX.X, op=ALU.add),
            reads=(ZR, "stt2"), writes=("stt2",), tag=k + ":os1")
        th["os2"] = lambda: fg.op("vector", lambda v: v.tensor_reduce(
            stt2[:, S_SS:S_SS + 4], z3(sqo[:]), axis=AX.X, op=ALU.add),
            reads=("sqo", "stt2"), writes=("stt2",), tag=k + ":os2")
        th["os3"] = lambda: fg.op("vector", lambda v: v.tensor_scalar(
            out=stt2[:, S_MU:S_MU + 8], in0=stt2[:, S_SUM:S_SUM + 8],
            scalar1=1.0 / HD, scalar2=None, op0=ALU.mult),
            reads=("stt2",), writes=("stt2",), tag=k + ":os3")
        th["os4"] = lambda: fg.op("vector", lambda v: v.tensor_tensor(
            stt2[:, S_S1:S_S1 + 4], stt2[:, S_MU:S_MU + 4],
            stt2[:, S_MU:S_MU + 4], ALU.mult),
            reads=("stt2",), writes=("stt2",), tag=k + ":os4")
        th["os5"] = lambda: fg.op("vector", lambda v: v.tensor_tensor(
            stt2[:, S_TMP:S_TMP + 4], stt2[:, S_MU + 4:S_MU + 8],
            stt2[:, S_S1:S_S1 + 4], ALU.subtract),
            reads=("stt2",), writes=("stt2",), tag=k + ":os5")
        th["orstd"] = lambda: fg.op("scalar", zrstd2, reads=("stt2",),
                                    writes=("stt2",), tag=k + ":orstd")
        th["on1"] = lambda: fg.op("vector", lambda v: v.tensor_tensor(
            z3(o_ln[:]), z3(zbb[:]), bc32(S_MU), ALU.subtract),
            reads=(ZR, "stt2", "o_ln"), writes=("o_ln",), tag=k + ":on1")
        th["on2"] = lambda: fg.op("vector", lambda v: v.tensor_tensor(
            z3(o_ln[:]), z3(o_ln[:]), bc32(S_RSTD), ALU.mult),
            reads=("stt2", "o_ln"), writes=("o_ln",), tag=k + ":on2")
        th["on3"] = lambda: fg.op("vector", lambda v: v.tensor_tensor(
            o_ln[:], o_ln[:], cst["ttt_g_tm"][:], ALU.mult),
            reads=("o_ln",), writes=("o_ln",), tag=k + ":on3")
        th["on4"] = lambda: fg.op("vector", lambda v: v.tensor_tensor(
            o_ln[:], o_ln[:], cst["ttt_b_tm"][:], ALU.add),
            reads=("o_ln",), writes=("o_ln",), tag=k + ":on4")

        def _tout():
            def t_out(t):
                t.transpose(p_ot[:, 0:64], o_ln[0:64, 0:128],
                            cst["ident_f"][0:64, 0:64])
                return t.transpose(p_ot[:, 64:128], o_ln[0:64, 128:256],
                                   cst["ident_f"][0:64, 0:64])
            fg.op("tensor", t_out, reads=("o_ln", "p_ot"), writes=("p_ot",),
                 tag=k + ":tout")
        th["tout"] = _tout
        th["oadd"] = lambda: fg.op("vector", lambda v, sl=sl, blkidx=blkidx:
            v.tensor_tensor(
                XO16[:, :, 64 * blkidx:64 * blkidx + 64],
                p_ot[:, 0:128].rearrange("p (a b) -> p a b", b=64),
                XQb[:, :, sl], ALU.add),
            reads=("p_ot", "XQb"), writes=(f"XO16:{blkidx}",), tag=k + ":oadd")

        def _pub():
            if blkidx == 15 or n == nch - 1:
                nblk = n // 16
                lo = 1024 * nblk
                w = 64 * (blkidx + 1)
                CW = T["CW"]
                cc = lo // CW
                off = lo - cc * CW
                fg.dma("sync", T[f"ymix_pub{cc}"].rearrange("(mc p) t -> p mc t", p=128)
                      [:, :, off:off + w], XO16[:, :, 0:w],
                      reads=tuple(f"XO16:{i}" for i in range(blkidx + 1)),
                      writes=(f"ymix_pub{cc}",), tag=k + ":pub")
        th["pub"] = _pub
        return th

    TAIL_ORDER = ("osq", "os1", "os2", "os3", "os4", "os5", "orstd",
                  "on1", "on2", "on3", "on4", "tout", "oadd", "pub")

    prev = None
    P = (lambda nm: prev[nm]() if prev is not None else None)
    for n in range(nch):
        k = f"sc{n}"
        sl = slice(n * C, (n + 1) * C)
        zp = n % 2
        zbb = zbf2[zp]
        ZW = f"zbf{zp}"

        def t_xk(t, sl=sl):
            t.transpose(p_tm[0:64, 0:128], XKb[:, 0, sl], cst["ident_b"][:])
            return t.transpose(p_tm[0:64, 128:256], XKb[:, 1, sl], cst["ident_b"][:])
        fg.op("tensor", t_xk, reads=("XKb",), writes=("p_tm",), tag=k + ":txk")

        def t_tg(t, sl=sl):
            t.transpose(p_tg[0:64, 0:128], tgt_fm[:, 0, sl], cst["ident_f"][:])
            return t.transpose(p_tg[0:64, 128:256], tgt_fm[:, 1, sl], cst["ident_f"][:])
        fg.op("tensor", t_tg, reads=("tgt_fm", "psB"), writes=("psB",), tag=k + ":ttg")

        def atmm(t, sl=sl):
            last = None
            for h in range(4):
                p, h2 = h // 2, h % 2
                dst = p_atA if h2 == 0 else p_atB
                last = t.matmul(dst[0:64, 64 * p:64 * p + 64],
                                XKb[64 * h2:64 * h2 + 64, p, sl],
                                XQb[64 * h2:64 * h2 + 64, p, sl], start=True, stop=True)
            return last
        fg.op("tensor", atmm, reads=("XKb", "XQb"), writes=("p_at",), tag=k + ":at")

        fg.op("vector", lambda v: v.tensor_copy(xk_tm[:], p_tm[0:64, :]),
             reads=("p_tm",), writes=("xk_tm",), tag=k + ":evxk")
        fg.op("vector", lambda v: v.tensor_tensor(at_m[0:64, 0:128], p_atA[0:64, :],
                                                  cst["trimask4"][:, 0:128], ALU.mult),
             reads=("p_at", "at_m"), writes=("at_m",), tag=k + ":evat")
        fg.op("vector", lambda v: v.tensor_tensor(at_m[0:64, 128:256], p_atB[0:64, :],
                                                  cst["trimask4"][:, 0:128], ALU.mult),
             reads=("p_at", "at_m"), writes=("at_m",), tag=k + ":evat2")

        P("osq")
        P("os1")
        P("os2")

        def z1mm(t, sl=sl):
            last = None
            for p in range(2):
                last = t.matmul(p_z[0:64, 128 * p:128 * p + 128], XKb[:, p, sl],
                                W2b[:, 128 * p:128 * p + 128], start=True, stop=True)
            return last
        fg.op("tensor", z1mm, reads=("XKb", "W2b", "psA"), writes=("psA",), tag=k + ":z1")
        fg.op("vector", lambda v: v.tensor_tensor(z1f[:], p_z[0:64, 0:256], b2[:], ALU.add),
             reads=("psA", "b2"), writes=("z1f",), tag=k + ":z1f")
        P("os3")
        fg.op("vector", lambda v: v.tensor_tensor(sq_tm[:], z1f[:], z1f[:], ALU.mult),
             reads=("z1f", "sq_tm"), writes=("sq_tm",), tag=k + ":zsq")

        fg.op("vector", lambda v: v.tensor_reduce(
            stt[:, S_SUM:S_SUM + 4], z3(z1f[:]), axis=AX.X, op=ALU.add),
            reads=("z1f", "stt"), writes=("stt",), tag=k + ":zst1")
        P("os4")
        fg.op("vector", lambda v: v.tensor_reduce(
            stt[:, S_SS:S_SS + 4], z3(sq_tm[:]), axis=AX.X, op=ALU.add),
            reads=("sq_tm", "stt"), writes=("stt",), tag=k + ":zst2")
        P("os5")
        P("orstd")
        fg.op("vector", lambda v: v.tensor_scalar(
            out=stt[:, S_MU:S_MU + 8], in0=stt[:, S_SUM:S_SUM + 8],
            scalar1=1.0 / HD, scalar2=None, op0=ALU.mult),
            reads=("stt",), writes=("stt",), tag=k + ":zst3")
        P("on1")
        fg.op("vector", lambda v: v.tensor_tensor(
            stt[:, S_S1:S_S1 + 4], stt[:, S_MU:S_MU + 4], stt[:, S_MU:S_MU + 4],
            ALU.mult),
            reads=("stt",), writes=("stt",), tag=k + ":zst4")
        P("on2")
        fg.op("vector", lambda v: v.tensor_tensor(
            stt[:, S_TMP:S_TMP + 4], stt[:, S_MU + 4:S_MU + 8],
            stt[:, S_S1:S_S1 + 4], ALU.subtract),
            reads=("stt",), writes=("stt",), tag=k + ":zst5")
        P("on3")

        def zrstd(s):
            s.activation(stt[:, S_RSTD:S_RSTD + 4], stt[:, S_TMP:S_TMP + 4], AF.Ln,
                         bias=cst["epsv"][:, 1:2])
            s.drain()
            s.activation(stt[:, S_RSTD:S_RSTD + 4], stt[:, S_RSTD:S_RSTD + 4],
                         AF.Exp, scale=-0.5)
            s.drain()
            return s.nop()
        fg.op("scalar", zrstd, reads=("stt",), writes=("stt",), tag=k + ":zrstd")

        fg.op("vector", lambda v: v.tensor_tensor(
            z3(xh[:]), z3(z1f[:]), bc3(S_MU), ALU.subtract),
            reads=("z1f", "stt", "xh"), writes=("xh",), tag=k + ":xh1")
        P("on4")
        P("tout")
        fg.op("vector", lambda v: v.tensor_tensor(
            z3(xh[:]), z3(xh[:]), bc3(S_RSTD), ALU.mult),
            reads=("stt", "xh"), writes=("xh",), tag=k + ":xh2")
        fg.op("vector", lambda v, n=n: v.scalar_tensor_tensor(
            out=stt[:, S_SCL:S_SCL + 4], in0=stt[:, S_RSTD:S_RSTD + 4],
            scalar=1.0 / HD, in1=etaT[:, 4 * n:4 * n + 4],
            op0=ALU.mult, op1=ALU.mult),
            reads=("stt", "etaT"), writes=("stt",), tag=k + ":scl1")
        fg.op("vector", lambda v: v.tensor_tensor(
            t1[:], xh[:], p_tg[0:64, 0:256], ALU.subtract),
            reads=("xh", "psB", "t1"), writes=("t1",), tag=k + ":go1")
        P("oadd")
        fg.op("vector", lambda v: v.tensor_tensor(t1[:], t1[:], g2_tm[:], ALU.mult),
             reads=("t1",), writes=("t1",), tag=k + ":go2")
        fg.op("vector", lambda v: v.tensor_tensor(t2[:], t1[:], xh[:], ALU.mult),
             reads=("t1", "xh", "t2"), writes=("t2",), tag=k + ":gs2")
        fg.op("vector", lambda v: v.tensor_reduce(
            stt[:, S_S1:S_S1 + 4], z3(t1[:]), axis=AX.X, op=ALU.add),
            reads=("t1", "stt"), writes=("stt",), tag=k + ":gs1")
        fg.op("vector", lambda v: v.tensor_reduce(
            stt[:, S_S2:S_S2 + 4], z3(t2[:]), axis=AX.X, op=ALU.add),
            reads=("t2", "stt"), writes=("stt",), tag=k + ":gs3")
        fg.op("vector", lambda v: v.scalar_tensor_tensor(
            out=z3(t2[:]), in0=z3(t1[:]), scalar=float(HD), in1=bc3(S_S1),
            op0=ALU.mult, op1=ALU.subtract),
            reads=("t1", "stt", "t2"), writes=("t2",), tag=k + ":gn1")
        fg.op("vector", lambda v: v.tensor_tensor(
            z3(xh[:]), z3(xh[:]), bc3(S_S2), ALU.mult),
            reads=("stt", "xh"), writes=("xh",), tag=k + ":gn2")
        fg.op("vector", lambda v: v.tensor_tensor(t2[:], t2[:], xh[:], ALU.subtract),
             reads=("t2", "xh"), writes=("t2",), tag=k + ":gn3")
        fg.op("vector", lambda v: v.tensor_tensor(
            z3(gneg[0:64, :]), z3(t2[:]), bc3(S_SCL), ALU.mult),
            reads=("t2", "stt", "gneg"), writes=("gneg",), tag=k + ":gn4")

        def zbmm(t, sl=sl):
            last = None
            for p in range(2):
                dst = psB if p == 0 else psA
                t.matmul(dst[0:64, 256:384], cst["triuTz"][:],
                         gneg[:, 128 * p:128 * p + 128], start=True, stop=False)
                t.matmul(dst[0:64, 256:384], XQb[:, p, sl],
                         W2b[:, 128 * p:128 * p + 128], start=False, stop=True)
                for h2 in range(2):
                    hh = 2 * p + h2
                    atoff = {0: 0, 2: 64, 1: 128, 3: 192}[hh]
                    last = t.matmul(dst[0:64, 256 + 64 * h2:256 + 64 * h2 + 64],
                                    at_m[:, atoff:atoff + 64],
                                    gneg[:, 64 * hh:64 * hh + 64],
                                    start=False, stop=True, skip_group_check=True)
            return last
        fg.op("tensor", zbmm, reads=("gneg", "XQb", "W2b", "at_m", "psA", "psB", "z1f"),
             writes=("psA", "psB"), tag=k + ":zb")

        fg.op("vector", lambda v, zbb=zbb: v.tensor_tensor(
            zbb[:, 0:128], psB[0:64, 256:384], b2[:, 0:128], ALU.add),
            reads=("psA", "psB", "b2", ZW), writes=(ZW,), tag=k + ":zbf")
        fg.op("vector", lambda v, zbb=zbb: v.tensor_tensor(
            zbb[:, 128:256], psA[0:64, 256:384], b2[:, 128:256], ALU.add),
            reads=("psA", "psB", "b2", ZW), writes=(ZW,), tag=k + ":zbf2")
        P("pub")

        def bnew(t):
            t.matmul(p_ot[0:64, 128:256], cst["ones64"][:], gneg[0:64, 0:128],
                     start=True, stop=True)
            return t.matmul(p_ot[0:64, 256:384], cst["ones64"][:],
                            gneg[0:64, 128:256], start=True, stop=True)
        fg.op("tensor", bnew, reads=("gneg", "p_ot"), writes=("p_bn",), tag=k + ":bn")

        def wdmm(t):
            t.matmul(p_wd[:, 0:128], xk_tm[0:64, 0:128], gneg[0:64, 0:128],
                     start=True, stop=True)
            return t.matmul(p_wd[:, 128:256], xk_tm[0:64, 128:256],
                            gneg[0:64, 128:256], start=True, stop=True)
        fg.op("tensor", wdmm, reads=("xk_tm", "gneg"), writes=("p_wd",), tag=k + ":wd")

        fg.op("vector", lambda v: v.tensor_tensor(
            wsc[:, 0:128], p_wd[:, 0:128], cst["wmask"][:], ALU.mult),
            reads=("p_wd", "wsc"), writes=("wsc",), tag=k + ":wu1")
        fg.op("vector", lambda v: v.tensor_tensor(
            wsc[:, 128:256], p_wd[:, 128:256], cst["wmask"][:], ALU.mult),
            reads=("p_wd", "wsc"), writes=("wsc",), tag=k + ":wu2")
        fg.op("vector", lambda v: v.tensor_tensor(
            b2[:], b2[:], p_ot[0:64, 128:384], ALU.add),
            reads=("p_bn", "b2"), writes=("b2",), tag=k + ":wu5")
        fg.op("vector", lambda v: v.tensor_tensor(W2[:], W2[:], wsc[:], ALU.add),
             reads=("wsc", "W2:0", "W2:1"), writes=("W2:0", "W2:1"), tag=k + ":wu3")
        fg.op("vector", lambda v: v.tensor_copy(b2b[0:64, :], b2[:]),
             reads=("b2", "b2b"), writes=("b2b",), tag=k + ":wu6")
        fg.op("vector", lambda v: v.tensor_copy(W2b[:], W2[:]),
             reads=("W2:0", "W2:1", "W2b"), writes=("W2b",), tag=k + ":wu4")

        prev = mk_tail(n)

    if prev is not None:
        for nm in TAIL_ORDER:
            prev[nm]()

    for cc in range(T["NCB_T"]):
        f.coll("AllGather", ALU.bypass, [T[f"ymix_pub{cc}"].opt()],
               [T[f"ymix_gat{cc}"].opt()], reads=(f"ymix_pub{cc}",),
               writes=(f"ymix_gat{cc}",), tag=f"sc:ag{cc}")
    d = T["dout"]("ymixg", [1024, seq], BF16)
    if d is not None:
        for cc in range(T["NCB_T"]):
            f.dma("sync", d[:, cc * T["CW"]:(cc + 1) * T["CW"]], T[f"ymix_gat{cc}"],
                  reads=(f"ymix_gat{cc}",), tag=f"dbg:ymix{cc}")


class LNH:

    def __init__(self, f, nc, cst, ctx, pfx):
        self.f, self.nc, self.cst, self.pfx = f, nc, cst, pfx
        sb = lambda n, s, dt=F32: ctx.enter_context(nc.sbuf_tensor(pfx + n, s, dt))
        ps = lambda n, s, dt=F32: ctx.enter_context(nc.psum_tensor(pfx + n, s, dt))
        self.xb = sb("xb", [128, 8, TB], BF16)
        self.sqb = sb("sqb", [128, 8, TB], BF16)
        self.srow = sb("srow", [1, 4 * TB])
        self.srow_b = sb("srowb", [1, 2 * TB], BF16)
        self.p_sum = ps("p_sum", [1, TB])
        self.p_ss = ps("p_ss", [1, TB])
        self.p_bc = ps("p_bc", [128, 2 * TB])

    def stats(self, tag, src_ap, src_res, src_bf16, eps_col, extra_reads=()):
        f, cst = self.f, self.cst
        xb, sqb, srow, srow_b = self.xb, self.sqb, self.srow, self.srow_b
        p_sum, p_ss, p_bc = self.p_sum, self.p_ss, self.p_bc
        P = self.pfx
        if not src_bf16:
            f.op("vector", lambda v: v.tensor_copy(xb[:], src_ap),
                 reads=(src_res,) + tuple(extra_reads), writes=(P + "xb",), tag=tag + ":cast")
            sums_src, sums_res = xb, P + "xb"
        else:
            sums_src, sums_res = src_ap, src_res
        f.op("scalar", lambda s: s.activation(sqb[:], (xb[:] if not src_bf16 else src_ap), AF.Square),
             reads=(sums_res if not src_bf16 else src_res,), writes=(P + "sqb",), tag=tag + ":sq")

        def mm_sum(t):
            last = None
            for kc in range(8):
                aa = sums_src[:, kc, :] if not src_bf16 else src_ap[:, kc, :]
                last = t.matmul(p_sum[0:1, :], cst["onescol_b"][:], aa,
                                start=(kc == 0), stop=(kc == 7))
            return last
        f.op("tensor", mm_sum, reads=(sums_res if not src_bf16 else src_res,),
             writes=(P + "p_sum",), tag=tag + ":mms")

        def mm_ss(t):
            last = None
            for kc in range(8):
                last = t.matmul(p_ss[0:1, :], cst["onescol_b"][:], sqb[:, kc, :],
                                start=(kc == 0), stop=(kc == 7))
            return last
        f.op("tensor", mm_ss, reads=(P + "sqb",), writes=(P + "p_ss",), tag=tag + ":mmss")

        def stats_v(v):
            v.tensor_scalar(out=srow[0:1, 0:TB], in0=p_sum[0:1, :], scalar1=1.0 / D,
                            scalar2=None, op0=ALU.mult)
            v.tensor_scalar(out=srow[0:1, 3 * TB:4 * TB], in0=p_ss[0:1, :],
                            scalar1=1.0 / D, scalar2=None, op0=ALU.mult)
            v.drain()
            v.tensor_tensor(srow[0:1, TB:2 * TB], srow[0:1, 0:TB], srow[0:1, 0:TB], ALU.mult)
            v.drain()
            v.tensor_tensor(srow[0:1, TB:2 * TB], srow[0:1, 3 * TB:4 * TB],
                            srow[0:1, TB:2 * TB], ALU.subtract)
            v.drain()
            return v.nop()
        f.op("vector", stats_v, reads=(P + "p_sum", P + "p_ss"), writes=(P + "srow",),
             tag=tag + ":st")

        def rstd_s(s):
            s.activation(srow[0:1, 2 * TB:3 * TB], srow[0:1, TB:2 * TB], AF.Ln,
                         bias=cst["epsv"][0:1, eps_col:eps_col + 1])
            s.drain()
            s.activation(srow[0:1, 2 * TB:3 * TB], srow[0:1, 2 * TB:3 * TB],
                         AF.Exp, scale=-0.5)
            s.drain()
            return s.nop()
        f.op("scalar", rstd_s, reads=(P + "srow",), writes=(P + "srow",), tag=tag + ":rstd")

        def tobf(v):
            v.tensor_copy(srow_b[0:1, 0:TB], srow[0:1, 0:TB])
            v.tensor_copy(srow_b[0:1, TB:2 * TB], srow[0:1, 2 * TB:3 * TB])
            v.drain()
            return v.nop()
        f.op("vector", tobf, reads=(P + "srow",), writes=(P + "srowb",), tag=tag + ":bf")

        def bcast(t):
            t.matmul(p_bc[:, 0:TB], cst["onesr_b"][:], srow_b[0:1, 0:TB],
                     start=True, stop=True)
            return t.matmul(p_bc[:, TB:2 * TB], cst["onesr_b"][:],
                            srow_b[0:1, TB:2 * TB], start=True, stop=True)
        f.op("tensor", bcast, reads=(P + "srowb", P + "p_bc"), writes=(P + "p_bc",),
             tag=tag + ":bc")
        return p_bc


def emit_p2(f, nc, T, ctx):
    cst = T["cst"]; nto = T["nto"]; ntb = T["ntb_o"]; seq = T["seq"]
    sb = lambda n, s, dt=F32: ctx.enter_context(nc.sbuf_tensor(n, s, dt))
    ps = lambda n, s, dt=F32: ctx.enter_context(nc.psum_tensor(n, s, dt))

    ym = sb("ym", [128, 8, nto], BF16)
    yn = sb("yn", [128, 8, TB], BF16)
    ao = sb("ao", [128, 8, TB])
    xf = sb("xf2", [128, 8, TB])
    x2 = sb("x2t", [128, 8, TB])
    ab = sb("abuf", [128, 8, TB], BF16)
    wo_sb = sb("wo_sb", [128, 8, 1024], BF16)
    scf = sb("scf2", [128, 8, TB])
    p_w = [ps("p_wA", [128, TB]), ps("p_wB", [128, TB])]
    ln = LNH(f, nc, cst, ctx, "l2_")

    f.dma("gpsimd", wo_sb[:], T["woT"].rearrange("(kc p) m -> p kc m", p=128),
          writes=("wo_sb",), tag="p2:ldwo")

    def gat(e, core):
        q = core % 4
        cc = (nto * q) // T["CW"]
        off = nto * q - cc * T["CW"]
        return e.dma_start(out=ym[:], in_=T[f"ymix_gat{cc}"]
                           .rearrange("(kc p) t -> p kc t", p=128)
                           [:, :, off:off + nto])
    f.dma_percore("sync", gat,
                  reads=tuple(f"ymix_gat{cc}" for cc in range(T["NCB_T"])),
                  writes=("ym",), tag="p2:gat")

    for tb in range(ntb):
        sl = slice(tb * TB, (tb + 1) * TB)
        k = f"p2.{tb}"
        p_bc = ln.stats(k + ":pn", ym[:, :, sl], "ym", True, 1)

        def pnorm(v, sl=sl):
            v.tensor_tensor(scf[:], ym[:, :, sl],
                            p_bc[:, 0:TB].unsqueeze(1).broadcast_to([128, 8, TB]), ALU.subtract)
            v.drain()
            v.tensor_tensor(scf[:], scf[:],
                            p_bc[:, TB:2 * TB].unsqueeze(1).broadcast_to([128, 8, TB]), ALU.mult)
            v.drain()
            v.tensor_tensor(scf[:], scf[:],
                            cst["post_nw"][:].unsqueeze(2).broadcast_to([128, 8, TB]), ALU.mult)
            v.drain()
            v.tensor_tensor(yn[:], scf[:],
                            cst["post_nb"][:].unsqueeze(2).broadcast_to([128, 8, TB]), ALU.add)
            v.drain()
            return v.nop()
        f.op("vector", pnorm, reads=("ym", "l2_p_bc"), writes=("yn", "scf2"),
             tag=k + ":pn")

        for mc in range(8):
            wp = mc % 2
            def womm(t, mc=mc, wp=wp):
                last = None
                for kc in range(8):
                    last = t.matmul(p_w[wp][:], wo_sb[:, kc, 128 * mc:128 * mc + 128],
                                    yn[:, kc, :], start=(kc == 0), stop=(kc == 7))
                return last
            f.op("tensor", womm, reads=("yn", "wo_sb"), writes=(f"p_w{wp}",), tag=k + f":wo{mc}")
            f.op("scalar", lambda s, mc=mc, wp=wp: s.activation(
                ao[:, mc, :], p_w[wp][:], AF.Identity, bias=cst["wo_b"][:, mc:mc + 1]),
                reads=(f"p_w{wp}",), writes=("ao",), tag=k + f":woev{mc}")

        f.dma("sync", xf[:], fm3(T["x_own"])[:, :, sl], writes=("xf2",), tag=k + ":ldx")
        p_bc = ln.stats(k + ":ol", ao[:], "ao", False, 0)

        def onorm(v):
            v.tensor_tensor(scf[:], ao[:],
                            p_bc[:, 0:TB].unsqueeze(1).broadcast_to([128, 8, TB]), ALU.subtract)
            v.drain()
            v.tensor_tensor(scf[:], scf[:],
                            p_bc[:, TB:2 * TB].unsqueeze(1).broadcast_to([128, 8, TB]), ALU.mult)
            v.drain()
            v.tensor_tensor(x2[:], scf[:], xf[:], ALU.add)
            v.drain()
            return v.nop()
        f.op("vector", onorm, reads=("ao", "l2_p_bc", "xf2"), writes=("x2t", "scf2"),
             tag=k + ":on")
        f.dma("sync", fm3(T["x2_spill"])[:, :, sl], x2[:], reads=("x2t",),
              writes=("x2_spill",), tag=k + ":spx2")

        p_bc2 = ln.stats(k + ":y0", x2[:], "x2t", False, 0)

        def y0norm(v):
            v.tensor_tensor(scf[:], x2[:],
                            p_bc2[:, 0:TB].unsqueeze(1).broadcast_to([128, 8, TB]), ALU.subtract)
            v.drain()
            v.tensor_tensor(xf[:], scf[:],
                            p_bc2[:, TB:2 * TB].unsqueeze(1).broadcast_to([128, 8, TB]), ALU.mult)
            v.drain()
            return v.nop()
        f.op("vector", y0norm, reads=("x2t", "l2_p_bc"), writes=("xf2", "scf2"),
             tag=k + ":y0n")
        f.dma("sync", fm3(T["y0_spill"])[:, :, sl], xf[:], reads=("xf2",),
              writes=("y0_spill",), tag=k + ":spy0")

        p_bc3 = ln.stats(k + ":ln1", xf[:], "xf2", False, 0)

        def anorm(v):
            v.tensor_tensor(scf[:], xf[:],
                            p_bc3[:, 0:TB].unsqueeze(1).broadcast_to([128, 8, TB]), ALU.subtract)
            v.drain()
            v.tensor_tensor(scf[:], scf[:],
                            p_bc3[:, TB:2 * TB].unsqueeze(1).broadcast_to([128, 8, TB]), ALU.mult)
            v.drain()
            v.tensor_tensor(ab[:], scf[:],
                            cst["ln1_w"][:].unsqueeze(2).broadcast_to([128, 8, TB]), ALU.mult)
            v.drain()
            return v.nop()
        f.op("vector", anorm, reads=("xf2", "l2_p_bc"), writes=("abuf", "scf2"),
             tag=k + ":an")
        f.dma("sync", fm3(T[f"a_pub{tb}"]), ab[:], reads=("abuf",),
              writes=(f"a_pub{tb}",), tag=k + ":pub")

    d = T["dout"]("x2sp", [D, nto])
    if d is not None:
        f.dma("sync", d, T["x2_spill"], reads=("x2_spill",), tag="dbg:x2")
    d = T["dout"]("agat", [4 * D, nto], BF16)
    if d is not None:
        for c in range(T["NCB_O"]):
            f.coll("AllGather", ALU.bypass, [T[f"a_pub{c}"].opt()],
                   [T[f"a_gat{c}"].opt()], reads=(f"a_pub{c}",),
                   writes=(f"a_gat{c}",), tag=f"p2:ag{c}")
            f.dma("sync", d[:, c * TB:(c + 1) * TB], T[f"a_gat{c}"],
                  reads=(f"a_gat{c}",), tag=f"dbg:agat{c}")
        T["_a_coll_done"] = True


def emit_p3(f, nc, T, ctx):
    cst = T["cst"]; nto = T["nto"]; seq = T["seq"]; ntb = T["ntb_s"]
    sb = lambda n, s, dt=F32: ctx.enter_context(nc.sbuf_tensor(n, s, dt))
    ps = lambda n, s, dt=F32: ctx.enter_context(nc.psum_tensor(n, s, dt))

    kvl = sb("kvl", [128, 2, seq], BF16)
    ql = sb("ql", [128, 2, seq], BF16)
    ku_sb = sb("ku_sb", [128, 2, 128], BF16)
    qu_sb = sb("qu_sb", [128, 2, 128], BF16)
    vu_sb = sb("vu_sb", [128, 2, 256], BF16)
    p_l = ps("p_l", [128, TB])
    p_rh = ps("p_rh", [128, TB])
    krb_d = nc.dram_tensor("krb_d", [128, seq], BF16, kind="Internal").ap()
    qrb_d = nc.dram_tensor("qrb_d", [128, seq], BF16, kind="Internal").ap()

    if not T.get("_a_coll_done"):
        for c in range(T["NCB_O"]):
            f.coll("AllGather", ALU.bypass, [T[f"a_pub{c}"].opt()],
                   [T[f"a_gat{c}"].opt()], reads=(f"a_pub{c}",),
                   writes=(f"a_gat{c}",), tag=f"p3:agc{c}")

    r3 = lambda ap: ap.rearrange("(kc p) m -> p kc m", p=128)
    for nm, dst in [("k_uT", ku_sb), ("q_uT", qu_sb), ("v_uT", vu_sb)]:
        f.dma("gpsimd", dst[:], r3(T[nm]), writes=(nm + "_sb",), tag="p3:ld" + nm)

    ctxa = ctx.enter_context(ExitStack())
    sba = lambda n, s, dt=F32: ctxa.enter_context(nc.sbuf_tensor(n, s, dt))
    krch = [sba("krch0", [128, TB], BF16), sba("krch1", [128, TB], BF16)]
    qrch = [sba("qrch0", [128, TB], BF16), sba("qrch1", [128, TB], BF16)]
    _ag0 = sba("ag0", [128, 8, TB], BF16)
    ag = [_ag0, _ag0]
    cosb = sba("cosb", [128, TB]); sinb = sba("sinb", [128, TB])
    rr1 = sba("rr1", [128, TB]); rr2 = sba("rr2", [128, TB]); rr3 = sba("rr3", [128, TB])
    kvd_sb = sba("kvd_sb", [128, 8, 256], BF16)
    qd_sb = sba("qd_sb", [128, 8, 256], BF16)
    rk_sb = sba("rk_sb", [128, 8, 128], BF16)
    rq_sb = sba("rq_sb", [128, 2, 128], BF16)
    for nm, dst in [("kv_dT", kvd_sb), ("q_dT", qd_sb), ("rope_kT", rk_sb),
                    ("rope_qT", rq_sb)]:
        f.dma("gpsimd", dst[:], r3(T[nm]), writes=(nm + "_sb",), tag="p3:ld" + nm)

    for gtb in range(ntb):
        par = 0
        cpar = gtb % 2
        sl = slice(gtb * TB, (gtb + 1) * TB)
        k = f"p3a.{gtb}"
        t0 = gtb * TB
        r = t0 // nto
        loc = t0 - r * nto
        ch = loc // TB
        src_ap = T[f"a_gat{ch}"][r * 1024:(r + 1) * 1024, :] \
            .rearrange("(kc p) t -> p kc t", p=128)
        f.dma("sync", ag[par][:], src_ap, reads=(f"a_gat{ch}",), writes=(f"ag{par}",),
              tag=k + ":ld")
        f.dma("sync", cosb[:], T["cos_t"][:, sl], writes=("cosb",), tag=k + ":ldc")
        f.dma("sync", sinb[:], T["sin_t"][:, sl], writes=("sinb",), tag=k + ":lds")

        for nm, wsb, dst, dres in [("kv", kvd_sb, kvl, "kvl"), ("q", qd_sb, ql, "ql")]:
            for mc in range(2):
                pl = p_l if mc == 0 else p_rh
                PL = "p_l" if mc == 0 else "p_rh"
                def latmm(t, wsb=wsb, mc=mc, par=par, pl=pl):
                    last = None
                    for kc in range(8):
                        last = t.matmul(pl[:], wsb[:, kc, 128 * mc:128 * mc + 128],
                                        ag[par][:, kc, :], start=(kc == 0), stop=(kc == 7))
                    return last
                f.op("tensor", latmm, reads=(f"ag{par}", ("kv_dT" if nm == "kv" else "q_dT") + "_sb", PL),
                     writes=(PL,), tag=k + f":{nm}mm{mc}")
                f.op("vector", lambda v, dst=dst, mc=mc, sl=sl, pl=pl: v.tensor_copy(
                    dst[:, mc, sl], pl[:]),
                    reads=(PL,), writes=(dres,), tag=k + f":{nm}ev{mc}")

        def krmm(t, par=par):
            last = None
            for kc in range(8):
                last = t.matmul(p_l[:], rk_sb[:, kc, :], ag[par][:, kc, :],
                                start=(kc == 0), stop=(kc == 7))
            return last
        f.op("tensor", krmm, reads=(f"ag{par}", "rope_kT_sb", "kvl", "ql", "p_l"),
             writes=("p_l",), tag=k + ":krmm")
        f.op("vector", lambda v: v.tensor_copy(rr1[:], p_l[:]),
             reads=("p_l", "rr1"), writes=("rr1",), tag=k + ":krev")
        f.op("tensor", lambda t: t.matmul(p_rh[:], cst["r4T"][:], rr1[:],
                                          start=True, stop=True),
             reads=("rr1",), writes=("p_rh",), tag=k + ":krrh")

        def krope(v, cpar=cpar):
            v.tensor_tensor(rr2[:], rr1[:], cosb[:], ALU.mult)
            v.drain()
            v.tensor_tensor(rr3[:], p_rh[:], sinb[:], ALU.mult)
            v.drain()
            v.tensor_tensor(krch[cpar][:], rr2[:], rr3[:], ALU.add)
            v.drain()
            return v.nop()
        f.op("vector", krope, reads=("rr1", "p_rh", "cosb", "sinb", f"krch{cpar}"),
             writes=(f"krch{cpar}", "rr2", "rr3"), tag=k + ":krope")
        f.dma("sync", krb_d[:, sl], krch[cpar][:], reads=(f"krch{cpar}",),
              writes=("krb_d",), tag=k + ":krsp")

        def qrmm(t, sl=sl):
            last = None
            for mc in range(2):
                last = t.matmul(p_l[:], rq_sb[:, mc, :], ql[:, mc, sl],
                                start=(mc == 0), stop=(mc == 1))
            return last
        f.op("tensor", qrmm, reads=("ql", "rope_qT_sb", "krb_d", "p_l"), writes=("p_l",),
             tag=k + ":qrmm")
        f.op("vector", lambda v: v.tensor_copy(rr1[:], p_l[:]),
             reads=("p_l", "rr1"), writes=("rr1",), tag=k + ":qrev")
        f.op("tensor", lambda t: t.matmul(p_rh[:], cst["r4T"][:], rr1[:],
                                          start=True, stop=True),
             reads=("rr1", "p_rh"), writes=("p_rh",), tag=k + ":qrrh")

        def qrope(v, cpar=cpar):
            v.tensor_tensor(rr2[:], rr1[:], cosb[:], ALU.mult)
            v.drain()
            v.tensor_tensor(rr3[:], p_rh[:], sinb[:], ALU.mult)
            v.drain()
            v.tensor_tensor(rr2[:], rr2[:], rr3[:], ALU.add)
            v.drain()
            v.tensor_scalar(out=qrch[cpar][:], in0=rr2[:], scalar1=0.125, scalar2=None,
                            op0=ALU.mult)
            v.drain()
            return v.nop()
        f.op("vector", qrope, reads=("rr1", "p_rh", "cosb", "sinb", f"qrch{cpar}"),
             writes=(f"qrch{cpar}", "rr2", "rr3", f"ag{par}", "cosb", "sinb"),
             tag=k + ":qrope")
        f.dma("sync", qrb_d[:, sl], qrch[cpar][:], reads=(f"qrch{cpar}",),
              writes=("qrb_d",), tag=k + ":qrsp")

    f.barrier("p3a")
    ctxa.close()

    kaug = [sb("kaugA", [65, seq], BF16), sb("kaugB", [65, seq], BF16)]
    qaug = [sb("qaugA", [65, seq], BF16), sb("qaugB", [65, seq], BF16)]
    vaug = [sb("vaugA", [128, seq // 128, 65], BF16),
            sb("vaugB", [128, seq // 128, 65], BF16)]
    sqsK = [sb("sqsKA", [64, TB], BF16), sb("sqsKB", [64, TB], BF16)]
    sqsQ = [sb("sqsQA", [64, TB], BF16), sb("sqsQB", [64, TB], BF16)]
    mkr = [sb("mkrA", [1, 8]), sb("mkrB", [1, 8])]
    rowb = sb("rowb", [1, seq], BF16)
    ybuf = sb("ybuf", [64, TB], BF16)
    osum = [sb("osumA", [65, TB]), sb("osumB", [65, TB])]
    rrow = [sb("rrowA", [1, TB]), sb("rrowB", [1, TB])]
    rrow_b = [sb("rrow_bA", [1, TB], BF16), sb("rrow_bB", [1, TB], BF16)]
    ppb = sb("ppb", [128, 16, TB], BF16)
    p_bc2 = ps("p_bc2", [128, TB])
    p_s = [ps("p_s0", [128, TB]), ps("p_s1", [128, TB]), p_l, p_bc2]
    PS_RES = ("p_s0", "p_s1", "p_l", "p_bc2")
    p_os = [ps("p_os0", [65, TB]), ps("p_os1", [65, TB])]
    p_kq = ps("p_kq", [128, TB])

    njc = seq // 128

    def make_prologue(h):
        hp = h % 2
        ka, qa, va, mk = kaug[hp], qaug[hp], vaug[hp], mkr[hp]
        sqK, sqQ = sqsK[hp], sqsQ[hp]
        KA, QA, VA, MK = (f"kaug{hp}", f"qaug{hp}", f"vaug{hp}", f"mkr{hp}")
        SK, SQ = f"sqsK{hp}", f"sqsQ{hp}"
        kh = f"p3b{h}"
        ops = []
        E = ops.append
        E(lambda: f.dma("sync", ka[0:32, :], krb_d[32 * h:32 * h + 32, :],
                        reads=("krb_d", KA), writes=(KA,), tag=kh + ":kr"))
        E(lambda: f.dma("sync", qa[0:32, :], qrb_d[32 * h:32 * h + 32, :],
                        reads=("qrb_d", QA), writes=(QA,), tag=kh + ":qr"))
        E(lambda: f.op("gpsimd", lambda g: g.memset(ka[64:65, :], 1.0),
                       reads=(KA,), writes=(KA,), tag=kh + ":k1"))
        E(lambda: f.op("gpsimd", lambda g: g.memset(va[:], 1.0),
                       reads=(VA,), writes=(VA,), tag=kh + ":v1"))
        for gtb in range(ntb):
            sl = slice(gtb * TB, (gtb + 1) * TB)
            def kumm(t, sl=sl):
                last = None
                for mc in range(2):
                    last = t.matmul(p_kq[0:32, :], ku_sb[:, mc, 32 * h:32 * h + 32],
                                    kvl[:, mc, sl], start=(mc == 0), stop=(mc == 1))
                return last
            E(lambda kumm=kumm, gtb=gtb: f.op("tensor", kumm,
                                              reads=("kvl", "k_uT_sb", "p_kq"),
                                              writes=("p_kq",), tag=kh + f":ku{gtb}"))
            E(lambda sl=sl, gtb=gtb: f.op(
                "vector", lambda v, sl=sl: v.tensor_copy(ka[32:64, sl], p_kq[0:32, :]),
                reads=("p_kq",), writes=(KA,), tag=kh + f":kuev{gtb}"))
            def qumm(t, sl=sl):
                last = None
                for mc in range(2):
                    last = t.matmul(p_kq[0:32, :], qu_sb[:, mc, 32 * h:32 * h + 32],
                                    ql[:, mc, sl], start=(mc == 0), stop=(mc == 1))
                return last
            E(lambda qumm=qumm, gtb=gtb: f.op("tensor", qumm,
                                              reads=("ql", "q_uT_sb", "p_kq"),
                                              writes=("p_kq",), tag=kh + f":qu{gtb}"))
            E(lambda sl=sl, gtb=gtb: f.op(
                "vector", lambda v, sl=sl: v.tensor_scalar(
                    out=qa[32:64, sl], in0=p_kq[0:32, :], scalar1=0.125, scalar2=None,
                    op0=ALU.mult),
                reads=("p_kq",), writes=(QA,), tag=kh + f":quev{gtb}"))
        for jc in range(njc):
            def vmm2(t, jc=jc):
                last = None
                for mc in range(2):
                    last = t.matmul(p_rh[:, 0:64], kvl[:, mc, 128 * jc:128 * jc + 128],
                                    vu_sb[:, mc, 64 * h:64 * h + 64],
                                    start=(mc == 0), stop=(mc == 1))
                return last
            E(lambda vmm2=vmm2, jc=jc: f.op("tensor", vmm2,
                                            reads=("kvl", "v_uT_sb", "p_rh"),
                                            writes=("p_rh",), tag=kh + f":v{jc}"))
            E(lambda jc=jc: f.op(
                "vector", lambda v, jc=jc: v.tensor_copy(va[:, jc, 0:64],
                                                         p_rh[:, 0:64]),
                reads=("p_rh", VA), writes=(VA,), tag=kh + f":vev{jc}"))
        sqd = [sqsK[0], sqsQ[0], sqsK[1], sqsQ[1]]
        SQD = ("sqsK0", "sqsQ0", "sqsK1", "sqsQ1")

        def em_sq(src_ap, SRC, gtb, nm):
            sl = slice(gtb * TB, (gtb + 1) * TB)
            d = gtb % 4
            E(lambda sl=sl, d=d, gtb=gtb: f.op(
                "vector", lambda v, sl=sl, d=d: v.tensor_tensor(
                    sqd[d][:], src_ap[0:64, sl], src_ap[0:64, sl], ALU.mult),
                reads=(SRC, SQD[d]), writes=(SQD[d],), tag=kh + f":sq{nm}{gtb}"))

        def em_ss(gtb, nm):
            sl = slice(gtb * TB, (gtb + 1) * TB)
            d = gtb % 4
            E(lambda d=d, gtb=gtb: f.op(
                "tensor", lambda t, d=d: t.matmul(
                    p_kq[0:1, :], cst["onescol_b"][0:64, 0:1], sqd[d][:],
                    start=True, stop=True),
                reads=(SQD[d], "p_kq"), writes=("p_kq",), tag=kh + f":ss{nm}{gtb}"))
            E(lambda sl=sl, gtb=gtb: f.op(
                "vector", lambda v, sl=sl: v.tensor_copy(rowb[0:1, sl], p_kq[0:1, :]),
                reads=("p_kq", "rowb"), writes=("rowb",), tag=kh + f":ev{nm}{gtb}"))

        LEAD = 3
        for gtb in range(min(LEAD, ntb)):
            em_sq(ka, KA, gtb, "k")
        for gtb in range(ntb):
            if gtb + LEAD < ntb:
                em_sq(ka, KA, gtb + LEAD, "k")
            em_ss(gtb, "k")
        E(lambda: f.op("vector", lambda v: v.tensor_reduce(
            mk[0:1, 0:1], rowb[0:1, :], axis=AX.X, op=ALU.max),
            reads=("rowb", MK), writes=(MK,), tag=kh + ":mkred"))
        E(lambda: f.op("scalar", lambda s: s.activation(mk[0:1, 1:2], mk[0:1, 0:1],
                                                        AF.Ln),
                       reads=(MK,), writes=(MK,), tag=kh + ":mkln"))
        for gtb in range(min(LEAD, ntb)):
            em_sq(qa, QA, gtb, "q")
        for gtb in range(ntb):
            if gtb + LEAD < ntb:
                em_sq(qa, QA, gtb + LEAD, "q")
            em_ss(gtb, "q")
        E(lambda: f.op("scalar", lambda s: s.activation(rowb[0:1, :], rowb[0:1, :],
                                                        AF.Ln),
                       reads=("rowb",), writes=("rowb",), tag=kh + ":bln"))
        E(lambda: f.op("vector", lambda v: v.tensor_scalar(
            out=rowb[0:1, :], in0=rowb[0:1, :], scalar1=mk[0:1, 1:2], scalar2=None,
            op0=ALU.add),
            reads=("rowb", MK), writes=("rowb",), tag=kh + ":badd"))
        E(lambda: f.op("scalar", lambda s: s.activation(rowb[0:1, :], rowb[0:1, :],
                                                        AF.Exp, scale=0.5),
                       reads=("rowb",), writes=("rowb",), tag=kh + ":bexp"))
        E(lambda: f.op("vector", lambda v: v.tensor_scalar(
            out=qa[64:65, :], in0=rowb[0:1, :], scalar1=-1.0, scalar2=None,
            op0=ALU.mult),
            reads=("rowb", QA), writes=(QA,), tag=kh + ":bneg"))
        return ops

    GN = 8
    nst = seq // TB

    def emit_head(h, pend):
        hp = h % 2
        ka, qa, va = kaug[hp], qaug[hp], vaug[hp]
        KA, QA, VA = f"kaug{hp}", f"qaug{hp}", f"vaug{hp}"
        kh = f"p3b{h}"
        chunks = []
        for st in range(nst):
            njt = 4 * (st + 1)
            nchk = (njt + GN - 1) // GN
            for ck in range(nchk):
                chunks.append((st, ck, ck * GN, min(GN, njt - ck * GN),
                               ck == nchk - 1))
        NCH = len(chunks)

        def s_exp(ci):
            st, ck, j0, jn, is_last = chunks[ci]
            isl = slice(st * TB, (st + 1) * TB)
            sbase = (ci % 2) * 8
            for jj in range(jn):
                jt = j0 + jj
                par = (ci * GN + jj) % 4
                slot = sbase + jj
                def smm(t, jt=jt, par=par, isl=isl):
                    return t.matmul(p_s[par][:], ka[:, 128 * jt:128 * jt + 128],
                                    qa[:, isl], start=True, stop=True)
                f.op("tensor", smm, reads=(KA, QA), writes=(PS_RES[par],),
                     tag=kh + f":s{st}.{jt}")

                def expf(s, slot=slot, par=par):
                    return s.activation(ppb[:, slot, :], p_s[par][:], AF.Exp)
                f.op("scalar", expf, reads=(PS_RES[par],), writes=(f"ppb{slot}",),
                     tag=kh + f":e{st}.{jt}")
                if jt >= 4 * st:
                    off = jt - 4 * st
                    f.op("vector", lambda v, slot=slot, off=off: v.tensor_tensor(
                        ppb[:, slot, :], ppb[:, slot, :], cst["amask"][:, off, :],
                        ALU.mult),
                        reads=(f"ppb{slot}",), writes=(f"ppb{slot}",),
                        tag=kh + f":m{st}.{jt}")

        def av(ci):
            st, ck, j0, jn, is_last = chunks[ci]
            opar = ci % 2
            sbase = (ci % 2) * 8
            osp = st % 2

            def avg(t, j0=j0, jn=jn, opar=opar, sbase=sbase):
                last = None
                for jj in range(jn):
                    last = t.matmul(p_os[opar][:], va[:, j0 + jj, :],
                                    ppb[:, sbase + jj, :], start=(jj == 0),
                                    stop=(jj == jn - 1))
                return last
            f.op("tensor", avg,
                 reads=tuple(f"ppb{sbase + jj}" for jj in range(jn))
                 + (VA, f"p_os{opar}"),
                 writes=(f"p_os{opar}",), tag=kh + f":av{st}.{ck}")
            if ck == 0:
                f.op("vector", lambda v, opar=opar, osp=osp: v.tensor_copy(
                    osum[osp][:], p_os[opar][:]),
                    reads=(f"p_os{opar}", f"osum{osp}"), writes=(f"osum{osp}",),
                    tag=kh + f":oi{st}")
            else:
                f.op("vector", lambda v, opar=opar, osp=osp: v.tensor_tensor(
                    osum[osp][:], osum[osp][:], p_os[opar][:], ALU.add),
                    reads=(f"p_os{opar}", f"osum{osp}"), writes=(f"osum{osp}",),
                    tag=kh + f":oa{st}.{ck}")
            if is_last:
                f.op("vector", lambda v, osp=osp: v.reciprocal(
                    rrow[osp][0:1, :], osum[osp][64:65, :]),
                    reads=(f"osum{osp}", f"rrow{osp}"), writes=(f"rrow{osp}",),
                    tag=kh + f":rc{st}")
                f.op("vector", lambda v, osp=osp: v.tensor_copy(
                    rrow_b[osp][0:1, :], rrow[osp][0:1, :]),
                    reads=(f"rrow{osp}", f"rrowb{osp}"), writes=(f"rrowb{osp}",),
                    tag=kh + f":rcb{st}")

        def tail(ci):
            st, ck, j0, jn, is_last = chunks[ci]
            if not is_last:
                return
            osp = st % 2
            isl = slice(st * TB, (st + 1) * TB)
            f.op("tensor", lambda t, osp=osp: t.matmul(
                p_bc2[0:64, :], cst["onesr_b"][0:1, 0:64], rrow_b[osp][0:1, :],
                start=True, stop=True),
                 reads=(f"rrowb{osp}", "p_bc2"), writes=("p_bc2",), tag=kh + f":rb{st}")
            f.op("vector", lambda v, osp=osp: v.tensor_tensor(
                ybuf[:], osum[osp][0:64, :], p_bc2[0:64, :], ALU.mult),
                reads=(f"osum{osp}", "p_bc2", "ybuf"), writes=("ybuf",),
                tag=kh + f":ymul{st}")
            cc = (st * TB) // T["CW"]
            off = st * TB - cc * T["CW"]
            f.dma("sync", T[f"yat_pub{cc}"][64 * h:64 * h + 64, off:off + TB],
                  ybuf[:], reads=("ybuf",), writes=(f"yat_pub{cc}",),
                  tag=kh + f":pub{st}")

        NPOP = 5
        for ci in range(NCH):
            s_exp(ci)
            if ci >= 1:
                av(ci - 1)
            if ci >= 2:
                tail(ci - 2)
            for _ in range(NPOP):
                if pend:
                    pend.pop(0)()
        av(NCH - 1)
        tail(NCH - 2)
        tail(NCH - 1)
        while pend:
            pend.pop(0)()

    for t in make_prologue(0):
        t()
    for h in range(4):
        pend = make_prologue(h + 1) if h < 3 else []
        emit_head(h, pend)

    for cc in range(T["NCB_T"]):
        f.coll("AllGather", ALU.bypass, [T[f"yat_pub{cc}"].opt()],
               [T[f"yat_gat{cc}"].opt()], reads=(f"yat_pub{cc}",),
               writes=(f"yat_gat{cc}",), tag=f"p3:ag{cc}")
    d = T["dout"]("yatg", [1024, seq], BF16)
    if d is not None:
        for cc in range(T["NCB_T"]):
            f.dma("sync", d[:, cc * T["CW"]:(cc + 1) * T["CW"]], T[f"yat_gat{cc}"],
                  reads=(f"yat_gat{cc}",), tag=f"dbg:yat{cc}")


def emit_p4(f, nc, T, ctx):
    cst = T["cst"]; nto = T["nto"]; ntb = T["ntb_o"]
    sb = lambda n, s, dt=F32: ctx.enter_context(nc.sbuf_tensor(n, s, dt))
    ps = lambda n, s, dt=F32: ctx.enter_context(nc.psum_tensor(n, s, dt))
    xm = T["xm"]

    ya = sb("ya", [128, 8, nto], BF16)
    ow_sb = sb("ow_sb", [128, 8, 1024], BF16)
    y0t = sb("y0t", [128, 8, TB])
    y1t = sb("y1t", [128, 8, TB])
    scf = sb("scf4", [128, 8, TB])
    p_w = [ps("p_w4A", [128, TB]), ps("p_w4B", [128, TB])]
    ln = LNH(f, nc, cst, ctx, "l4_")

    f.dma("gpsimd", ow_sb[:], T["o_wT"].rearrange("(kc p) m -> p kc m", p=128),
          writes=("ow_sb",), tag="p4:ldow")

    def gat(e, core):
        q = core % 4
        cc = (nto * q) // T["CW"]
        off = nto * q - cc * T["CW"]
        return e.dma_start(out=ya[:], in_=T[f"yat_gat{cc}"]
                           .rearrange("(kc p) t -> p kc t", p=128)
                           [:, :, off:off + nto])
    f.dma_percore("sync", gat,
                  reads=tuple(f"yat_gat{cc}" for cc in range(T["NCB_T"])),
                  writes=("ya",), tag="p4:gat")

    for tb in range(ntb):
        sl = slice(tb * TB, (tb + 1) * TB)
        k = f"p4.{tb}"
        f.dma("sync", y0t[:], fm3(T["y0_spill"])[:, :, sl], reads=("y0_spill",),
              writes=("y0t",), tag=k + ":ldy0")
        for mc in range(8):
            wp = mc % 2
            def owmm(t, mc=mc, sl=sl, wp=wp):
                last = None
                for kc in range(8):
                    last = t.matmul(p_w[wp][:], ow_sb[:, kc, 128 * mc:128 * mc + 128],
                                    ya[:, kc, sl], start=(kc == 0), stop=(kc == 7))
                return last
            f.op("tensor", owmm, reads=("ya", "ow_sb"), writes=(f"p_w4{wp}",), tag=k + f":ow{mc}")

            def y1ev(v, mc=mc, wp=wp):
                v.tensor_tensor(y1t[:, mc, :], p_w[wp][:], y0t[:, mc, :], ALU.add)
                v.drain()
                return v.nop()
            f.op("vector", y1ev, reads=(f"p_w4{wp}", "y0t"), writes=("y1t",), tag=k + f":y1{mc}")
        f.dma("sync", fm3(T["y1_spill"])[:, :, sl], y1t[:], reads=("y1t",),
              writes=("y1_spill",), tag=k + ":spy1")

        p_bc = ln.stats(k + ":ln2", y1t[:], "y1t", False, 0)

        def xnorm(v, sl=sl):
            v.tensor_tensor(scf[:], y1t[:],
                            p_bc[:, 0:TB].unsqueeze(1).broadcast_to([128, 8, TB]), ALU.subtract)
            v.drain()
            v.tensor_tensor(scf[:], scf[:],
                            p_bc[:, TB:2 * TB].unsqueeze(1).broadcast_to([128, 8, TB]), ALU.mult)
            v.drain()
            v.tensor_tensor(xm[:, :, sl], scf[:],
                            cst["ln2_w"][:].unsqueeze(2).broadcast_to([128, 8, TB]), ALU.mult)
            v.drain()
            return v.nop()
        f.op("vector", xnorm, reads=("y1t", "l4_p_bc"), writes=("xm", "scf4"),
             tag=k + ":xn")

    d = T["dout"]("xmo", [128, 8, nto], BF16)
    if d is not None:
        f.dma("sync", d, xm[:], reads=("xm",), tag="dbg:xm")


def emit_p5(f, nc, T, ctx):
    cst = T["cst"]; nto = T["nto"]; ntb = T["ntb_o"]
    sb = lambda n, s, dt=F32: ctx.enter_context(nc.sbuf_tensor(n, s, dt))
    ps = lambda n, s, dt=F32: ctx.enter_context(nc.psum_tensor(n, s, dt))
    xm = T["xm"]
    BIG = 1.0e30

    accum = T["accum"]
    rw_sb = sb("rw_sb", [128, 8, 8], BF16)
    rbb = sb("rbb", [128, 8])
    lg = sb("lg", [128, nto // 128, 8])
    g1 = sb("g1", [128, nto // 128, 8])
    g2 = sb("g2", [128, nto // 128, 8])
    q3 = sb("q3", [128, nto // 128, 8])
    stt5 = sb("stt5", [128, 8 * (nto // 128)])
    wgt_fmb = sb("wgt_fmb", [8, nto], BF16)
    act_sb = sb("act_sb", [128, 16, 2 * TB], BF16)
    wg_sb = sb("wg_sb", [128, 8, TB], BF16)
    wu_sb = sb("wu_sb", [128, 8, TB], BF16)
    dn_sb = sb("dn_sb", [128, 16, 1024], BF16)
    sg = sb("sg", [128, TB], BF16)
    sgd = sb("sgd", [128, TB])
    wbt = sb("wbt", [128, 2 * TB], BF16)
    p_g = ps("p_g", [128, TB])
    p_u = ps("p_u", [128, TB])
    p_d = [ps("p_dA", [128, TB]), ps("p_dB", [128, TB])]
    p_wb = ps("p_wb", [128, 2 * TB])
    p_r = ps("p_r", [128, 8])
    ntt = nto // 128

    f.dma("gpsimd", rw_sb[:], T["r_wT"].rearrange("(kc p) m -> p kc m", p=128),
          writes=("rw_sb",), tag="p5:ldrw")
    f.op("tensor", lambda t: t.matmul(p_r[:], cst["onesr_f"][:], cst["r_b"][:],
                                      start=True, stop=True),
         reads=(), writes=("p_r",), tag="p5:rbb")
    f.op("vector", lambda v: v.tensor_copy(rbb[:], p_r[:]),
         reads=("p_r",), writes=("rbb",), tag="p5:rbbev")

    for tt in range(ntt):
        tsl = slice(tt * 128, tt * 128 + 128)
        def rmm(t, tsl=tsl):
            last = None
            for kc in range(8):
                last = t.matmul(p_r[:], xm[:, kc, tsl], rw_sb[:, kc, :],
                                start=(kc == 0), stop=(kc == 7))
            return last
        f.op("tensor", rmm, reads=("xm", "rw_sb", "rbb"), writes=("p_r",),
             tag=f"p5:rmm{tt}")
        f.op("vector", lambda v, tt=tt: v.tensor_tensor(lg[:, tt, :], p_r[:], rbb[:],
                                                        ALU.add),
             reads=("p_r", "rbb"), writes=("lg",), tag=f"p5:lgev{tt}")

    f.op("scalar", lambda s: s.activation(lg[:], lg[:], AF.Tanh, scale=0.5),
         reads=("lg",), writes=("lg",), tag="p5:tanh")
    f.op("vector", lambda v: v.tensor_scalar(out=lg[:], in0=lg[:], scalar1=0.5,
                                             scalar2=0.5, op0=ALU.mult, op1=ALU.add),
         reads=("lg",), writes=("lg",), tag="p5:probs")

    OF_M1, OF_I1, OF_M2, OF_I2, OF_W1, OF_W2, OF_T = (i * ntt for i in range(7))
    iob = cst["iota7"][:, 0:8].unsqueeze(1).broadcast_to([128, ntt, 8])
    bcc = lambda off: stt5[:, off:off + ntt].unsqueeze(2).broadcast_to([128, ntt, 8])

    def top2(v):
        v.tensor_reduce(stt5[:, OF_M1:OF_M1 + ntt], lg[:], axis=AX.X, op=ALU.max)
        v.drain()
        v.tensor_tensor(g1[:], lg[:], bcc(OF_M1), ALU.is_ge)
        v.drain()
        v.tensor_scalar(out=q3[:], in0=g1[:], scalar1=-BIG, scalar2=BIG,
                        op0=ALU.mult, op1=ALU.add)
        v.drain()
        v.tensor_tensor(q3[:], q3[:], iob, ALU.add)
        v.drain()
        v.tensor_reduce(stt5[:, OF_I1:OF_I1 + ntt], q3[:], axis=AX.X, op=ALU.min)
        v.drain()
        v.tensor_tensor(g1[:], iob, bcc(OF_I1), ALU.is_equal)
        v.drain()
        v.scalar_tensor_tensor(out=q3[:], in0=g1[:], scalar=-BIG, in1=lg[:],
                               op0=ALU.mult, op1=ALU.add)
        v.drain()
        v.tensor_reduce(stt5[:, OF_M2:OF_M2 + ntt], q3[:], axis=AX.X, op=ALU.max)
        v.drain()
        v.tensor_tensor(g2[:], q3[:], bcc(OF_M2), ALU.is_ge)
        v.drain()
        v.tensor_scalar(out=q3[:], in0=g2[:], scalar1=-BIG, scalar2=BIG,
                        op0=ALU.mult, op1=ALU.add)
        v.drain()
        v.tensor_tensor(q3[:], q3[:], iob, ALU.add)
        v.drain()
        v.tensor_reduce(stt5[:, OF_I2:OF_I2 + ntt], q3[:], axis=AX.X, op=ALU.min)
        v.drain()
        v.tensor_tensor(g2[:], iob, bcc(OF_I2), ALU.is_equal)
        v.drain()
        v.tensor_tensor(stt5[:, OF_T:OF_T + ntt], stt5[:, OF_M1:OF_M1 + ntt],
                        stt5[:, OF_M2:OF_M2 + ntt], ALU.add)
        v.drain()
        v.reciprocal(stt5[:, OF_T:OF_T + ntt], stt5[:, OF_T:OF_T + ntt])
        v.drain()
        v.tensor_tensor(stt5[:, OF_W1:OF_W1 + ntt], stt5[:, OF_M1:OF_M1 + ntt],
                        stt5[:, OF_T:OF_T + ntt], ALU.mult)
        v.tensor_tensor(stt5[:, OF_W2:OF_W2 + ntt], stt5[:, OF_M2:OF_M2 + ntt],
                        stt5[:, OF_T:OF_T + ntt], ALU.mult)
        v.drain()
        v.tensor_tensor(g1[:], g1[:], bcc(OF_W1), ALU.mult)
        v.tensor_tensor(g2[:], g2[:], bcc(OF_W2), ALU.mult)
        v.drain()
        v.tensor_tensor(g1[:], g1[:], g2[:], ALU.add)
        v.drain()
        return v.nop()
    f.op("vector", top2, reads=("lg",), writes=("g1", "g2", "q3", "stt5"),
         tag="p5:top2")

    for half in range((ntt + 7) // 8):
        tts = range(half * 8, min(ntt, half * 8 + 8))
        def wtr(t, tts=tts, half=half):
            last = None
            for tt in tts:
                last = t.transpose(p_wb[0:8, (tt - half * 8) * 128:(tt - half * 8) * 128 + 128],
                                   g1[:, tt, 0:8], cst["ident_f"][:])
            return last
        f.op("tensor", wtr, reads=("g1", "p_wb"), writes=("p_wb",), tag=f"p5:wtr{half}")
        n0 = half * 1024
        n1 = min(nto, n0 + 1024)
        f.op("vector", lambda v, n0=n0, n1=n1: v.tensor_copy(
            wgt_fmb[0:8, n0:n1], p_wb[0:8, 0:n1 - n0]),
            reads=("p_wb",), writes=("wgt_fmb",), tag=f"p5:wev{half}")

    NH2 = nto // (2 * TB) if nto >= 2 * TB else 1
    HLEN = min(2 * TB, nto)
    ntc2 = HLEN // TB
    for th in range(NH2):
        h0 = th * HLEN
        for e in range(NE):
            ke = f"p5.{th}.{e}"
            for ib in range(4):
                io = ib * TB
                for sub in range(4):
                    f.dma("gpsimd", wg_sb[:, :, 128 * sub:128 * sub + 128],
                          T["moe_gT"][e].rearrange("(kc p) m -> p kc m", p=128)
                          [:, :, io + 128 * sub:io + 128 * sub + 128],
                          writes=(f"wg_sb{sub}",), tag=ke + f":ldg{ib}.{sub}")
                for sub in range(4):
                    f.dma("gpsimd", wu_sb[:, :, 128 * sub:128 * sub + 128],
                          T["moe_uT"][e].rearrange("(kc p) m -> p kc m", p=128)
                          [:, :, io + 128 * sub:io + 128 * sub + 128],
                          writes=(f"wu_sb{sub}",), tag=ke + f":ldu{ib}.{sub}")
                for tc2 in range(ntc2):
                    tsl = slice(h0 + tc2 * TB, h0 + tc2 * TB + TB)
                    for sub in range(4):
                        ic = 4 * ib + sub
                        def gmm(t, sub=sub, tsl=tsl):
                            last = None
                            for kc in range(8):
                                last = t.matmul(p_g[:],
                                                wg_sb[:, kc, 128 * sub:128 * sub + 128],
                                                xm[:, kc, tsl],
                                                start=(kc == 0), stop=(kc == 7))
                            return last
                        f.op("tensor", gmm, reads=(f"wg_sb{sub}", "xm"), writes=("p_g",),
                             tag=ke + f":g{ib}.{tc2}.{sub}")
                        def umm(t, sub=sub, tsl=tsl):
                            last = None
                            for kc in range(8):
                                last = t.matmul(p_u[:],
                                                wu_sb[:, kc, 128 * sub:128 * sub + 128],
                                                xm[:, kc, tsl],
                                                start=(kc == 0), stop=(kc == 7))
                            return last
                        f.op("tensor", umm, reads=(f"wu_sb{sub}", "xm"), writes=("p_u",),
                             tag=ke + f":u{ib}.{tc2}.{sub}")
                        f.op("scalar", lambda s: s.activation(sg[:], p_g[:], AF.Silu),
                             reads=("p_g",), writes=("sg",), tag=ke + f":si{ib}.{tc2}.{sub}")
                        f.op("vector", lambda v, ic=ic, tc2=tc2: v.tensor_tensor(
                            act_sb[:, ic, tc2 * TB:tc2 * TB + TB], p_u[:], sg[:], ALU.mult),
                            reads=("p_u", "sg"), writes=("act_sb",),
                            tag=ke + f":am{ib}.{tc2}.{sub}")
            for icg in range(4):
                f.dma("gpsimd", dn_sb[:, 4 * icg:4 * icg + 4, :],
                      T["moe_dT"][e].rearrange("(ic p) m -> p ic m", p=128)
                      [:, 4 * icg:4 * icg + 4, :],
                      writes=(f"dn_sb{icg}",), tag=ke + f":ldd{icg}")
            if e > 0:
                def wbc(t, e=e, h0=h0):
                    last = None
                    for tc2 in range(ntc2):
                        last = t.matmul(p_wb[:, tc2 * TB:tc2 * TB + TB],
                                        cst["esel"][:, 128 * (e - 1):128 * (e - 1) + 128],
                                        wgt_fmb[0:8, h0 + tc2 * TB:h0 + tc2 * TB + TB],
                                        start=True, stop=True)
                    return last
                f.op("tensor", wbc, reads=("wgt_fmb", "p_wb"), writes=("p_wb",),
                     tag=ke + ":wbc")
                f.op("vector", lambda v: v.tensor_copy(wbt[:, 0:ntc2 * TB],
                                                       p_wb[:, 0:ntc2 * TB]),
                     reads=("p_wb",), writes=("wbt",), tag=ke + ":wbev")
            for tc2 in range(ntc2):
                tsl = slice(h0 + tc2 * TB, h0 + tc2 * TB + TB)
                for hc in range(8):
                    dp = hc % 2
                    def dmm(t, hc=hc, tc2=tc2, dp=dp):
                        last = None
                        for ic in range(16):
                            last = t.matmul(p_d[dp][:], dn_sb[:, ic, 128 * hc:128 * hc + 128],
                                            act_sb[:, ic, tc2 * TB:tc2 * TB + TB],
                                            start=(ic == 0), stop=(ic == 15))
                        return last
                    f.op("tensor", dmm,
                         reads=("dn_sb0", "dn_sb1", "dn_sb2", "dn_sb3", "act_sb"),
                         writes=(f"p_d{dp}",), tag=ke + f":d{tc2}.{hc}")
                    if e == 0:
                        f.op("vector", lambda v, hc=hc, tsl=tsl, dp=dp: v.tensor_copy(
                            accum[:, hc, tsl], p_d[dp][:]),
                            reads=(f"p_d{dp}",), writes=("accum",), tag=ke + f":ac{tc2}.{hc}")
                    else:
                        def acc(v, hc=hc, tsl=tsl, tc2=tc2, dp=dp):
                            v.tensor_tensor(sgd[:], p_d[dp][:], wbt[:, tc2 * TB:tc2 * TB + TB],
                                            ALU.mult)
                            v.drain()
                            v.tensor_tensor(accum[:, hc, tsl], accum[:, hc, tsl],
                                            sgd[:], ALU.add)
                            v.drain()
                            return v.nop()
                        f.op("vector", acc, reads=(f"p_d{dp}", "wbt", "accum"),
                             writes=("accum", "sgd"), tag=ke + f":ac{tc2}.{hc}")

    d = T["dout"]("moeo", [128, 8, nto])
    if d is not None:
        f.dma("sync", d, accum[:], reads=("accum",), tag="dbg:moe")


def emit_p6(f, nc, T, ctx):
    cst = T["cst"]; nto = T["nto"]; ntb = T["ntb_o"]
    sb = lambda n, s, dt=F32: ctx.enter_context(nc.sbuf_tensor(n, s, dt))
    accum = T["accum"]
    y1t = sb("y1t6", [128, 8, TB])
    x2t = sb("x2t6", [128, 8, TB])
    scf = sb("scf6", [128, 8, TB])
    outt = sb("outt", [128, 8, TB])
    ln = LNH(f, nc, cst, ctx, "l6_")

    for tb in range(ntb):
        sl = slice(tb * TB, (tb + 1) * TB)
        k = f"p6.{tb}"
        f.dma("sync", y1t[:], fm3(T["y1_spill"])[:, :, sl], reads=("y1_spill",),
              writes=("y1t6",), tag=k + ":ldy1")
        f.dma("sync", x2t[:], fm3(T["x2_spill"])[:, :, sl], reads=("x2_spill",),
              writes=("x2t6",), tag=k + ":ldx2")

        def y2c(v, sl=sl):
            v.tensor_tensor(y1t[:], y1t[:], accum[:, :, sl], ALU.add)
            v.drain()
            return v.nop()
        f.op("vector", y2c, reads=("y1t6", "accum"), writes=("y1t6",), tag=k + ":y2")

        p_bc = ln.stats(k + ":fl", y1t[:], "y1t6", False, 0)

        def onorm(v):
            v.tensor_tensor(scf[:], y1t[:],
                            p_bc[:, 0:TB].unsqueeze(1).broadcast_to([128, 8, TB]), ALU.subtract)
            v.drain()
            v.tensor_tensor(scf[:], scf[:],
                            p_bc[:, TB:2 * TB].unsqueeze(1).broadcast_to([128, 8, TB]), ALU.mult)
            v.drain()
            v.tensor_tensor(outt[:], scf[:], x2t[:], ALU.add)
            v.drain()
            return v.nop()
        f.op("vector", onorm, reads=("y1t6", "l6_p_bc", "x2t6"),
             writes=("outt", "scf6"), tag=k + ":on")
        f.dma("sync", fm3(T["out"])[:, :, sl], outt[:], reads=("outt",),
              writes=("out",), tag=k + ":st")


def _cc(a):
    return np.ascontiguousarray(a)


def prep_in_maps(inputs, seq=L):
    ii = {k: np.asarray(v) for k, v in inputs.items()}
    x = ii["L0_attn_hidden_states"].astype(np.float32, copy=False)
    nto = seq // 4
    nch = seq // C

    def A(name):
        return ii[name]

    ident_f = np.eye(128, dtype=np.float32)
    import ml_dtypes as _mld
    ident_b = np.eye(128, dtype=_mld.bfloat16)
    triuT_b = np.triu(np.ones((64, 64), np.float32)).astype("bfloat16_np", copy=False) \
        if False else np.triu(np.ones((64, 64))).astype(np.float32)
    import ml_dtypes
    bf = ml_dtypes.bfloat16
    triuT_b = np.triu(np.ones((64, 64))).astype(bf)
    i64z = np.zeros((128, 64), bf); i64z[0:64] = np.eye(64)
    triuTz = np.zeros((128, 64), bf); triuTz[0:64] = np.triu(np.ones((64, 64)))
    ones64 = np.ones((64, 64), bf)
    trimask4 = np.tile(np.triu(np.ones((64, 64), np.float32)), (1, 4))
    e2_b = np.zeros((128, 2), bf); e2_b[0:64, 0] = 1; e2_b[64:128, 1] = 1
    eb2_b = np.zeros((2, 128), bf); eb2_b[0, 0:64] = 1; eb2_b[1, 64:128] = 1
    onesr_b = np.ones((1, 128), bf)
    onesr_f = np.ones((1, 128), np.float32)
    onescol_b = np.ones((128, 1), bf)
    wmask = np.zeros((128, 128), np.float32)
    wmask[0:64, 0:64] = 1; wmask[64:128, 64:128] = 1
    amask = np.zeros((128, 4, TB), bf)
    for o in range(4):
        jj = 128 * o + np.arange(128)[:, None]
        amask[:, o, :] = (jj <= np.arange(TB)[None, :]).astype(bf)
    r4 = np.zeros((32, 32), np.float32)
    for dp in range(16):
        r4[dp + 16, dp] = -1.0
        r4[dp, dp + 16] = 1.0
    r4T = np.zeros((128, 128), np.float32)
    for blkk in range(4):
        r4T[32 * blkk:32 * blkk + 32, 32 * blkk:32 * blkk + 32] = r4
    iota7 = np.broadcast_to(np.arange(8, dtype=np.float32), (128, 8)).copy()
    epsv = np.zeros((64, 8), np.float32)
    epsv[:, 0] = 1e-5; epsv[:, 1] = 1e-6; epsv[:, 2] = 1e-24
    esel = np.zeros((8, 1024), bf)
    for e in range(8):
        esel[e, 128 * e:128 * e + 128] = 1
    inv = 1.0 / 10000.0 ** (np.arange(0, RD, 2, dtype=np.float32) / RD)
    tpos = np.arange(seq, dtype=np.float32)
    fr = tpos[:, None] * inv[None, :]
    emb = np.concatenate([fr, fr], -1)
    cos1 = np.cos(emb).T.astype(np.float32)
    sin1 = np.sin(emb).T.astype(np.float32)
    cos_t = np.tile(cos1, (4, 1)); sin_t = np.tile(sin1, (4, 1))

    woT = _cc(A("L0_attn_wo_w").T)
    wo_b = _cc(A("L0_attn_wo_b").reshape(8, 128).T)
    post_nw = _cc(A("L0_attn_post_norm_w").reshape(8, 128).T)
    post_nb = _cc(A("L0_attn_post_norm_b").reshape(8, 128).T)
    ln1_w = _cc(A("L0_ffn_ln1_w").reshape(8, 128).T)
    ln2_w = _cc(A("L0_ffn_ln2_w").reshape(8, 128).T)
    kv_dT = _cc(A("L0_ffn_kv_d").T); q_dT = _cc(A("L0_ffn_q_d").T)
    v_u = A("L0_ffn_v_u"); k_u = A("L0_ffn_k_u"); q_u = A("L0_ffn_q_u")
    rope_q = A("L0_ffn_rope_q_w"); rope_k = A("L0_ffn_rope_k_w")
    o_wT = _cc(A("L0_ffn_o_w").T)
    r_wT = np.zeros((D, 8), np.float32)
    r_wT[:, 0:7] = A("L0_ffn_router_w").T
    r_b = np.zeros((1, 8), np.float32)
    r_b[0, 0:7] = A("L0_ffn_routing_bias")
    r_b[0, 7] = -1.0e9
    moe_gT = np.zeros((NE, D, EI), np.float32)
    moe_uT = np.zeros((NE, D, EI), np.float32)
    moe_dT = np.zeros((NE, EI, D), np.float32)
    moe_gT[0] = A("L0_ffn_shared_gate")[0].T
    moe_uT[0] = A("L0_ffn_shared_up")[0].T
    moe_dT[0] = A("L0_ffn_shared_down")[0].T
    for e in range(7):
        moe_gT[e + 1] = A("L0_ffn_routed_gate")[e].T
        moe_uT[e + 1] = A("L0_ffn_routed_up")[e].T
        moe_dT[e + 1] = A("L0_ffn_routed_down")[e].T

    wq, wk, wv = A("L0_attn_wq_w"), A("L0_attn_wk_w"), A("L0_attn_wv_w")
    wqb, wkb, wvb = A("L0_attn_wq_b"), A("L0_attn_wk_b"), A("L0_attn_wv_b")
    gw = A("L0_attn_ttt_norm_weight"); gb = A("L0_attn_ttt_norm_bias")
    lrw = A("L0_attn_ttt_lr_weight"); lrb = A("L0_attn_ttt_lr_bias")
    W1 = A("L0_attn_W1"); b1 = A("L0_attn_b1")

    xfm = np.ascontiguousarray(x.transpose(0, 2, 1))

    in_maps = []
    for c in range(8):
        b, q = c // 4, c % 4
        hsl = slice(256 * q, 256 * q + 256)
        hh = slice(4 * q, 4 * q + 4)
        m = dict(
            x_own=_cc(xfm[b][:, nto * q:nto * q + nto]),
            wqT=_cc(wq[hsl].T), wkT=_cc(wk[hsl].T), wvT=_cc(wv[hsl].T),
            qkv_b=_cc(np.stack([wqb[hsl].reshape(2, 128).T,
                                wkb[hsl].reshape(2, 128).T,
                                wvb[hsl].reshape(2, 128).T], axis=-1)),
            lrwT=_cc(lrw[hh, 0, :].T), lr_b=_cc(lrb[hh].reshape(4, 1)),
            ttt_g2=_cc(gw[hh].reshape(256).reshape(2, 128).T),
            ttt_b2=_cc(gb[hh].reshape(256).reshape(2, 128).T),
            ttt_g_tm=np.broadcast_to(gw[hh].reshape(256), (64, 256)).copy(),
            ttt_b_tm=np.broadcast_to(gb[hh].reshape(256), (64, 256)).copy(),
            W2_0=_blkdiag(W1[4 * q], W1[4 * q + 1]),
            W2_1=_blkdiag(W1[4 * q + 2], W1[4 * q + 3]),
            b2_bc=np.broadcast_to(b1[hh].reshape(256), (64, 256)).copy(),
            post_nw=post_nw, post_nb=post_nb, woT=woT, wo_b=wo_b,
            ln1_w=ln1_w, ln2_w=ln2_w, kv_dT=kv_dT, q_dT=q_dT,
            k_uT=_cc(k_u.reshape(H, HD, 256)[hh, 32:64].reshape(128, 256).T),
            q_uT=_cc(q_u.reshape(H, HD, 256)[hh, 32:64].reshape(128, 256).T),
            v_uT=_cc(v_u.reshape(H, HD, 256)[hh].reshape(256, 256).T),
            rope_qT=_cc(rope_q.reshape(H, RD, 256)[hh].reshape(128, 256).T),
            rope_kT=_cc(rope_k.reshape(H, RD, D)[hh].reshape(128, D).T),
            o_wT=o_wT, cos_t=cos_t, sin_t=sin_t, r_wT=r_wT, r_b=r_b,
            moe_gT=moe_gT, moe_uT=moe_uT, moe_dT=moe_dT,
            ident_f=ident_f, ident_b=ident_b, triuT_b=triuT_b, i64z=i64z,
            triuTz=triuTz, ones64=ones64, trimask4=trimask4,
            e2_b=e2_b, eb2_b=eb2_b, onesr_b=onesr_b, onesr_f=onesr_f,
            onescol_b=onescol_b, wmask=wmask, amask=amask, r4T=r4T, iota7=iota7,
            epsv=epsv, esel=esel,
        )
        in_maps.append(m)
    return in_maps


def _blkdiag(a, bmat):
    W = np.zeros((128, 128), np.float32)
    W[0:64, 0:64] = a
    W[64:128, 64:128] = bmat
    return W


_BUILT = {}


def kernel(**inputs):
    seq = int(np.asarray(inputs["L0_attn_hidden_states"]).shape[1])
    key = seq
    if key not in _BUILT:
        _BUILT[key] = build(seq=seq)
    nc, _ = _BUILT[key]
    in_maps = prep_in_maps(inputs, seq=seq)
    res = run_bass_kernel_spmd(nc, in_maps, core_ids=list(range(8)))
    nto = seq // 4
    out = np.zeros((B, seq, D), np.float32)
    for c in range(8):
        b, q = c // 4, c % 4
        out[b, nto * q:nto * q + nto, :] = np.asarray(res.results[c]["out"]).T
    return out


# version 19 (final)
# speedup vs baseline: 1.0227x; 1.0080x over previous
import numpy as np
from contextlib import ExitStack

import concourse.bass as bass
import concourse.mybir as mybir
from concourse.bass_utils import run_bass_kernel_spmd

F32 = mybir.dt.float32
BF16 = mybir.dt.bfloat16
AF = mybir.ActivationFunctionType
ALU = mybir.AluOpType
AX = mybir.AxisListType

B, L, D = 2, 8192, 1024
H, HD, C = 16, 64, 64
LAT, RD = 256, 32
NE, EI = 8, 2048
TB = 512
GROUPS = [[0, 1, 2, 3], [4, 5, 6, 7]]

ENGS = ("tensor", "vector", "scalar", "gpsimd", "sync")
_DMA_Q = {"sync": "dma_sy", "gpsimd": "dma_gp", "scalar": "dma_sc"}


class Op:
    __slots__ = ("i", "eng", "emit", "reads", "writes", "kind", "needs_inc",
                 "inc_no", "waits", "drain_before", "tag")

    def __init__(self, i, eng, emit, reads, writes, kind, tag):
        self.i, self.eng, self.emit = i, eng, emit
        self.reads, self.writes = reads, writes
        self.kind, self.tag = kind, tag
        self.needs_inc = False
        self.inc_no = None
        self.waits = []
        self.drain_before = False


class Flow:

    NQ = 8
    NCQ = 4

    def __init__(self, nc, ctx):
        self.nc = nc
        self.ops = []
        self.sems = {}
        for e in ENGS:
            self.sems["e_" + e] = ctx.enter_context(nc.semaphore("e_" + e))
        for q in ("dma_sy", "dma_gp", "dma_sc"):
            for i in range(self.NQ):
                self.sems[f"{q}{i}"] = ctx.enter_context(nc.semaphore(f"{q}{i}"))
        for i in range(self.NCQ):
            self.sems[f"coll{i}"] = ctx.enter_context(nc.semaphore(f"coll{i}"))
        self.sems["bar"] = ctx.enter_context(nc.semaphore("bar"))
        self.nbar = 0

    def op(self, eng, emit, reads=(), writes=(), kind="compute", tag=""):
        o = Op(len(self.ops), eng, emit, tuple(reads), tuple(writes), kind, tag)
        self.ops.append(o)
        return o

    def dma(self, eng, out, in_, reads=(), writes=(), tag="", **kw):
        def emit(e):
            return e.dma_start(out=out, in_=in_, **kw)
        return self.op(eng, emit, reads, writes, kind="dma", tag=tag)

    def dma_percore(self, eng, fn, reads=(), writes=(), tag=""):
        return self.op(eng, fn, reads, writes, kind="dma_mc", tag=tag)

    def coll(self, kind, alu, ins, outs, reads, writes, tag=""):
        def emit(g):
            return g.collective_compute(kind, alu, replica_groups=GROUPS,
                                        ins=ins, outs=outs)
        return self.op("gpsimd", emit, reads, writes, kind="coll", tag=tag)

    def barrier(self, tag=""):
        self.nbar += 1
        for e in ENGS:
            self.op(e, None, kind="barrier", tag=f"bar{self.nbar}:{tag}")

    def finalize(self, blk):
        ops = self.ops
        edges = [()] * len(ops)
        lw, rd = {}, {}
        seen_bar = 0
        for o in ops:
            if o.kind == "barrier":
                seen_bar += 1
                if seen_bar % len(ENGS) == 0:
                    lw, rd = {}, {}
                continue
            deps = set()
            for r in o.reads:
                if r in lw:
                    deps.add(lw[r])
            for w in o.writes:
                if w in lw:
                    deps.add(lw[w])
                deps.update(rd.get(w, ()))
            deps.discard(o.i)
            edges[o.i] = tuple(sorted(deps))
            for r in o.reads:
                rd.setdefault(r, []).append(o.i)
            for w in o.writes:
                lw[w] = o.i
                rd[w] = []
        for o in ops:
            for pi in edges[o.i]:
                pr = ops[pi]
                if pr.kind == "compute" and pr.eng != o.eng:
                    pr.needs_inc = True
        eng_inc = {e: 0 for e in ENGS}
        q_cnt = {"dma_sy": 0, "dma_gp": 0, "dma_sc": 0, "coll": 0}
        for o in ops:
            if o.kind == "compute" and o.needs_inc:
                eng_inc[o.eng] += 1
                o.inc_no = eng_inc[o.eng]
            elif o.kind in ("dma", "dma_mc"):
                q = _DMA_Q[o.eng]
                j = q_cnt[q]; q_cnt[q] += 1
                o.inc_no = (f"{q}{j % self.NQ}", 16 * (j // self.NQ + 1))
            elif o.kind == "coll":
                j = q_cnt["coll"]; q_cnt["coll"] += 1
                o.inc_no = (f"coll{j % self.NCQ}", j // self.NCQ + 1)
        vpos = {}
        _vp = 0
        for o in ops:
            if o.eng == "vector" and o.kind == "compute":
                vpos[o.i] = _vp
                _vp += 1
        wm = {e: {} for e in ENGS}
        last_drain = {e: -1 for e in ENGS}
        qmax = {e: {} for e in ENGS}
        for o in ops:
            if o.kind == "barrier":
                w = wm[o.eng]
                for s, v in sorted(qmax[o.eng].items()):
                    if w.get(s, 0) < v:
                        o.waits.append((s, v))
                        w[s] = v
                continue
            need = {}
            for pi in edges[o.i]:
                pr = ops[pi]
                if pr.kind == "compute":
                    if pr.eng == o.eng:
                        if pr.eng == "vector":
                            if (pr.i > last_drain[o.eng]
                                    and vpos.get(o.i, 0) - vpos.get(pr.i, 0) < 2):
                                o.drain_before = True
                        elif pr.eng != "tensor" and pr.i > last_drain[o.eng]:
                            o.drain_before = True
                    else:
                        need["e_" + pr.eng] = max(need.get("e_" + pr.eng, 0), pr.inc_no)
                elif pr.kind in ("dma", "dma_mc", "coll"):
                    s, v = pr.inc_no
                    need[s] = max(need.get(s, 0), v)
            if o.kind in ("dma", "dma_mc", "coll"):
                s, v = o.inc_no
                prev = v - (16 if o.kind != "coll" else 1)
                if prev > 0:
                    need[s] = max(need.get(s, 0), prev)
                qmax[o.eng][s] = v
            if o.drain_before:
                last_drain[o.eng] = o.i - 1
            w = wm[o.eng]
            for s, v in sorted(need.items()):
                if w.get(s, 0) < v:
                    o.waits.append((s, v))
                    w[s] = v
        per = {e: [o for o in ops if o.eng == e] for e in ENGS}
        flow = self

        def run(eng_obj, ename):
            bar_idx = 0
            for o in per[ename]:
                if o.kind == "barrier":
                    bar_idx += 1
                    for s, v in o.waits:
                        eng_obj.wait_ge(flow.sems[s], v)
                    if ename in ("vector", "scalar"):
                        eng_obj.drain()
                    if ename == "tensor":
                        eng_obj.nop()
                    eng_obj.sem_inc(flow.sems["bar"], 1)
                    eng_obj.wait_ge(flow.sems["bar"], bar_idx * len(ENGS))
                    continue
                for s, v in o.waits:
                    eng_obj.wait_ge(flow.sems[s], v)
                if o.drain_before:
                    eng_obj.drain()
                if o.kind == "dma_mc":
                    for arm in self.nc.sync.switch_core_id(n=128):
                        try:
                            core = arm.logical
                        except Exception:
                            core = int(arm) % 8
                        instr = o.emit(eng_obj, core)
                        instr.then_inc(flow.sems[o.inc_no[0]], 16)
                    continue
                instr = o.emit(eng_obj)
                if o.kind == "compute" and o.needs_inc:
                    instr.then_inc(flow.sems["e_" + ename], 1)
                elif o.kind == "dma":
                    instr.then_inc(flow.sems[o.inc_no[0]], 16)
                elif o.kind == "coll":
                    instr.then_inc(flow.sems[o.inc_no[0]], 1)

        @blk.tensor
        def _(t):
            run(t, "tensor")

        @blk.vector
        def _(v):
            run(v, "vector")

        @blk.scalar
        def _(s):
            run(s, "scalar")

        @blk.gpsimd
        def _(g):
            run(g, "gpsimd")

        @blk.sync
        def _(sy):
            run(sy, "sync")


def fm3(ap):
    return ap.rearrange("(kc p) t -> p kc t", p=128)


def build(seq=L, dbg=(), stop_after="p6", scan_steps=None, scan_groups=None):
    dbg = set(dbg)
    nto = seq // 4
    nch = seq // C
    ntb_o = nto // TB
    ntb_s = seq // TB
    nc = bass.Bass("TRN2", target_bir_lowering=False, debug=False)
    T = {}

    def din(name, shape, dt=F32):
        ap = nc.dram_tensor(name, shape, dt, kind="ExternalInput").ap()
        T[name] = ap
        return ap

    def dint(name, shape, dt=BF16):
        ap = nc.dram_tensor(name, shape, dt, kind="Internal").ap()
        T[name] = ap
        return ap

    din("x_own", [D, nto])
    din("wqT", [D, 256]); din("wkT", [D, 256]); din("wvT", [D, 256])
    din("qkv_b", [128, 2, 3])
    din("lrwT", [D, 4]); din("lr_b", [4, 1])
    din("ttt_g2", [128, 2]); din("ttt_b2", [128, 2])
    din("ttt_g_tm", [64, 256]); din("ttt_b_tm", [64, 256])
    din("W2_0", [128, 128]); din("W2_1", [128, 128]); din("b2_bc", [64, 256])
    din("post_nw", [128, 8]); din("post_nb", [128, 8])
    din("woT", [D, D]); din("wo_b", [128, 8])
    din("ln1_w", [128, 8]); din("ln2_w", [128, 8])
    din("kv_dT", [D, LAT]); din("q_dT", [D, LAT])
    din("k_uT", [LAT, 128]); din("q_uT", [LAT, 128]); din("v_uT", [LAT, 256])
    din("rope_qT", [LAT, 128]); din("rope_kT", [D, 128])
    din("o_wT", [D, D])
    din("cos_t", [128, seq]); din("sin_t", [128, seq])
    din("r_wT", [D, 8]); din("r_b", [1, 8])
    din("moe_gT", [NE, D, EI]); din("moe_uT", [NE, D, EI]); din("moe_dT", [NE, EI, D])
    din("ident_f", [128, 128])
    din("ident_b", [128, 128], BF16)
    din("triuT_b", [64, 64], BF16)
    din("i64z", [128, 64], BF16)
    din("triuTz", [128, 64], BF16)
    din("ones64", [64, 64], BF16)
    din("trimask4", [64, 256])
    din("e2_b", [128, 2], BF16)
    din("eb2_b", [2, 128], BF16)
    din("onesr_b", [1, 128], BF16)
    din("onesr_f", [1, 128])
    din("onescol_b", [128, 1], BF16)
    din("wmask", [128, 128])
    din("amask", [128, 4, TB], BF16)
    din("r4T", [128, 128])
    din("iota7", [128, 8])
    din("esel", [8, 1024], BF16)
    din("epsv", [64, 8])

    T["out"] = nc.dram_tensor("out", [D, nto], F32, kind="ExternalOutput").ap()

    dbg_out = {}

    def dout(name, shape, dt=F32):
        if name in dbg:
            dbg_out[name] = nc.dram_tensor("dbg_" + name, shape, dt,
                                           kind="ExternalOutput").ap()
        return dbg_out.get(name)
    T["dout"] = dout

    NCB_O = nto // TB
    NCB_T = seq // 2048 if seq >= 2048 else 1
    CW = seq // NCB_T
    for c in range(NCB_O):
        dint(f"h1_pub{c}", [D, TB]); dint(f"h1_gat{c}", [4 * D, TB])
        dint(f"a_pub{c}", [D, TB]); dint(f"a_gat{c}", [4 * D, TB])
    for c in range(NCB_T):
        dint(f"ymix_pub{c}", [256, CW]); dint(f"ymix_gat{c}", [1024, CW])
        dint(f"yat_pub{c}", [256, CW]); dint(f"yat_gat{c}", [1024, CW])
    T.update(NCB_O=NCB_O, NCB_T=NCB_T, CW=CW)
    dint("x2_spill", [D, nto], F32); dint("y0_spill", [D, nto], F32)
    dint("y1_spill", [D, nto], F32)
    dint("rot_sp", [256, seq])

    T.update(seq=seq, nto=nto, nch=nch, ntb_o=ntb_o, ntb_s=ntb_s)

    with ExitStack() as root:
        f = Flow(nc, root)
        T["f"] = f
        cst = {}
        for nm, shape, dt in [("ident_f", [128, 128], F32), ("ident_b", [128, 128], BF16),
                              ("triuT_b", [64, 64], BF16), ("i64z", [128, 64], BF16), ("triuTz", [128, 64], BF16), ("ones64", [64, 64], BF16),
                              ("trimask4", [64, 256], F32), ("e2_b", [128, 2], BF16),
                              ("eb2_b", [2, 128], BF16), ("onesr_b", [1, 128], BF16),
                              ("onesr_f", [1, 128], F32), ("onescol_b", [128, 1], BF16),
                              ("wmask", [128, 128], F32), ("qkv_b", [128, 2, 3], F32),
                              ("ttt_g2", [128, 2], F32), ("ttt_b2", [128, 2], F32),
                              ("ttt_g_tm", [64, 256], F32), ("ttt_b_tm", [64, 256], F32),
                              ("lr_b", [4, 1], F32), ("r4T", [128, 128], F32),
                              ("iota7", [128, 8], F32), ("r_b", [1, 8], F32), ("esel", [8, 1024], BF16),
                              ("epsv", [64, 8], F32), ("amask", [128, 4, TB], BF16),
                              ("post_nw", [128, 8], F32),
                              ("post_nb", [128, 8], F32), ("wo_b", [128, 8], F32),
                              ("ln1_w", [128, 8], F32), ("ln2_w", [128, 8], F32)]:
            cst[nm] = root.enter_context(nc.sbuf_tensor("c_" + nm, shape, dt))
            f.dma("sync", cst[nm][:], T[nm], writes=("c_" + nm,), tag="ldc:" + nm)
        T["cst"] = cst
        cst["lrbh"] = root.enter_context(nc.sbuf_tensor("c_lrbh", [4, 1], F32))
        f.op("scalar", lambda s: s.mul(cst["lrbh"][:], cst["lr_b"][:], 0.5),
             reads=("c_lr_b",), writes=("c_lrbh",), tag="lrbh")
        f.barrier("consts")

        with ExitStack() as ph0:
            emit_p0(f, nc, T, ph0)
        if stop_after != "p0":
            with ExitStack() as ph:
                T["XQb"] = ph.enter_context(nc.sbuf_tensor("XQb", [128, 2, seq], BF16))
                T["XKb"] = ph.enter_context(nc.sbuf_tensor("XKb", [128, 2, seq], BF16))
                T["tgt_fm"] = ph.enter_context(nc.sbuf_tensor("tgt_fm", [128, 2, seq], F32))
                T["etaT"] = ph.enter_context(nc.sbuf_tensor("etaT", [64, nch * 4], F32))
                with ExitStack() as ph2:
                    emit_p1a(f, nc, T, ph2)
                f.barrier("p1a")
                if stop_after != "p1a":
                    T["scan_steps"] = scan_steps
                    T["scan_groups"] = scan_groups
                    with ExitStack() as ph2:
                        emit_p1b(f, nc, T, ph2)
                    f.barrier("p1b")
        if stop_after not in ("p0", "p1a", "p1b"):
            with ExitStack() as ph:
                emit_p2(f, nc, T, ph)
            f.barrier("p2")
        if stop_after not in ("p0", "p1a", "p1b", "p2"):
            with ExitStack() as ph:
                emit_p3(f, nc, T, ph)
            f.barrier("p3")
        if stop_after not in ("p0", "p1a", "p1b", "p2", "p3"):
            with ExitStack() as ph:
                T["xm"] = ph.enter_context(nc.sbuf_tensor("xm", [128, 8, nto], BF16))
                with ExitStack() as ph2:
                    emit_p4(f, nc, T, ph2)
                f.barrier("p4")
                T["accum"] = ph.enter_context(nc.sbuf_tensor("accum", [128, 8, nto], F32))
                with ExitStack() as ph2:
                    emit_p5(f, nc, T, ph2)
                f.barrier("p5")
                with ExitStack() as ph2:
                    emit_p6(f, nc, T, ph2)
        f.barrier("final")
        f.finalize(blk=root.enter_context(nc.Block()))
    return nc, dbg_out


def emit_p0(f, nc, T, ctx):
    cst = T["cst"]; nto = T["nto"]; ntb_o = T["ntb_o"]
    dout = T["dout"]
    sb = lambda n, s, dt=F32: ctx.enter_context(nc.sbuf_tensor(n, s, dt))
    ps = lambda n, s, dt=F32: ctx.enter_context(nc.psum_tensor(n, s, dt))

    xf = sb("xf", [128, 8, nto])
    hb = sb("hb", [128, 8, nto], BF16)
    T["xf_p0"] = xf
    xb = sb("xb0", [128, 8, TB], BF16)
    sqb = sb("sqb0", [128, 8, TB], BF16)
    srow = sb("srow0", [1, 4 * TB])
    srow_b = sb("srowb0", [1, 2 * TB], BF16)
    scf = sb("scf0", [128, 8, TB])
    p_sum = ps("p_sum0", [1, TB])
    p_ss = ps("p_ss0", [1, TB])
    p_bc = ps("p_bc0", [128, 2 * TB])

    f.dma("sync", xf[:], fm3(T["x_own"]), writes=("xf",), tag="p0:ldx")

    for tb in range(ntb_o):
        sl = slice(tb * TB, (tb + 1) * TB)
        k = f"p0.{tb}"
        f.op("vector", lambda v, sl=sl: v.tensor_copy(xb[:], xf[:, :, sl]),
             reads=("xf",), writes=("xb",), tag=k + ":cast")
        f.op("scalar", lambda s: s.activation(sqb[:], xb[:], AF.Square),
             reads=("xb",), writes=("sqb",), tag=k + ":sq")

        def mm_sum(t):
            last = None
            for kc in range(8):
                last = t.matmul(p_sum[0:1, :], cst["onescol_b"][:], xb[:, kc, :],
                                start=(kc == 0), stop=(kc == 7))
            return last
        f.op("tensor", mm_sum, reads=("xb",), writes=("p_sum",), tag=k + ":mms")

        def mm_ss(t):
            last = None
            for kc in range(8):
                last = t.matmul(p_ss[0:1, :], cst["onescol_b"][:], sqb[:, kc, :],
                                start=(kc == 0), stop=(kc == 7))
            return last
        f.op("tensor", mm_ss, reads=("sqb",), writes=("p_ss",), tag=k + ":mmss")

        def stats(v):
            v.tensor_scalar(out=srow[0:1, 0:TB], in0=p_sum[0:1, :], scalar1=1.0 / D,
                            scalar2=None, op0=ALU.mult)
            v.tensor_scalar(out=srow[0:1, 3 * TB:4 * TB], in0=p_ss[0:1, :],
                            scalar1=1.0 / D, scalar2=None, op0=ALU.mult)
            v.drain()
            v.tensor_tensor(srow[0:1, TB:2 * TB], srow[0:1, 0:TB], srow[0:1, 0:TB], ALU.mult)
            v.drain()
            v.tensor_tensor(srow[0:1, TB:2 * TB], srow[0:1, 3 * TB:4 * TB],
                            srow[0:1, TB:2 * TB], ALU.subtract)
            v.drain()
            return v.nop()
        f.op("vector", stats, reads=("p_sum", "p_ss"), writes=("srow",), tag=k + ":st")

        def rstd(s):
            s.activation(srow[0:1, 2 * TB:3 * TB], srow[0:1, TB:2 * TB], AF.Ln,
                         bias=cst['epsv'][0:1, 0:1])
            s.drain()
            s.activation(srow[0:1, 2 * TB:3 * TB], srow[0:1, 2 * TB:3 * TB], AF.Exp, scale=-0.5)
            s.drain()
            return s.nop()
        f.op("scalar", rstd, reads=("srow",), writes=("srow.r",), tag=k + ":rstd")

        def tobf(v):
            v.tensor_copy(srow_b[0:1, 0:TB], srow[0:1, 0:TB])
            v.tensor_copy(srow_b[0:1, TB:2 * TB], srow[0:1, 2 * TB:3 * TB])
            v.drain()
            return v.nop()
        f.op("vector", tobf, reads=("srow", "srow.r"), writes=("srow_b",), tag=k + ":bf")

        def bcast(t):
            t.matmul(p_bc[:, 0:TB], cst["onesr_b"][:], srow_b[0:1, 0:TB], start=True, stop=True)
            return t.matmul(p_bc[:, TB:2 * TB], cst["onesr_b"][:], srow_b[0:1, TB:2 * TB],
                            start=True, stop=True)
        f.op("tensor", bcast, reads=("srow_b",), writes=("p_bc",), tag=k + ":bc")

        def norm(v, sl=sl):
            v.tensor_tensor(scf[:], xf[:, :, sl],
                            p_bc[:, 0:TB].unsqueeze(1).broadcast_to([128, 8, TB]), ALU.subtract)
            v.drain()
            v.tensor_tensor(hb[:, :, sl], scf[:],
                            p_bc[:, TB:2 * TB].unsqueeze(1).broadcast_to([128, 8, TB]), ALU.mult)
            v.drain()
            return v.nop()
        f.op("vector", norm, reads=("xf", "p_bc"), writes=("hb", "scf", "xb"),
             tag=k + ":norm")

    for tb in range(ntb_o):
        sl = slice(tb * TB, (tb + 1) * TB)
        f.dma("sync", fm3(T[f"h1_pub{tb}"]), hb[:, :, sl], reads=("hb",),
              writes=(f"h1_pub{tb}",), tag=f"p0:pub{tb}")

    d = dout("h1gat", [4 * D, nto], BF16)
    if d is not None:
        for c in range(T["NCB_O"]):
            f.coll("AllGather", ALU.bypass, [T[f"h1_pub{c}"].opt()],
                   [T[f"h1_gat{c}"].opt()], reads=(f"h1_pub{c}",),
                   writes=(f"h1_gat{c}",), tag=f"p0:ag{c}")
            f.dma("sync", d[:, c * TB:(c + 1) * TB], T[f"h1_gat{c}"],
                  reads=(f"h1_gat{c}",), tag=f"p0:dbg{c}")
        T["_h1_coll_done"] = True
    f.barrier("p0")


def emit_p1a(f, nc, T, ctx):
    cst = T["cst"]; seq = T["seq"]; nto = T["nto"]; ntb = T["ntb_s"]
    sb = lambda n, s, dt=F32: ctx.enter_context(nc.sbuf_tensor(n, s, dt))
    ps = lambda n, s, dt=F32: ctx.enter_context(nc.psum_tensor(n, s, dt))
    XQb, XKb, tgt_fm, etaT = T["XQb"], T["XKb"], T["tgt_fm"], T["etaT"]

    wq_sb = sb("wq_sb", [128, 8, 256], BF16)
    wk_sb = sb("wk_sb", [128, 8, 256], BF16)
    wv_sb = sb("wv_sb", [128, 8, 256], BF16)
    lrw_sb = sb("lrw_sb", [128, 8, 4], BF16)
    hg = [sb("hg0", [128, 8, TB], BF16), sb("hg1", [128, 8, TB], BF16)]
    praw = sb("praw", [128, 2, TB])
    xkf = sb("xkf", [128, 2, TB])
    sqb = sb("sqb1", [128, 2, TB], BF16)
    dvec = sb("dvec", [128, 2, TB])
    dvb = sb("dvb", [128, 2, TB], BF16)
    st = sb("st_p1", [2, 4 * TB])
    stb = sb("stb_p1", [2, 6 * TB], BF16)
    etrow = sb("etrow", [4, TB])
    p_q = ps("p_q", [128, TB])
    p_b1 = ps("p_b1", [128, 2 * TB])
    p_mix = ps("p_mix", [64, 4 * TB])
    p_eta = ps("p_eta", [4, TB])

    r3 = lambda ap: ap.rearrange("(kc p) m -> p kc m", p=128)
    f.dma("gpsimd", wq_sb[:], r3(T["wqT"]), writes=("wq_sb",), tag="p1:ldwq")
    f.dma("gpsimd", wk_sb[:], r3(T["wkT"]), writes=("wk_sb",), tag="p1:ldwk")
    f.dma("gpsimd", wv_sb[:], r3(T["wvT"]), writes=("wv_sb",), tag="p1:ldwv")
    f.dma("gpsimd", lrw_sb[:], r3(T["lrwT"]), writes=("lrw_sb",), tag="p1:ldlr")

    if not T.get("_h1_coll_done"):
        for c in range(T["NCB_O"]):
            f.coll("AllGather", ALU.bypass, [T[f"h1_pub{c}"].opt()],
                   [T[f"h1_gat{c}"].opt()], reads=(f"h1_pub{c}",),
                   writes=(f"h1_gat{c}",), tag=f"p1:ag{c}")

    RN_K = 0; MU_D = 2; SD_D = 4; RN_Q = 0

    def proj(t, wsb, mc, par, extra_sq_bias=None):
        last = None
        for kc in range(8):
            last = t.matmul(p_q[:], wsb[:, kc, 128 * mc:128 * mc + 128],
                            hg[par][:, kc, :], start=(kc == 0), stop=(kc == 7))
        return last

    for gtb in range(ntb):
        par = gtb % 2
        sl = slice(gtb * TB, (gtb + 1) * TB)
        k = f"p1.{gtb}"
        t0 = gtb * TB
        r = t0 // nto
        loc = t0 - r * nto
        ch = loc // TB
        src = T[f"h1_gat{ch}"][r * 1024:(r + 1) * 1024, :] \
            .rearrange("(kc p) t -> p kc t", p=128)
        f.dma("sync", hg[par][:], src, reads=(f"h1_gat{ch}",), writes=(f"hg{par}",),
              tag=k + ":ld")

        for mc in range(2):
            f.op("tensor", lambda t, mc=mc, par=par: proj(t, wk_sb, mc, par),
                 reads=(f"hg{par}", "wk_sb"), writes=("p_q",), tag=k + f":kmm{mc}")

            f.op("scalar", lambda s, mc=mc: s.activation(
                praw[:, mc, :], p_q[:], AF.Identity, bias=cst["qkv_b"][:, mc, 1:2]),
                reads=("p_q",), writes=(f"praw{mc}",), tag=k + f":kraw{mc}")
            f.op("vector", lambda v, mc=mc: v.tensor_tensor(
                sqb[:, mc, :], praw[:, mc, :], praw[:, mc, :], ALU.mult),
                reads=(f"praw{mc}", f"sqb{mc}"), writes=(f"sqb{mc}",),
                tag=k + f":krsq{mc}")
            f.op("tensor", lambda t, mc=mc: t.matmul(
                p_mix[0:2, mc * TB:mc * TB + TB], cst["e2_b"][:], sqb[:, mc, :],
                start=True, stop=True),
                reads=(f"sqb{mc}",), writes=(f"p_mix{mc}",), tag=k + f":kss{mc}")

        def k_lnexp(s):
            s.activation(st[0:2, RN_K * TB:(RN_K + 2) * TB], p_mix[0:2, 0:2 * TB],
                         AF.Ln, bias=cst["epsv"][0:2, 2:3])
            s.drain()
            s.activation(stb[0:2, RN_K * TB:(RN_K + 2) * TB],
                         st[0:2, RN_K * TB:(RN_K + 2) * TB], AF.Exp, scale=-0.5)
            s.drain()
            return s.nop()
        f.op("scalar", k_lnexp, reads=("p_mix0", "p_mix1", "stb_k"),
             writes=("st_k", "stb_k"), tag=k + ":klnexp")

        f.op("tensor", lambda t: (
            t.matmul(p_b1[:, 0:TB], cst["eb2_b"][:], stb[0:2, RN_K * TB:(RN_K + 1) * TB],
                     start=True, stop=True),
            t.matmul(p_b1[:, TB:2 * TB], cst["eb2_b"][:],
                     stb[0:2, (RN_K + 1) * TB:(RN_K + 2) * TB], start=True, stop=True))[-1],
            reads=("stb_k",), writes=("p_b1",), tag=k + ":kbc")

        def k_norm(v, sl=sl):
            for mc in range(2):
                v.tensor_tensor(xkf[:, mc, :], praw[:, mc, :],
                                p_b1[:, mc * TB:mc * TB + TB], ALU.mult)
            v.drain()
            v.tensor_copy(XKb[:, :, sl], xkf[:])
            v.drain()
            return v.nop()
        f.op("vector", k_norm, reads=("praw0", "praw1", "p_b1"),
             writes=("xkf", "XKb"), tag=k + ":knorm")

        for mc in range(2):
            f.op("tensor", lambda t, mc=mc, par=par: proj(t, wv_sb, mc, par),
                 reads=(f"hg{par}", "wv_sb", f"praw{mc}"), writes=("p_q",),
                 tag=k + f":vmm{mc}")
            f.op("scalar", lambda s, mc=mc: s.activation(
                praw[:, mc, :], p_q[:], AF.Identity, bias=cst["qkv_b"][:, mc, 2:3]),
                reads=("p_q",), writes=(f"praw{mc}",), tag=k + f":vraw{mc}")

        def dcalc(v):
            for mc in range(2):
                v.tensor_tensor(dvec[:, mc, :], praw[:, mc, :], xkf[:, mc, :], ALU.subtract)
            v.drain()
            v.tensor_copy(dvb[:], dvec[:])
            v.drain()
            return v.nop()
        f.op("vector", dcalc, reads=("praw0", "praw1", "xkf"),
             writes=("dvec", "dvb"), tag=k + ":dcalc")

        f.op("scalar", lambda s: s.activation(sqb[:], dvec[:], AF.Square),
             reads=("dvec",), writes=("sqb0", "sqb1"), tag=k + ":dsq")

        def dsums(t):
            last = None
            for mc in range(2):
                t.matmul(p_mix[0:2, mc * TB:mc * TB + TB], cst["e2_b"][:],
                         dvb[:, mc, :], start=True, stop=True)
            for mc in range(2):
                last = t.matmul(p_mix[0:2, (2 + mc) * TB:(2 + mc) * TB + TB],
                                cst["e2_b"][:], sqb[:, mc, :], start=True, stop=True)
            return last
        f.op("tensor", dsums, reads=("dvb", "sqb0", "sqb1"),
             writes=("p_mix0", "p_mix1", "p_mix2", "p_mix3"), tag=k + ":dsums")

        def dstat(v):
            v.tensor_scalar(out=st[0:2, MU_D * TB:(MU_D + 2) * TB],
                            in0=p_mix[0:2, 0:2 * TB], scalar1=1.0 / HD,
                            scalar2=None, op0=ALU.mult)
            v.drain()
            v.tensor_tensor(st[0:2, 0:2 * TB], p_mix[0:2, 0:2 * TB],
                            st[0:2, MU_D * TB:(MU_D + 2) * TB], ALU.mult)
            v.drain()
            v.tensor_tensor(st[0:2, 0:2 * TB], p_mix[0:2, 2 * TB:4 * TB],
                            st[0:2, 0:2 * TB], ALU.subtract)
            v.drain()
            v.tensor_scalar(out=st[0:2, 0:2 * TB], in0=st[0:2, 0:2 * TB],
                            scalar1=1.0 / (HD - 1), scalar2=None, op0=ALU.mult)
            v.drain()
            v.tensor_copy(stb[0:2, MU_D * TB:(MU_D + 2) * TB],
                          st[0:2, MU_D * TB:(MU_D + 2) * TB])
            v.drain()
            return v.nop()
        f.op("vector", dstat, reads=("p_mix0", "p_mix1", "p_mix2", "p_mix3", "st_k"),
             writes=("st_mu", "stb_mu", "st_sd0"), tag=k + ":dstat")

        def dsd(s):
            rng = slice(0, 2 * TB)
            s.activation(st[0:2, rng], st[0:2, rng], AF.Ln)
            s.drain()
            s.activation(st[0:2, rng], st[0:2, rng], AF.Exp, scale=0.5)
            s.drain()
            return s.nop()
        f.op("scalar", dsd, reads=("st_mu", "st_sd0"), writes=("st_sd",), tag=k + ":dsd")

        def dr2s(s):
            s.activation(st[0:2, 0:2 * TB], st[0:2, 0:2 * TB], AF.Ln,
                         bias=cst["epsv"][0:2, 0:1])
            s.drain()
            s.activation(stb[0:2, SD_D * TB:(SD_D + 2) * TB], st[0:2, 0:2 * TB],
                         AF.Exp, scale=-1.0)
            s.drain()
            return s.nop()
        f.op("scalar", dr2s, reads=("st_sd", "stb_r2"), writes=("st_sd", "stb_r2"),
             tag=k + ":dr2s")

        f.op("tensor", lambda t: (
            t.matmul(p_b1[:, 0:TB], cst["eb2_b"][:], stb[0:2, MU_D * TB:(MU_D + 1) * TB],
                     start=True, stop=True),
            t.matmul(p_b1[:, TB:2 * TB], cst["eb2_b"][:],
                     stb[0:2, (MU_D + 1) * TB:(MU_D + 2) * TB], start=True, stop=True))[-1],
            reads=("stb_mu", "p_b1"), writes=("p_b1",), tag=k + ":mubc")

        def dsub(v):
            for mc in range(2):
                v.tensor_tensor(dvec[:, mc, :], dvec[:, mc, :],
                                p_b1[:, mc * TB:mc * TB + TB], ALU.subtract)
            v.drain()
            return v.nop()
        f.op("vector", dsub, reads=("dvec", "p_b1"), writes=("dvec",), tag=k + ":dsub")

        f.op("tensor", lambda t: (
            t.matmul(p_b1[:, 0:TB], cst["eb2_b"][:], stb[0:2, SD_D * TB:(SD_D + 1) * TB],
                     start=True, stop=True),
            t.matmul(p_b1[:, TB:2 * TB], cst["eb2_b"][:],
                     stb[0:2, (SD_D + 1) * TB:(SD_D + 2) * TB], start=True, stop=True))[-1],
            reads=("stb_r2", "dvec", "p_b1"), writes=("p_b1",), tag=k + ":r2bc")

        def dmul(v, sl=sl):
            for mc in range(2):
                v.tensor_tensor(tgt_fm[:, mc, sl], dvec[:, mc, :],
                                p_b1[:, mc * TB:mc * TB + TB], ALU.mult)
            v.drain()
            return v.nop()
        f.op("vector", dmul, reads=("dvec", "p_b1"), writes=("tgt_fm", "dvec"),
             tag=k + ":dmul")

        for mc in range(2):
            f.op("tensor", lambda t, mc=mc, par=par: proj(t, wq_sb, mc, par),
                 reads=(f"hg{par}", "wq_sb", f"praw{mc}", "dvec"), writes=("p_q",),
                 tag=k + f":qmm{mc}")

            f.op("scalar", lambda s, mc=mc: s.activation(
                praw[:, mc, :], p_q[:], AF.Identity, bias=cst["qkv_b"][:, mc, 0:1]),
                reads=("p_q",), writes=(f"praw{mc}",), tag=k + f":qraw{mc}")
            f.op("vector", lambda v, mc=mc: v.tensor_tensor(
                sqb[:, mc, :], praw[:, mc, :], praw[:, mc, :], ALU.mult),
                reads=(f"praw{mc}", f"sqb{mc}"), writes=(f"sqb{mc}",),
                tag=k + f":qrsq{mc}")
            f.op("tensor", lambda t, mc=mc: t.matmul(
                p_mix[0:2, mc * TB:mc * TB + TB], cst["e2_b"][:], sqb[:, mc, :],
                start=True, stop=True),
                reads=(f"sqb{mc}", "st_mu"), writes=(f"p_mix{mc}",), tag=k + f":qss{mc}")

        def q_lnexp(s):
            s.activation(st[0:2, RN_Q * TB:(RN_Q + 2) * TB], p_mix[0:2, 0:2 * TB],
                         AF.Ln, bias=cst["epsv"][0:2, 2:3])
            s.drain()
            s.activation(stb[0:2, RN_Q * TB:(RN_Q + 2) * TB],
                         st[0:2, RN_Q * TB:(RN_Q + 2) * TB], AF.Exp, scale=-0.5)
            s.drain()
            return s.nop()
        f.op("scalar", q_lnexp, reads=("p_mix0", "p_mix1", "stb_q", "st_sd"),
             writes=("st_q", "stb_q"), tag=k + ":qlnexp")

        f.op("tensor", lambda t: (
            t.matmul(p_b1[:, 0:TB], cst["eb2_b"][:], stb[0:2, RN_Q * TB:(RN_Q + 1) * TB],
                     start=True, stop=True),
            t.matmul(p_b1[:, TB:2 * TB], cst["eb2_b"][:],
                     stb[0:2, (RN_Q + 1) * TB:(RN_Q + 2) * TB], start=True, stop=True))[-1],
            reads=("stb_q", "dvec", "p_b1"), writes=("p_b1",), tag=k + ":qbc")

        def q_norm(v, sl=sl):
            for mc in range(2):
                v.tensor_tensor(XQb[:, mc, sl], praw[:, mc, :],
                                p_b1[:, mc * TB:mc * TB + TB], ALU.mult)
            v.drain()
            return v.nop()
        f.op("vector", q_norm, reads=("praw0", "praw1", "p_b1"),
             writes=("XQb",), tag=k + ":qnorm")

        def etamm(t, par=par):
            last = None
            for kc in range(8):
                last = t.matmul(p_eta[:], lrw_sb[:, kc, :], hg[par][:, kc, :],
                                start=(kc == 0), stop=(kc == 7))
            return last
        f.op("tensor", etamm, reads=(f"hg{par}", "lrw_sb"), writes=("p_eta",),
             tag=k + ":etamm")
        f.op("scalar", lambda s: s.activation(etrow[:], p_eta[:], AF.Tanh, scale=0.5,
                                              bias=cst["lrbh"][:]),
             reads=("p_eta", "c_lrbh"), writes=("etrow",), tag=k + ":etatanh")
        f.op("vector", lambda v: v.tensor_scalar(
            out=etrow[:], in0=etrow[:], scalar1=-1.0 / 8192.0, scalar2=-1.0 / 8192.0,
            op0=ALU.mult, op1=ALU.add),
            reads=("etrow",), writes=("etrow",), tag=k + ":etasc")

        def etat(t):
            last = None
            for ch in range(8):
                last = t.transpose(p_mix[0:64, ch * 4:ch * 4 + 4],
                                   etrow[:, ch * 64:ch * 64 + 64], cst["ident_f"][0:4, 0:4])
            return last
        f.op("tensor", etat, reads=("etrow", "p_mix0"), writes=("p_mix0",),
             tag=k + ":etat")
        f.op("vector", lambda v, gtb=gtb: v.tensor_copy(
            etaT[:, gtb * 32:gtb * 32 + 32], p_mix[0:64, 0:32]),
            reads=("p_mix0",), writes=("etaT", f"hg{par}"), tag=k + ":etaev")

    d = T["dout"]("xqb", [128, 2, seq], BF16)
    if d is not None:
        f.dma("sync", d, XQb[:], reads=("XQb",), tag="dbg:xqb")
    d = T["dout"]("xkb", [128, 2, seq], BF16)
    if d is not None:
        f.dma("sync", d, XKb[:], reads=("XKb",), tag="dbg:xkb")
    d = T["dout"]("tgtfm", [128, 2, seq])
    if d is not None:
        f.dma("sync", d, tgt_fm[:], reads=("tgt_fm",), tag="dbg:tgt")
    d = T["dout"]("etat", [64, T["nch"] * 4])
    if d is not None:
        f.dma("sync", d, etaT[:], reads=("etaT",), tag="dbg:eta")


def emit_p1b(f, nc, T, ctx):
    cst = T["cst"]; seq = T["seq"]; nch = T["nch"]
    if T.get("scan_steps"):
        nch = min(nch, T["scan_steps"])
    sb = lambda n, s, dt=F32: ctx.enter_context(nc.sbuf_tensor(n, s, dt))
    ps = lambda n, s, dt=F32: ctx.enter_context(nc.psum_tensor(n, s, dt))
    XQb, XKb, tgt_fm, etaT = T["XQb"], T["XKb"], T["tgt_fm"], T["etaT"]

    W2 = sb("W2", [128, 256]); W2b = sb("W2b", [128, 256], BF16)
    b2 = sb("b2", [64, 256]); b2b = sb("b2b", [128, 256], BF16)
    gneg = sb("gneg", [128, 256], BF16)
    at_m = sb("at_m", [128, 256], BF16)
    xk_tm = sb("xk_tm", [64, 256], BF16)
    sq_tm = sb("sq_tm", [64, 256])
    xh = sb("xh", [64, 256])
    t1 = sb("t1s", [64, 256])
    t2 = sb("t2s", [64, 256])
    o_ln = sb("o_ln", [64, 256])
    wsc = sb("wsc", [128, 256])
    stt = sb("stt", [64, 32])
    XO16 = sb("XO16", [128, 2, 1024], BF16)
    z1f = sb("z1f", [64, 256])
    zbf = sb("zbf", [64, 256])
    p_tm = ps("p_tm", [64, 256], BF16)
    psA = ps("psA", [64, 384])
    psB = ps("psB", [64, 384])
    p_atA = ps("p_atA", [64, 128])
    p_atB = ps("p_atB", [64, 128])
    p_wd = ps("p_wd", [128, 256])
    p_ot = ps("p_ot", [128, 384])
    p_z = psA
    p_tg = psB

    f.dma("sync", W2[:, 0:128], T["W2_0"], writes=("W2:0",), tag="sc:ldW0")
    f.dma("sync", W2[:, 128:256], T["W2_1"], writes=("W2:1",), tag="sc:ldW1")
    f.dma("sync", b2[:], T["b2_bc"], writes=("b2",), tag="sc:ldb")

    g2_tm = sb("g2_tm", [64, 256])

    def initcast(v):
        v.tensor_tensor(g2_tm[:], cst["ttt_g_tm"][:], cst["ttt_g_tm"][:], ALU.mult)
        v.drain()
        v.tensor_copy(W2b[:], W2[:])
        v.memset(b2b[64:128, :], 0.0)
        v.memset(gneg[64:128, :], 0.0)
        v.memset(at_m[64:128, :], 0.0)
        v.drain()
        v.tensor_copy(b2b[0:64, :], b2[:])
        v.drain()
        return v.nop()
    f.op("vector", initcast, reads=("W2:0", "W2:1", "b2"),
         writes=("W2b", "b2b", "gneg", "at_m"), tag="sc:initcast")

    z3 = lambda ap: ap.rearrange("p (h e) -> p h e", e=64)
    bc3 = lambda col: stt[:, col:col + 4].unsqueeze(2).broadcast_to([64, 4, 64])
    S_SUM, S_SS, S_MU, S_RSTD, S_S1, S_S2, S_SCL, S_TMP = 0, 4, 8, 12, 16, 20, 24, 28

    GG = T.get("scan_groups")
    realop = f.op
    realdma = f.dma
    import re as _re
    def gate(tag):
        m = _re.search(r":([a-z0-9]+)$", tag)
        grp = m.group(1) if m else tag
        return GG is None or grp in GG
    def gop(eng, emit, reads=(), writes=(), kind="compute", tag=""):
        if gate(tag):
            return realop(eng, emit, reads, writes, kind=kind, tag=tag)
    fop_save = f.op

    class _G:
        def op(self, *a, **k):
            if gate(k.get("tag", a[-1] if a else "")):
                return realop(*a, **k)
            return None
        def dma(self, *a, **k):
            if gate(k.get("tag", "")):
                return realdma(*a, **k)
            return None
        def __getattr__(self, n):
            return getattr(f, n)
    fg = _G()

    stt2 = sb("stt2", [64, 32])
    sqo = sb("sqo", [64, 256])
    zbf2 = [zbf, sb("zbfB", [64, 256])]
    bc32 = lambda col: stt2[:, col:col + 4].unsqueeze(2).broadcast_to([64, 4, 64])

    def zrstd2(s):
        s.activation(stt2[:, S_RSTD:S_RSTD + 4], stt2[:, S_TMP:S_TMP + 4], AF.Ln,
                     bias=cst["epsv"][:, 1:2])
        s.drain()
        s.activation(stt2[:, S_RSTD:S_RSTD + 4], stt2[:, S_RSTD:S_RSTD + 4],
                     AF.Exp, scale=-0.5)
        s.drain()
        return s.nop()

    def mk_tail(n):
        k = f"sc{n}"
        sl = slice(n * C, (n + 1) * C)
        blkidx = n % 16
        zp = n % 2
        zbb = zbf2[zp]
        ZR = f"zbf{zp}"
        th = {}

        th["osq"] = lambda: fg.op("scalar", lambda s: s.activation(
            sqo[:], zbb[:], AF.Square), reads=(ZR, "sqo"), writes=("sqo",),
            tag=k + ":osq")
        th["os1"] = lambda: fg.op("vector", lambda v: v.tensor_reduce(
            stt2[:, S_SUM:S_SUM + 4], z3(zbb[:]), axis=AX.X, op=ALU.add),
            reads=(ZR, "stt2"), writes=("stt2",), tag=k + ":os1")
        th["os2"] = lambda: fg.op("vector", lambda v: v.tensor_reduce(
            stt2[:, S_SS:S_SS + 4], z3(sqo[:]), axis=AX.X, op=ALU.add),
            reads=("sqo", "stt2"), writes=("stt2",), tag=k + ":os2")
        th["os3"] = lambda: fg.op("vector", lambda v: v.tensor_scalar(
            out=stt2[:, S_MU:S_MU + 8], in0=stt2[:, S_SUM:S_SUM + 8],
            scalar1=1.0 / HD, scalar2=None, op0=ALU.mult),
            reads=("stt2",), writes=("stt2",), tag=k + ":os3")
        th["os4"] = lambda: fg.op("vector", lambda v: v.tensor_tensor(
            stt2[:, S_S1:S_S1 + 4], stt2[:, S_MU:S_MU + 4],
            stt2[:, S_MU:S_MU + 4], ALU.mult),
            reads=("stt2",), writes=("stt2",), tag=k + ":os4")
        th["os5"] = lambda: fg.op("vector", lambda v: v.tensor_tensor(
            stt2[:, S_TMP:S_TMP + 4], stt2[:, S_MU + 4:S_MU + 8],
            stt2[:, S_S1:S_S1 + 4], ALU.subtract),
            reads=("stt2",), writes=("stt2",), tag=k + ":os5")
        th["orstd"] = lambda: fg.op("scalar", zrstd2, reads=("stt2",),
                                    writes=("stt2",), tag=k + ":orstd")
        th["on1"] = lambda: fg.op("vector", lambda v: v.tensor_tensor(
            z3(o_ln[:]), z3(zbb[:]), bc32(S_MU), ALU.subtract),
            reads=(ZR, "stt2", "o_ln"), writes=("o_ln",), tag=k + ":on1")
        th["on2"] = lambda: fg.op("vector", lambda v: v.tensor_tensor(
            z3(o_ln[:]), z3(o_ln[:]), bc32(S_RSTD), ALU.mult),
            reads=("stt2", "o_ln"), writes=("o_ln",), tag=k + ":on2")
        th["on3"] = lambda: fg.op("vector", lambda v: v.tensor_tensor(
            o_ln[:], o_ln[:], cst["ttt_g_tm"][:], ALU.mult),
            reads=("o_ln",), writes=("o_ln",), tag=k + ":on3")
        th["on4"] = lambda: fg.op("vector", lambda v: v.tensor_tensor(
            o_ln[:], o_ln[:], cst["ttt_b_tm"][:], ALU.add),
            reads=("o_ln",), writes=("o_ln",), tag=k + ":on4")

        def _tout():
            def t_out(t):
                t.transpose(p_ot[:, 0:64], o_ln[0:64, 0:128],
                            cst["ident_f"][0:64, 0:64])
                return t.transpose(p_ot[:, 64:128], o_ln[0:64, 128:256],
                                   cst["ident_f"][0:64, 0:64])
            fg.op("tensor", t_out, reads=("o_ln", "p_ot"), writes=("p_ot",),
                 tag=k + ":tout")
        th["tout"] = _tout
        th["oadd"] = lambda: fg.op("vector", lambda v, sl=sl, blkidx=blkidx:
            v.tensor_tensor(
                XO16[:, :, 64 * blkidx:64 * blkidx + 64],
                p_ot[:, 0:128].rearrange("p (a b) -> p a b", b=64),
                XQb[:, :, sl], ALU.add),
            reads=("p_ot", "XQb"), writes=(f"XO16:{blkidx}",), tag=k + ":oadd")

        def _pub():
            if blkidx == 15 or n == nch - 1:
                nblk = n // 16
                lo = 1024 * nblk
                w = 64 * (blkidx + 1)
                CW = T["CW"]
                cc = lo // CW
                off = lo - cc * CW
                fg.dma("sync", T[f"ymix_pub{cc}"].rearrange("(mc p) t -> p mc t", p=128)
                      [:, :, off:off + w], XO16[:, :, 0:w],
                      reads=tuple(f"XO16:{i}" for i in range(blkidx + 1)),
                      writes=(f"ymix_pub{cc}",), tag=k + ":pub")
        th["pub"] = _pub
        return th

    TAIL_ORDER = ("osq", "os1", "os2", "os3", "os4", "os5", "orstd",
                  "on1", "on2", "on3", "on4", "tout", "oadd", "pub")

    prev = None
    P = (lambda nm: prev[nm]() if prev is not None else None)
    for n in range(nch):
        k = f"sc{n}"
        sl = slice(n * C, (n + 1) * C)
        zp = n % 2
        zbb = zbf2[zp]
        ZW = f"zbf{zp}"

        def t_xk(t, sl=sl):
            t.transpose(p_tm[0:64, 0:128], XKb[:, 0, sl], cst["ident_b"][:])
            return t.transpose(p_tm[0:64, 128:256], XKb[:, 1, sl], cst["ident_b"][:])
        fg.op("tensor", t_xk, reads=("XKb",), writes=("p_tm",), tag=k + ":txk")

        def t_tg(t, sl=sl):
            t.transpose(p_tg[0:64, 0:128], tgt_fm[:, 0, sl], cst["ident_f"][:])
            return t.transpose(p_tg[0:64, 128:256], tgt_fm[:, 1, sl], cst["ident_f"][:])
        fg.op("tensor", t_tg, reads=("tgt_fm", "psB"), writes=("psB",), tag=k + ":ttg")

        def atmm(t, sl=sl):
            last = None
            for h in range(4):
                p, h2 = h // 2, h % 2
                dst = p_atA if h2 == 0 else p_atB
                last = t.matmul(dst[0:64, 64 * p:64 * p + 64],
                                XKb[64 * h2:64 * h2 + 64, p, sl],
                                XQb[64 * h2:64 * h2 + 64, p, sl], start=True, stop=True)
            return last
        fg.op("tensor", atmm, reads=("XKb", "XQb"), writes=("p_at",), tag=k + ":at")

        fg.op("vector", lambda v: v.tensor_copy(xk_tm[:], p_tm[0:64, :]),
             reads=("p_tm",), writes=("xk_tm",), tag=k + ":evxk")
        fg.op("vector", lambda v: v.tensor_tensor(at_m[0:64, 0:128], p_atA[0:64, :],
                                                  cst["trimask4"][:, 0:128], ALU.mult),
             reads=("p_at", "at_m"), writes=("at_m",), tag=k + ":evat")
        fg.op("vector", lambda v: v.tensor_tensor(at_m[0:64, 128:256], p_atB[0:64, :],
                                                  cst["trimask4"][:, 0:128], ALU.mult),
             reads=("p_at", "at_m"), writes=("at_m",), tag=k + ":evat2")

        P("osq")
        P("os1")
        P("os2")

        def z1mm(t, sl=sl):
            last = None
            for p in range(2):
                last = t.matmul(p_z[0:64, 128 * p:128 * p + 128], XKb[:, p, sl],
                                W2b[:, 128 * p:128 * p + 128], start=True, stop=True)
            return last
        fg.op("tensor", z1mm, reads=("XKb", "W2b", "psA"), writes=("psA",), tag=k + ":z1")
        fg.op("vector", lambda v: v.tensor_tensor(z1f[:], p_z[0:64, 0:256], b2[:], ALU.add),
             reads=("psA", "b2"), writes=("z1f",), tag=k + ":z1f")
        P("os3")
        fg.op("vector", lambda v: v.tensor_tensor(sq_tm[:], z1f[:], z1f[:], ALU.mult),
             reads=("z1f", "sq_tm"), writes=("sq_tm",), tag=k + ":zsq")

        fg.op("vector", lambda v: v.tensor_reduce(
            stt[:, S_SUM:S_SUM + 4], z3(z1f[:]), axis=AX.X, op=ALU.add),
            reads=("z1f", "stt"), writes=("stt",), tag=k + ":zst1")
        P("os4")
        fg.op("vector", lambda v: v.tensor_reduce(
            stt[:, S_SS:S_SS + 4], z3(sq_tm[:]), axis=AX.X, op=ALU.add),
            reads=("sq_tm", "stt"), writes=("stt",), tag=k + ":zst2")
        P("os5")
        P("orstd")
        fg.op("vector", lambda v: v.tensor_scalar(
            out=stt[:, S_MU:S_MU + 8], in0=stt[:, S_SUM:S_SUM + 8],
            scalar1=1.0 / HD, scalar2=None, op0=ALU.mult),
            reads=("stt",), writes=("stt",), tag=k + ":zst3")
        P("on1")
        fg.op("vector", lambda v: v.tensor_tensor(
            stt[:, S_S1:S_S1 + 4], stt[:, S_MU:S_MU + 4], stt[:, S_MU:S_MU + 4],
            ALU.mult),
            reads=("stt",), writes=("stt",), tag=k + ":zst4")
        P("on2")
        fg.op("vector", lambda v: v.tensor_tensor(
            stt[:, S_TMP:S_TMP + 4], stt[:, S_MU + 4:S_MU + 8],
            stt[:, S_S1:S_S1 + 4], ALU.subtract),
            reads=("stt",), writes=("stt",), tag=k + ":zst5")
        P("on3")

        def zrstd(s):
            s.activation(stt[:, S_RSTD:S_RSTD + 4], stt[:, S_TMP:S_TMP + 4], AF.Ln,
                         bias=cst["epsv"][:, 1:2])
            s.drain()
            s.activation(stt[:, S_RSTD:S_RSTD + 4], stt[:, S_RSTD:S_RSTD + 4],
                         AF.Exp, scale=-0.5)
            s.drain()
            return s.nop()
        fg.op("scalar", zrstd, reads=("stt",), writes=("stt",), tag=k + ":zrstd")

        fg.op("vector", lambda v: v.tensor_tensor(
            z3(xh[:]), z3(z1f[:]), bc3(S_MU), ALU.subtract),
            reads=("z1f", "stt", "xh"), writes=("xh",), tag=k + ":xh1")
        P("on4")
        P("tout")
        fg.op("vector", lambda v: v.tensor_tensor(
            z3(xh[:]), z3(xh[:]), bc3(S_RSTD), ALU.mult),
            reads=("stt", "xh"), writes=("xh",), tag=k + ":xh2")
        fg.op("vector", lambda v, n=n: v.scalar_tensor_tensor(
            out=stt[:, S_SCL:S_SCL + 4], in0=stt[:, S_RSTD:S_RSTD + 4],
            scalar=1.0 / HD, in1=etaT[:, 4 * n:4 * n + 4],
            op0=ALU.mult, op1=ALU.mult),
            reads=("stt", "etaT"), writes=("stt",), tag=k + ":scl1")
        fg.op("vector", lambda v: v.tensor_tensor(
            t1[:], xh[:], p_tg[0:64, 0:256], ALU.subtract),
            reads=("xh", "psB", "t1"), writes=("t1",), tag=k + ":go1")
        P("oadd")
        fg.op("vector", lambda v: v.tensor_tensor(t1[:], t1[:], g2_tm[:], ALU.mult),
             reads=("t1",), writes=("t1",), tag=k + ":go2")
        fg.op("vector", lambda v: v.tensor_tensor(t2[:], t1[:], xh[:], ALU.mult),
             reads=("t1", "xh", "t2"), writes=("t2",), tag=k + ":gs2")
        fg.op("vector", lambda v: v.tensor_reduce(
            stt[:, S_S1:S_S1 + 4], z3(t1[:]), axis=AX.X, op=ALU.add),
            reads=("t1", "stt"), writes=("stt",), tag=k + ":gs1")
        fg.op("vector", lambda v: v.tensor_reduce(
            stt[:, S_S2:S_S2 + 4], z3(t2[:]), axis=AX.X, op=ALU.add),
            reads=("t2", "stt"), writes=("stt",), tag=k + ":gs3")
        fg.op("vector", lambda v: v.scalar_tensor_tensor(
            out=z3(t2[:]), in0=z3(t1[:]), scalar=float(HD), in1=bc3(S_S1),
            op0=ALU.mult, op1=ALU.subtract),
            reads=("t1", "stt", "t2"), writes=("t2",), tag=k + ":gn1")
        fg.op("vector", lambda v: v.tensor_tensor(
            z3(xh[:]), z3(xh[:]), bc3(S_S2), ALU.mult),
            reads=("stt", "xh"), writes=("xh",), tag=k + ":gn2")
        fg.op("vector", lambda v: v.tensor_tensor(t2[:], t2[:], xh[:], ALU.subtract),
             reads=("t2", "xh"), writes=("t2",), tag=k + ":gn3")
        fg.op("vector", lambda v: v.tensor_tensor(
            z3(gneg[0:64, :]), z3(t2[:]), bc3(S_SCL), ALU.mult),
            reads=("t2", "stt", "gneg"), writes=("gneg",), tag=k + ":gn4")

        def zbmm(t, sl=sl):
            last = None
            for p in range(2):
                dst = psB if p == 0 else psA
                t.matmul(dst[0:64, 256:384], cst["triuTz"][:],
                         gneg[:, 128 * p:128 * p + 128], start=True, stop=False)
                t.matmul(dst[0:64, 256:384], XQb[:, p, sl],
                         W2b[:, 128 * p:128 * p + 128], start=False, stop=True)
                for h2 in range(2):
                    hh = 2 * p + h2
                    atoff = {0: 0, 2: 64, 1: 128, 3: 192}[hh]
                    last = t.matmul(dst[0:64, 256 + 64 * h2:256 + 64 * h2 + 64],
                                    at_m[:, atoff:atoff + 64],
                                    gneg[:, 64 * hh:64 * hh + 64],
                                    start=False, stop=True, skip_group_check=True)
            return last
        fg.op("tensor", zbmm, reads=("gneg", "XQb", "W2b", "at_m", "psA", "psB", "z1f"),
             writes=("psA", "psB"), tag=k + ":zb")

        fg.op("vector", lambda v, zbb=zbb: v.tensor_tensor(
            zbb[:, 0:128], psB[0:64, 256:384], b2[:, 0:128], ALU.add),
            reads=("psA", "psB", "b2", ZW), writes=(ZW,), tag=k + ":zbf")
        fg.op("vector", lambda v, zbb=zbb: v.tensor_tensor(
            zbb[:, 128:256], psA[0:64, 256:384], b2[:, 128:256], ALU.add),
            reads=("psA", "psB", "b2", ZW), writes=(ZW,), tag=k + ":zbf2")
        P("pub")

        def bnew(t):
            t.matmul(p_ot[0:64, 128:256], cst["ones64"][:], gneg[0:64, 0:128],
                     start=True, stop=True)
            return t.matmul(p_ot[0:64, 256:384], cst["ones64"][:],
                            gneg[0:64, 128:256], start=True, stop=True)
        fg.op("tensor", bnew, reads=("gneg", "p_ot"), writes=("p_bn",), tag=k + ":bn")

        def wdmm(t):
            t.matmul(p_wd[:, 0:128], xk_tm[0:64, 0:128], gneg[0:64, 0:128],
                     start=True, stop=True)
            return t.matmul(p_wd[:, 128:256], xk_tm[0:64, 128:256],
                            gneg[0:64, 128:256], start=True, stop=True)
        fg.op("tensor", wdmm, reads=("xk_tm", "gneg"), writes=("p_wd",), tag=k + ":wd")

        fg.op("vector", lambda v: v.tensor_tensor(
            wsc[:, 0:128], p_wd[:, 0:128], cst["wmask"][:], ALU.mult),
            reads=("p_wd", "wsc"), writes=("wsc",), tag=k + ":wu1")
        fg.op("vector", lambda v: v.tensor_tensor(
            wsc[:, 128:256], p_wd[:, 128:256], cst["wmask"][:], ALU.mult),
            reads=("p_wd", "wsc"), writes=("wsc",), tag=k + ":wu2")
        fg.op("vector", lambda v: v.tensor_tensor(
            b2[:], b2[:], p_ot[0:64, 128:384], ALU.add),
            reads=("p_bn", "b2"), writes=("b2",), tag=k + ":wu5")
        fg.op("vector", lambda v: v.tensor_tensor(W2[:], W2[:], wsc[:], ALU.add),
             reads=("wsc", "W2:0", "W2:1"), writes=("W2:0", "W2:1"), tag=k + ":wu3")
        fg.op("vector", lambda v: v.tensor_copy(b2b[0:64, :], b2[:]),
             reads=("b2", "b2b"), writes=("b2b",), tag=k + ":wu6")
        fg.op("vector", lambda v: v.tensor_copy(W2b[:], W2[:]),
             reads=("W2:0", "W2:1", "W2b"), writes=("W2b",), tag=k + ":wu4")

        prev = mk_tail(n)

    if prev is not None:
        for nm in TAIL_ORDER:
            prev[nm]()

    for cc in range(T["NCB_T"]):
        f.coll("AllGather", ALU.bypass, [T[f"ymix_pub{cc}"].opt()],
               [T[f"ymix_gat{cc}"].opt()], reads=(f"ymix_pub{cc}",),
               writes=(f"ymix_gat{cc}",), tag=f"sc:ag{cc}")
    d = T["dout"]("ymixg", [1024, seq], BF16)
    if d is not None:
        for cc in range(T["NCB_T"]):
            f.dma("sync", d[:, cc * T["CW"]:(cc + 1) * T["CW"]], T[f"ymix_gat{cc}"],
                  reads=(f"ymix_gat{cc}",), tag=f"dbg:ymix{cc}")


class LNH:

    def __init__(self, f, nc, cst, ctx, pfx):
        self.f, self.nc, self.cst, self.pfx = f, nc, cst, pfx
        sb = lambda n, s, dt=F32: ctx.enter_context(nc.sbuf_tensor(pfx + n, s, dt))
        ps = lambda n, s, dt=F32: ctx.enter_context(nc.psum_tensor(pfx + n, s, dt))
        self.xb = sb("xb", [128, 8, TB], BF16)
        self.sqb = sb("sqb", [128, 8, TB], BF16)
        self.srow = sb("srow", [1, 4 * TB])
        self.srow_b = sb("srowb", [1, 2 * TB], BF16)
        self.p_sum = ps("p_sum", [1, TB])
        self.p_ss = ps("p_ss", [1, TB])
        self.p_bc = ps("p_bc", [128, 2 * TB])

    def stats(self, tag, src_ap, src_res, src_bf16, eps_col, extra_reads=()):
        f, cst = self.f, self.cst
        xb, sqb, srow, srow_b = self.xb, self.sqb, self.srow, self.srow_b
        p_sum, p_ss, p_bc = self.p_sum, self.p_ss, self.p_bc
        P = self.pfx
        if not src_bf16:
            f.op("vector", lambda v: v.tensor_copy(xb[:], src_ap),
                 reads=(src_res,) + tuple(extra_reads), writes=(P + "xb",), tag=tag + ":cast")
            sums_src, sums_res = xb, P + "xb"
        else:
            sums_src, sums_res = src_ap, src_res
        f.op("scalar", lambda s: s.activation(sqb[:], (xb[:] if not src_bf16 else src_ap), AF.Square),
             reads=(sums_res if not src_bf16 else src_res,), writes=(P + "sqb",), tag=tag + ":sq")

        def mm_sum(t):
            last = None
            for kc in range(8):
                aa = sums_src[:, kc, :] if not src_bf16 else src_ap[:, kc, :]
                last = t.matmul(p_sum[0:1, :], cst["onescol_b"][:], aa,
                                start=(kc == 0), stop=(kc == 7))
            return last
        f.op("tensor", mm_sum, reads=(sums_res if not src_bf16 else src_res,),
             writes=(P + "p_sum",), tag=tag + ":mms")

        def mm_ss(t):
            last = None
            for kc in range(8):
                last = t.matmul(p_ss[0:1, :], cst["onescol_b"][:], sqb[:, kc, :],
                                start=(kc == 0), stop=(kc == 7))
            return last
        f.op("tensor", mm_ss, reads=(P + "sqb",), writes=(P + "p_ss",), tag=tag + ":mmss")

        def stats_v(v):
            v.tensor_scalar(out=srow[0:1, 0:TB], in0=p_sum[0:1, :], scalar1=1.0 / D,
                            scalar2=None, op0=ALU.mult)
            v.tensor_scalar(out=srow[0:1, 3 * TB:4 * TB], in0=p_ss[0:1, :],
                            scalar1=1.0 / D, scalar2=None, op0=ALU.mult)
            v.drain()
            v.tensor_tensor(srow[0:1, TB:2 * TB], srow[0:1, 0:TB], srow[0:1, 0:TB], ALU.mult)
            v.drain()
            v.tensor_tensor(srow[0:1, TB:2 * TB], srow[0:1, 3 * TB:4 * TB],
                            srow[0:1, TB:2 * TB], ALU.subtract)
            v.drain()
            return v.nop()
        f.op("vector", stats_v, reads=(P + "p_sum", P + "p_ss"), writes=(P + "srow",),
             tag=tag + ":st")

        def rstd_s(s):
            s.activation(srow[0:1, 2 * TB:3 * TB], srow[0:1, TB:2 * TB], AF.Ln,
                         bias=cst["epsv"][0:1, eps_col:eps_col + 1])
            s.drain()
            s.activation(srow[0:1, 2 * TB:3 * TB], srow[0:1, 2 * TB:3 * TB],
                         AF.Exp, scale=-0.5)
            s.drain()
            return s.nop()
        f.op("scalar", rstd_s, reads=(P + "srow",), writes=(P + "srow",), tag=tag + ":rstd")

        def tobf(v):
            v.tensor_copy(srow_b[0:1, 0:TB], srow[0:1, 0:TB])
            v.tensor_copy(srow_b[0:1, TB:2 * TB], srow[0:1, 2 * TB:3 * TB])
            v.drain()
            return v.nop()
        f.op("vector", tobf, reads=(P + "srow",), writes=(P + "srowb",), tag=tag + ":bf")

        def bcast(t):
            t.matmul(p_bc[:, 0:TB], cst["onesr_b"][:], srow_b[0:1, 0:TB],
                     start=True, stop=True)
            return t.matmul(p_bc[:, TB:2 * TB], cst["onesr_b"][:],
                            srow_b[0:1, TB:2 * TB], start=True, stop=True)
        f.op("tensor", bcast, reads=(P + "srowb", P + "p_bc"), writes=(P + "p_bc",),
             tag=tag + ":bc")
        return p_bc


def emit_p2(f, nc, T, ctx):
    cst = T["cst"]; nto = T["nto"]; ntb = T["ntb_o"]; seq = T["seq"]
    sb = lambda n, s, dt=F32: ctx.enter_context(nc.sbuf_tensor(n, s, dt))
    ps = lambda n, s, dt=F32: ctx.enter_context(nc.psum_tensor(n, s, dt))

    ym = sb("ym", [128, 8, nto], BF16)
    yn = sb("yn", [128, 8, TB], BF16)
    ao = sb("ao", [128, 8, TB])
    xf = sb("xf2", [128, 8, TB])
    x2 = sb("x2t", [128, 8, TB])
    ab = sb("abuf", [128, 8, TB], BF16)
    wo_sb = sb("wo_sb", [128, 8, 1024], BF16)
    scf = sb("scf2", [128, 8, TB])
    p_w = [ps("p_wA", [128, TB]), ps("p_wB", [128, TB])]
    ln = LNH(f, nc, cst, ctx, "l2_")

    f.dma("gpsimd", wo_sb[:], T["woT"].rearrange("(kc p) m -> p kc m", p=128),
          writes=("wo_sb",), tag="p2:ldwo")

    def gat(e, core):
        q = core % 4
        cc = (nto * q) // T["CW"]
        off = nto * q - cc * T["CW"]
        return e.dma_start(out=ym[:], in_=T[f"ymix_gat{cc}"]
                           .rearrange("(kc p) t -> p kc t", p=128)
                           [:, :, off:off + nto])
    f.dma_percore("sync", gat,
                  reads=tuple(f"ymix_gat{cc}" for cc in range(T["NCB_T"])),
                  writes=("ym",), tag="p2:gat")

    for tb in range(ntb):
        sl = slice(tb * TB, (tb + 1) * TB)
        k = f"p2.{tb}"
        p_bc = ln.stats(k + ":pn", ym[:, :, sl], "ym", True, 1)

        def pnorm(v, sl=sl):
            v.tensor_tensor(scf[:], ym[:, :, sl],
                            p_bc[:, 0:TB].unsqueeze(1).broadcast_to([128, 8, TB]), ALU.subtract)
            v.drain()
            v.tensor_tensor(scf[:], scf[:],
                            p_bc[:, TB:2 * TB].unsqueeze(1).broadcast_to([128, 8, TB]), ALU.mult)
            v.drain()
            v.tensor_tensor(scf[:], scf[:],
                            cst["post_nw"][:].unsqueeze(2).broadcast_to([128, 8, TB]), ALU.mult)
            v.drain()
            v.tensor_tensor(yn[:], scf[:],
                            cst["post_nb"][:].unsqueeze(2).broadcast_to([128, 8, TB]), ALU.add)
            v.drain()
            return v.nop()
        f.op("vector", pnorm, reads=("ym", "l2_p_bc"), writes=("yn", "scf2"),
             tag=k + ":pn")

        for mc in range(8):
            wp = mc % 2
            def womm(t, mc=mc, wp=wp):
                last = None
                for kc in range(8):
                    last = t.matmul(p_w[wp][:], wo_sb[:, kc, 128 * mc:128 * mc + 128],
                                    yn[:, kc, :], start=(kc == 0), stop=(kc == 7))
                return last
            f.op("tensor", womm, reads=("yn", "wo_sb"), writes=(f"p_w{wp}",), tag=k + f":wo{mc}")
            f.op("scalar", lambda s, mc=mc, wp=wp: s.activation(
                ao[:, mc, :], p_w[wp][:], AF.Identity, bias=cst["wo_b"][:, mc:mc + 1]),
                reads=(f"p_w{wp}",), writes=("ao",), tag=k + f":woev{mc}")

        f.dma("sync", xf[:], fm3(T["x_own"])[:, :, sl], writes=("xf2",), tag=k + ":ldx")
        p_bc = ln.stats(k + ":ol", ao[:], "ao", False, 0)

        def onorm(v):
            v.tensor_tensor(scf[:], ao[:],
                            p_bc[:, 0:TB].unsqueeze(1).broadcast_to([128, 8, TB]), ALU.subtract)
            v.drain()
            v.tensor_tensor(scf[:], scf[:],
                            p_bc[:, TB:2 * TB].unsqueeze(1).broadcast_to([128, 8, TB]), ALU.mult)
            v.drain()
            v.tensor_tensor(x2[:], scf[:], xf[:], ALU.add)
            v.drain()
            return v.nop()
        f.op("vector", onorm, reads=("ao", "l2_p_bc", "xf2"), writes=("x2t", "scf2"),
             tag=k + ":on")
        f.dma("sync", fm3(T["x2_spill"])[:, :, sl], x2[:], reads=("x2t",),
              writes=("x2_spill",), tag=k + ":spx2")

        p_bc2 = ln.stats(k + ":y0", x2[:], "x2t", False, 0)

        def y0norm(v):
            v.tensor_tensor(scf[:], x2[:],
                            p_bc2[:, 0:TB].unsqueeze(1).broadcast_to([128, 8, TB]), ALU.subtract)
            v.drain()
            v.tensor_tensor(xf[:], scf[:],
                            p_bc2[:, TB:2 * TB].unsqueeze(1).broadcast_to([128, 8, TB]), ALU.mult)
            v.drain()
            return v.nop()
        f.op("vector", y0norm, reads=("x2t", "l2_p_bc"), writes=("xf2", "scf2"),
             tag=k + ":y0n")
        f.dma("sync", fm3(T["y0_spill"])[:, :, sl], xf[:], reads=("xf2",),
              writes=("y0_spill",), tag=k + ":spy0")

        p_bc3 = ln.stats(k + ":ln1", xf[:], "xf2", False, 0)

        def anorm(v):
            v.tensor_tensor(scf[:], xf[:],
                            p_bc3[:, 0:TB].unsqueeze(1).broadcast_to([128, 8, TB]), ALU.subtract)
            v.drain()
            v.tensor_tensor(scf[:], scf[:],
                            p_bc3[:, TB:2 * TB].unsqueeze(1).broadcast_to([128, 8, TB]), ALU.mult)
            v.drain()
            v.tensor_tensor(ab[:], scf[:],
                            cst["ln1_w"][:].unsqueeze(2).broadcast_to([128, 8, TB]), ALU.mult)
            v.drain()
            return v.nop()
        f.op("vector", anorm, reads=("xf2", "l2_p_bc"), writes=("abuf", "scf2"),
             tag=k + ":an")
        f.dma("sync", fm3(T[f"a_pub{tb}"]), ab[:], reads=("abuf",),
              writes=(f"a_pub{tb}",), tag=k + ":pub")

    d = T["dout"]("x2sp", [D, nto])
    if d is not None:
        f.dma("sync", d, T["x2_spill"], reads=("x2_spill",), tag="dbg:x2")
    d = T["dout"]("agat", [4 * D, nto], BF16)
    if d is not None:
        for c in range(T["NCB_O"]):
            f.coll("AllGather", ALU.bypass, [T[f"a_pub{c}"].opt()],
                   [T[f"a_gat{c}"].opt()], reads=(f"a_pub{c}",),
                   writes=(f"a_gat{c}",), tag=f"p2:ag{c}")
            f.dma("sync", d[:, c * TB:(c + 1) * TB], T[f"a_gat{c}"],
                  reads=(f"a_gat{c}",), tag=f"dbg:agat{c}")
        T["_a_coll_done"] = True


def emit_p3(f, nc, T, ctx):
    cst = T["cst"]; nto = T["nto"]; seq = T["seq"]; ntb = T["ntb_s"]
    sb = lambda n, s, dt=F32: ctx.enter_context(nc.sbuf_tensor(n, s, dt))
    ps = lambda n, s, dt=F32: ctx.enter_context(nc.psum_tensor(n, s, dt))

    kvl = sb("kvl", [128, 2, seq], BF16)
    ql = sb("ql", [128, 2, seq], BF16)
    ku_sb = sb("ku_sb", [128, 2, 128], BF16)
    qu_sb = sb("qu_sb", [128, 2, 128], BF16)
    vu_sb = sb("vu_sb", [128, 2, 256], BF16)
    p_l = ps("p_l", [128, TB])
    p_rh = ps("p_rh", [128, TB])
    krb_d = nc.dram_tensor("krb_d", [128, seq], BF16, kind="Internal").ap()
    qrb_d = nc.dram_tensor("qrb_d", [128, seq], BF16, kind="Internal").ap()

    if not T.get("_a_coll_done"):
        for c in range(T["NCB_O"]):
            f.coll("AllGather", ALU.bypass, [T[f"a_pub{c}"].opt()],
                   [T[f"a_gat{c}"].opt()], reads=(f"a_pub{c}",),
                   writes=(f"a_gat{c}",), tag=f"p3:agc{c}")

    r3 = lambda ap: ap.rearrange("(kc p) m -> p kc m", p=128)
    for nm, dst in [("k_uT", ku_sb), ("q_uT", qu_sb), ("v_uT", vu_sb)]:
        f.dma("gpsimd", dst[:], r3(T[nm]), writes=(nm + "_sb",), tag="p3:ld" + nm)

    ctxa = ctx.enter_context(ExitStack())
    sba = lambda n, s, dt=F32: ctxa.enter_context(nc.sbuf_tensor(n, s, dt))
    krch = [sba("krch0", [128, TB], BF16), sba("krch1", [128, TB], BF16)]
    qrch = [sba("qrch0", [128, TB], BF16), sba("qrch1", [128, TB], BF16)]
    _ag0 = sba("ag0", [128, 8, TB], BF16)
    ag = [_ag0, _ag0]
    cosb = sba("cosb", [128, TB]); sinb = sba("sinb", [128, TB])
    rr1 = sba("rr1", [128, TB]); rr2 = sba("rr2", [128, TB]); rr3 = sba("rr3", [128, TB])
    kvd_sb = sba("kvd_sb", [128, 8, 256], BF16)
    qd_sb = sba("qd_sb", [128, 8, 256], BF16)
    rk_sb = sba("rk_sb", [128, 8, 128], BF16)
    rq_sb = sba("rq_sb", [128, 2, 128], BF16)
    for nm, dst in [("kv_dT", kvd_sb), ("q_dT", qd_sb), ("rope_kT", rk_sb),
                    ("rope_qT", rq_sb)]:
        f.dma("gpsimd", dst[:], r3(T[nm]), writes=(nm + "_sb",), tag="p3:ld" + nm)

    for gtb in range(ntb):
        par = 0
        cpar = gtb % 2
        sl = slice(gtb * TB, (gtb + 1) * TB)
        k = f"p3a.{gtb}"
        t0 = gtb * TB
        r = t0 // nto
        loc = t0 - r * nto
        ch = loc // TB
        src_ap = T[f"a_gat{ch}"][r * 1024:(r + 1) * 1024, :] \
            .rearrange("(kc p) t -> p kc t", p=128)
        f.dma("sync", ag[par][:], src_ap, reads=(f"a_gat{ch}",), writes=(f"ag{par}",),
              tag=k + ":ld")
        f.dma("sync", cosb[:], T["cos_t"][:, sl], writes=("cosb",), tag=k + ":ldc")
        f.dma("sync", sinb[:], T["sin_t"][:, sl], writes=("sinb",), tag=k + ":lds")

        for nm, wsb, dst, dres in [("kv", kvd_sb, kvl, "kvl"), ("q", qd_sb, ql, "ql")]:
            for mc in range(2):
                pl = p_l if mc == 0 else p_rh
                PL = "p_l" if mc == 0 else "p_rh"
                def latmm(t, wsb=wsb, mc=mc, par=par, pl=pl):
                    last = None
                    for kc in range(8):
                        last = t.matmul(pl[:], wsb[:, kc, 128 * mc:128 * mc + 128],
                                        ag[par][:, kc, :], start=(kc == 0), stop=(kc == 7))
                    return last
                f.op("tensor", latmm, reads=(f"ag{par}", ("kv_dT" if nm == "kv" else "q_dT") + "_sb", PL),
                     writes=(PL,), tag=k + f":{nm}mm{mc}")
                f.op("vector", lambda v, dst=dst, mc=mc, sl=sl, pl=pl: v.tensor_copy(
                    dst[:, mc, sl], pl[:]),
                    reads=(PL,), writes=(dres,), tag=k + f":{nm}ev{mc}")

        def krmm(t, par=par):
            last = None
            for kc in range(8):
                last = t.matmul(p_l[:], rk_sb[:, kc, :], ag[par][:, kc, :],
                                start=(kc == 0), stop=(kc == 7))
            return last
        f.op("tensor", krmm, reads=(f"ag{par}", "rope_kT_sb", "kvl", "ql", "p_l"),
             writes=("p_l",), tag=k + ":krmm")
        f.op("vector", lambda v: v.tensor_copy(rr1[:], p_l[:]),
             reads=("p_l", "rr1"), writes=("rr1",), tag=k + ":krev")
        f.op("tensor", lambda t: t.matmul(p_rh[:], cst["r4T"][:], rr1[:],
                                          start=True, stop=True),
             reads=("rr1",), writes=("p_rh",), tag=k + ":krrh")

        def krope(v, cpar=cpar):
            v.tensor_tensor(rr2[:], rr1[:], cosb[:], ALU.mult)
            v.drain()
            v.tensor_tensor(rr3[:], p_rh[:], sinb[:], ALU.mult)
            v.drain()
            v.tensor_tensor(krch[cpar][:], rr2[:], rr3[:], ALU.add)
            v.drain()
            return v.nop()
        f.op("vector", krope, reads=("rr1", "p_rh", "cosb", "sinb", f"krch{cpar}"),
             writes=(f"krch{cpar}", "rr2", "rr3"), tag=k + ":krope")
        f.dma("sync", krb_d[:, sl], krch[cpar][:], reads=(f"krch{cpar}",),
              writes=("krb_d",), tag=k + ":krsp")

        def qrmm(t, sl=sl):
            last = None
            for mc in range(2):
                last = t.matmul(p_l[:], rq_sb[:, mc, :], ql[:, mc, sl],
                                start=(mc == 0), stop=(mc == 1))
            return last
        f.op("tensor", qrmm, reads=("ql", "rope_qT_sb", "krb_d", "p_l"), writes=("p_l",),
             tag=k + ":qrmm")
        f.op("vector", lambda v: v.tensor_copy(rr1[:], p_l[:]),
             reads=("p_l", "rr1"), writes=("rr1",), tag=k + ":qrev")
        f.op("tensor", lambda t: t.matmul(p_rh[:], cst["r4T"][:], rr1[:],
                                          start=True, stop=True),
             reads=("rr1", "p_rh"), writes=("p_rh",), tag=k + ":qrrh")

        def qrope(v, cpar=cpar):
            v.tensor_tensor(rr2[:], rr1[:], cosb[:], ALU.mult)
            v.drain()
            v.tensor_tensor(rr3[:], p_rh[:], sinb[:], ALU.mult)
            v.drain()
            v.tensor_tensor(rr2[:], rr2[:], rr3[:], ALU.add)
            v.drain()
            v.tensor_scalar(out=qrch[cpar][:], in0=rr2[:], scalar1=0.125, scalar2=None,
                            op0=ALU.mult)
            v.drain()
            return v.nop()
        f.op("vector", qrope, reads=("rr1", "p_rh", "cosb", "sinb", f"qrch{cpar}"),
             writes=(f"qrch{cpar}", "rr2", "rr3", f"ag{par}", "cosb", "sinb"),
             tag=k + ":qrope")
        f.dma("sync", qrb_d[:, sl], qrch[cpar][:], reads=(f"qrch{cpar}",),
              writes=("qrb_d",), tag=k + ":qrsp")

    f.barrier("p3a")
    ctxa.close()

    kaug = [sb("kaugA", [65, seq], BF16), sb("kaugB", [65, seq], BF16)]
    qaug = [sb("qaugA", [65, seq], BF16), sb("qaugB", [65, seq], BF16)]
    vaug = [sb("vaugA", [128, seq // 128, 65], BF16),
            sb("vaugB", [128, seq // 128, 65], BF16)]
    sqsK = [sb("sqsKA", [64, TB], BF16), sb("sqsKB", [64, TB], BF16)]
    sqsQ = [sb("sqsQA", [64, TB], BF16), sb("sqsQB", [64, TB], BF16)]
    mkr = [sb("mkrA", [1, 8]), sb("mkrB", [1, 8])]
    rowb = sb("rowb", [1, seq], BF16)
    ybuf = sb("ybuf", [64, TB], BF16)
    osum = [sb("osumA", [65, TB]), sb("osumB", [65, TB])]
    rrow = [sb("rrowA", [1, TB]), sb("rrowB", [1, TB])]
    rrow_b = [sb("rrow_bA", [1, TB], BF16), sb("rrow_bB", [1, TB], BF16)]
    ppb = sb("ppb", [128, 16, TB], BF16)
    p_bc2 = ps("p_bc2", [128, TB])
    p_s = [ps("p_s0", [128, TB]), ps("p_s1", [128, TB]), p_l, p_bc2]
    PS_RES = ("p_s0", "p_s1", "p_l", "p_bc2")
    p_os = [ps("p_os0", [65, TB]), ps("p_os1", [65, TB])]
    p_kq = ps("p_kq", [128, TB])

    njc = seq // 128

    def make_prologue(h):
        hp = h % 2
        ka, qa, va, mk = kaug[hp], qaug[hp], vaug[hp], mkr[hp]
        sqK, sqQ = sqsK[hp], sqsQ[hp]
        KA, QA, VA, MK = (f"kaug{hp}", f"qaug{hp}", f"vaug{hp}", f"mkr{hp}")
        SK, SQ = f"sqsK{hp}", f"sqsQ{hp}"
        kh = f"p3b{h}"
        ops = []
        E = ops.append
        E(lambda: f.dma("sync", ka[0:32, :], krb_d[32 * h:32 * h + 32, :],
                        reads=("krb_d", KA), writes=(KA,), tag=kh + ":kr"))
        E(lambda: f.dma("sync", qa[0:32, :], qrb_d[32 * h:32 * h + 32, :],
                        reads=("qrb_d", QA), writes=(QA,), tag=kh + ":qr"))
        E(lambda: f.op("gpsimd", lambda g: g.memset(ka[64:65, :], 1.0),
                       reads=(KA,), writes=(KA,), tag=kh + ":k1"))
        E(lambda: f.op("gpsimd", lambda g: g.memset(va[:], 1.0),
                       reads=(VA,), writes=(VA,), tag=kh + ":v1"))
        for gtb in range(ntb):
            sl = slice(gtb * TB, (gtb + 1) * TB)
            def kumm(t, sl=sl):
                last = None
                for mc in range(2):
                    last = t.matmul(p_kq[0:32, :], ku_sb[:, mc, 32 * h:32 * h + 32],
                                    kvl[:, mc, sl], start=(mc == 0), stop=(mc == 1))
                return last
            E(lambda kumm=kumm, gtb=gtb: f.op("tensor", kumm,
                                              reads=("kvl", "k_uT_sb", "p_kq"),
                                              writes=("p_kq",), tag=kh + f":ku{gtb}"))
            E(lambda sl=sl, gtb=gtb: f.op(
                "vector", lambda v, sl=sl: v.tensor_copy(ka[32:64, sl], p_kq[0:32, :]),
                reads=("p_kq",), writes=(KA,), tag=kh + f":kuev{gtb}"))
            def qumm(t, sl=sl):
                last = None
                for mc in range(2):
                    last = t.matmul(p_kq[0:32, :], qu_sb[:, mc, 32 * h:32 * h + 32],
                                    ql[:, mc, sl], start=(mc == 0), stop=(mc == 1))
                return last
            E(lambda qumm=qumm, gtb=gtb: f.op("tensor", qumm,
                                              reads=("ql", "q_uT_sb", "p_kq"),
                                              writes=("p_kq",), tag=kh + f":qu{gtb}"))
            E(lambda sl=sl, gtb=gtb: f.op(
                "vector", lambda v, sl=sl: v.tensor_scalar(
                    out=qa[32:64, sl], in0=p_kq[0:32, :], scalar1=0.125, scalar2=None,
                    op0=ALU.mult),
                reads=("p_kq",), writes=(QA,), tag=kh + f":quev{gtb}"))
        for jc in range(njc):
            def vmm2(t, jc=jc):
                last = None
                for mc in range(2):
                    last = t.matmul(p_rh[:, 0:64], kvl[:, mc, 128 * jc:128 * jc + 128],
                                    vu_sb[:, mc, 64 * h:64 * h + 64],
                                    start=(mc == 0), stop=(mc == 1))
                return last
            E(lambda vmm2=vmm2, jc=jc: f.op("tensor", vmm2,
                                            reads=("kvl", "v_uT_sb", "p_rh"),
                                            writes=("p_rh",), tag=kh + f":v{jc}"))
            E(lambda jc=jc: f.op(
                "vector", lambda v, jc=jc: v.tensor_copy(va[:, jc, 0:64],
                                                         p_rh[:, 0:64]),
                reads=("p_rh", VA), writes=(VA,), tag=kh + f":vev{jc}"))
        sqd = [sqsK[0], sqsQ[0], sqsK[1], sqsQ[1]]
        SQD = ("sqsK0", "sqsQ0", "sqsK1", "sqsQ1")

        def em_sq(src_ap, SRC, gtb, nm):
            sl = slice(gtb * TB, (gtb + 1) * TB)
            d = gtb % 4
            E(lambda sl=sl, d=d, gtb=gtb: f.op(
                "vector", lambda v, sl=sl, d=d: v.tensor_tensor(
                    sqd[d][:], src_ap[0:64, sl], src_ap[0:64, sl], ALU.mult),
                reads=(SRC, SQD[d]), writes=(SQD[d],), tag=kh + f":sq{nm}{gtb}"))

        def em_ss(gtb, nm):
            sl = slice(gtb * TB, (gtb + 1) * TB)
            d = gtb % 4
            E(lambda d=d, gtb=gtb: f.op(
                "tensor", lambda t, d=d: t.matmul(
                    p_kq[0:1, :], cst["onescol_b"][0:64, 0:1], sqd[d][:],
                    start=True, stop=True),
                reads=(SQD[d], "p_kq"), writes=("p_kq",), tag=kh + f":ss{nm}{gtb}"))
            E(lambda sl=sl, gtb=gtb: f.op(
                "vector", lambda v, sl=sl: v.tensor_copy(rowb[0:1, sl], p_kq[0:1, :]),
                reads=("p_kq", "rowb"), writes=("rowb",), tag=kh + f":ev{nm}{gtb}"))

        LEAD = 3
        for gtb in range(min(LEAD, ntb)):
            em_sq(ka, KA, gtb, "k")
        for gtb in range(ntb):
            if gtb + LEAD < ntb:
                em_sq(ka, KA, gtb + LEAD, "k")
            em_ss(gtb, "k")
        E(lambda: f.op("vector", lambda v: v.tensor_reduce(
            mk[0:1, 0:1], rowb[0:1, :], axis=AX.X, op=ALU.max),
            reads=("rowb", MK), writes=(MK,), tag=kh + ":mkred"))
        E(lambda: f.op("scalar", lambda s: s.activation(mk[0:1, 1:2], mk[0:1, 0:1],
                                                        AF.Ln),
                       reads=(MK,), writes=(MK,), tag=kh + ":mkln"))
        for gtb in range(min(LEAD, ntb)):
            em_sq(qa, QA, gtb, "q")
        for gtb in range(ntb):
            if gtb + LEAD < ntb:
                em_sq(qa, QA, gtb + LEAD, "q")
            em_ss(gtb, "q")
        E(lambda: f.op("scalar", lambda s: s.activation(rowb[0:1, :], rowb[0:1, :],
                                                        AF.Ln),
                       reads=("rowb",), writes=("rowb",), tag=kh + ":bln"))
        E(lambda: f.op("vector", lambda v: v.tensor_scalar(
            out=rowb[0:1, :], in0=rowb[0:1, :], scalar1=mk[0:1, 1:2], scalar2=None,
            op0=ALU.add),
            reads=("rowb", MK), writes=("rowb",), tag=kh + ":badd"))
        E(lambda: f.op("scalar", lambda s: s.activation(rowb[0:1, :], rowb[0:1, :],
                                                        AF.Exp, scale=0.5),
                       reads=("rowb",), writes=("rowb",), tag=kh + ":bexp"))
        E(lambda: f.op("vector", lambda v: v.tensor_scalar(
            out=qa[64:65, :], in0=rowb[0:1, :], scalar1=-1.0, scalar2=None,
            op0=ALU.mult),
            reads=("rowb", QA), writes=(QA,), tag=kh + ":bneg"))
        return ops

    GN = 8
    nst = seq // TB

    def emit_head(h, pend):
        hp = h % 2
        ka, qa, va = kaug[hp], qaug[hp], vaug[hp]
        KA, QA, VA = f"kaug{hp}", f"qaug{hp}", f"vaug{hp}"
        kh = f"p3b{h}"
        chunks = []
        for st in range(nst):
            njt = 4 * (st + 1)
            nchk = (njt + GN - 1) // GN
            for ck in range(nchk):
                chunks.append((st, ck, ck * GN, min(GN, njt - ck * GN),
                               ck == nchk - 1))
        NCH = len(chunks)

        def s_exp(ci):
            st, ck, j0, jn, is_last = chunks[ci]
            isl = slice(st * TB, (st + 1) * TB)
            sbase = (ci % 2) * 8
            for jj in range(jn):
                jt = j0 + jj
                par = (ci * GN + jj) % 4
                slot = sbase + jj
                def smm(t, jt=jt, par=par, isl=isl):
                    return t.matmul(p_s[par][:], ka[:, 128 * jt:128 * jt + 128],
                                    qa[:, isl], start=True, stop=True)
                f.op("tensor", smm, reads=(KA, QA), writes=(PS_RES[par],),
                     tag=kh + f":s{st}.{jt}")

                def expf(s, slot=slot, par=par):
                    return s.activation(ppb[:, slot, :], p_s[par][:], AF.Exp)
                f.op("scalar", expf, reads=(PS_RES[par],), writes=(f"ppb{slot}",),
                     tag=kh + f":e{st}.{jt}")
                if jt >= 4 * st:
                    off = jt - 4 * st
                    f.op("vector", lambda v, slot=slot, off=off: v.tensor_tensor(
                        ppb[:, slot, :], ppb[:, slot, :], cst["amask"][:, off, :],
                        ALU.mult),
                        reads=(f"ppb{slot}",), writes=(f"ppb{slot}",),
                        tag=kh + f":m{st}.{jt}")

        def av(ci):
            st, ck, j0, jn, is_last = chunks[ci]
            opar = ci % 2
            sbase = (ci % 2) * 8
            osp = st % 2

            def avg(t, j0=j0, jn=jn, opar=opar, sbase=sbase):
                last = None
                for jj in range(jn):
                    last = t.matmul(p_os[opar][:], va[:, j0 + jj, :],
                                    ppb[:, sbase + jj, :], start=(jj == 0),
                                    stop=(jj == jn - 1))
                return last
            f.op("tensor", avg,
                 reads=tuple(f"ppb{sbase + jj}" for jj in range(jn))
                 + (VA, f"p_os{opar}"),
                 writes=(f"p_os{opar}",), tag=kh + f":av{st}.{ck}")
            if ck == 0:
                f.op("vector", lambda v, opar=opar, osp=osp: v.tensor_copy(
                    osum[osp][:], p_os[opar][:]),
                    reads=(f"p_os{opar}", f"osum{osp}"), writes=(f"osum{osp}",),
                    tag=kh + f":oi{st}")
            else:
                f.op("vector", lambda v, opar=opar, osp=osp: v.tensor_tensor(
                    osum[osp][:], osum[osp][:], p_os[opar][:], ALU.add),
                    reads=(f"p_os{opar}", f"osum{osp}"), writes=(f"osum{osp}",),
                    tag=kh + f":oa{st}.{ck}")
            if is_last:
                f.op("vector", lambda v, osp=osp: v.reciprocal(
                    rrow[osp][0:1, :], osum[osp][64:65, :]),
                    reads=(f"osum{osp}", f"rrow{osp}"), writes=(f"rrow{osp}",),
                    tag=kh + f":rc{st}")
                f.op("vector", lambda v, osp=osp: v.tensor_copy(
                    rrow_b[osp][0:1, :], rrow[osp][0:1, :]),
                    reads=(f"rrow{osp}", f"rrowb{osp}"), writes=(f"rrowb{osp}",),
                    tag=kh + f":rcb{st}")

        def tail(ci):
            st, ck, j0, jn, is_last = chunks[ci]
            if not is_last:
                return
            osp = st % 2
            isl = slice(st * TB, (st + 1) * TB)
            f.op("tensor", lambda t, osp=osp: t.matmul(
                p_bc2[0:64, :], cst["onesr_b"][0:1, 0:64], rrow_b[osp][0:1, :],
                start=True, stop=True),
                 reads=(f"rrowb{osp}", "p_bc2"), writes=("p_bc2",), tag=kh + f":rb{st}")
            f.op("vector", lambda v, osp=osp: v.tensor_tensor(
                ybuf[:], osum[osp][0:64, :], p_bc2[0:64, :], ALU.mult),
                reads=(f"osum{osp}", "p_bc2", "ybuf"), writes=("ybuf",),
                tag=kh + f":ymul{st}")
            cc = (st * TB) // T["CW"]
            off = st * TB - cc * T["CW"]
            f.dma("sync", T[f"yat_pub{cc}"][64 * h:64 * h + 64, off:off + TB],
                  ybuf[:], reads=("ybuf",), writes=(f"yat_pub{cc}",),
                  tag=kh + f":pub{st}")

        NPOP = 6
        for ci in range(NCH):
            s_exp(ci)
            if ci >= 1:
                av(ci - 1)
            if ci >= 2:
                tail(ci - 2)
            for _ in range(NPOP):
                if pend:
                    pend.pop(0)()
        av(NCH - 1)
        tail(NCH - 2)
        tail(NCH - 1)
        while pend:
            pend.pop(0)()

    for t in make_prologue(0):
        t()
    for h in range(4):
        pend = make_prologue(h + 1) if h < 3 else []
        emit_head(h, pend)

    for cc in range(T["NCB_T"]):
        f.coll("AllGather", ALU.bypass, [T[f"yat_pub{cc}"].opt()],
               [T[f"yat_gat{cc}"].opt()], reads=(f"yat_pub{cc}",),
               writes=(f"yat_gat{cc}",), tag=f"p3:ag{cc}")
    d = T["dout"]("yatg", [1024, seq], BF16)
    if d is not None:
        for cc in range(T["NCB_T"]):
            f.dma("sync", d[:, cc * T["CW"]:(cc + 1) * T["CW"]], T[f"yat_gat{cc}"],
                  reads=(f"yat_gat{cc}",), tag=f"dbg:yat{cc}")


def emit_p4(f, nc, T, ctx):
    cst = T["cst"]; nto = T["nto"]; ntb = T["ntb_o"]
    sb = lambda n, s, dt=F32: ctx.enter_context(nc.sbuf_tensor(n, s, dt))
    ps = lambda n, s, dt=F32: ctx.enter_context(nc.psum_tensor(n, s, dt))
    xm = T["xm"]

    ya = sb("ya", [128, 8, nto], BF16)
    ow_sb = sb("ow_sb", [128, 8, 1024], BF16)
    y0t = sb("y0t", [128, 8, TB])
    y1t = sb("y1t", [128, 8, TB])
    scf = sb("scf4", [128, 8, TB])
    p_w = [ps("p_w4A", [128, TB]), ps("p_w4B", [128, TB])]
    ln = LNH(f, nc, cst, ctx, "l4_")

    f.dma("gpsimd", ow_sb[:], T["o_wT"].rearrange("(kc p) m -> p kc m", p=128),
          writes=("ow_sb",), tag="p4:ldow")

    for gtb in range(ntb):
        def gat(e, core, gtb=gtb):
            q = core % 4
            cc = (nto * q) // T["CW"]
            off = nto * q - cc * T["CW"] + gtb * TB
            return e.dma_start(out=ya[:, :, gtb * TB:(gtb + 1) * TB],
                               in_=T[f"yat_gat{cc}"]
                               .rearrange("(kc p) t -> p kc t", p=128)
                               [:, :, off:off + TB])
        f.dma_percore("sync", gat,
                      reads=tuple(f"yat_gat{cc}" for cc in range(T["NCB_T"])),
                      writes=(f"ya{gtb}",), tag=f"p4:gat{gtb}")

    for tb in range(ntb):
        sl = slice(tb * TB, (tb + 1) * TB)
        k = f"p4.{tb}"
        f.dma("sync", y0t[:], fm3(T["y0_spill"])[:, :, sl], reads=("y0_spill",),
              writes=("y0t",), tag=k + ":ldy0")
        for mc in range(8):
            wp = mc % 2
            def owmm(t, mc=mc, sl=sl, wp=wp):
                last = None
                for kc in range(8):
                    last = t.matmul(p_w[wp][:], ow_sb[:, kc, 128 * mc:128 * mc + 128],
                                    ya[:, kc, sl], start=(kc == 0), stop=(kc == 7))
                return last
            f.op("tensor", owmm, reads=(f"ya{tb}", "ow_sb"), writes=(f"p_w4{wp}",), tag=k + f":ow{mc}")

            def y1ev(v, mc=mc, wp=wp):
                v.tensor_tensor(y1t[:, mc, :], p_w[wp][:], y0t[:, mc, :], ALU.add)
                v.drain()
                return v.nop()
            f.op("vector", y1ev, reads=(f"p_w4{wp}", "y0t"), writes=("y1t",), tag=k + f":y1{mc}")
        f.dma("sync", fm3(T["y1_spill"])[:, :, sl], y1t[:], reads=("y1t",),
              writes=("y1_spill",), tag=k + ":spy1")

        p_bc = ln.stats(k + ":ln2", y1t[:], "y1t", False, 0)

        def xnorm(v, sl=sl):
            v.tensor_tensor(scf[:], y1t[:],
                            p_bc[:, 0:TB].unsqueeze(1).broadcast_to([128, 8, TB]), ALU.subtract)
            v.drain()
            v.tensor_tensor(scf[:], scf[:],
                            p_bc[:, TB:2 * TB].unsqueeze(1).broadcast_to([128, 8, TB]), ALU.mult)
            v.drain()
            v.tensor_tensor(xm[:, :, sl], scf[:],
                            cst["ln2_w"][:].unsqueeze(2).broadcast_to([128, 8, TB]), ALU.mult)
            v.drain()
            return v.nop()
        f.op("vector", xnorm, reads=("y1t", "l4_p_bc"), writes=("xm", "scf4"),
             tag=k + ":xn")

    d = T["dout"]("xmo", [128, 8, nto], BF16)
    if d is not None:
        f.dma("sync", d, xm[:], reads=("xm",), tag="dbg:xm")


def emit_p5(f, nc, T, ctx):
    cst = T["cst"]; nto = T["nto"]; ntb = T["ntb_o"]
    sb = lambda n, s, dt=F32: ctx.enter_context(nc.sbuf_tensor(n, s, dt))
    ps = lambda n, s, dt=F32: ctx.enter_context(nc.psum_tensor(n, s, dt))
    xm = T["xm"]
    BIG = 1.0e30

    accum = T["accum"]
    rw_sb = sb("rw_sb", [128, 8, 8], BF16)
    rbb = sb("rbb", [128, 8])
    lg = sb("lg", [128, nto // 128, 8])
    g1 = sb("g1", [128, nto // 128, 8])
    g2 = sb("g2", [128, nto // 128, 8])
    q3 = sb("q3", [128, nto // 128, 8])
    stt5 = sb("stt5", [128, 8 * (nto // 128)])
    wgt_fmb = sb("wgt_fmb", [8, nto], BF16)
    act_sb = sb("act_sb", [128, 16, 2 * TB], BF16)
    wg_sb = sb("wg_sb", [128, 8, TB], BF16)
    wu_sb = sb("wu_sb", [128, 8, TB], BF16)
    dn_sb = sb("dn_sb", [128, 16, 1024], BF16)
    sg = sb("sg", [128, TB], BF16)
    sgd = sb("sgd", [128, TB])
    wbt = sb("wbt", [128, 2 * TB], BF16)
    p_g = ps("p_g", [128, TB])
    p_u = ps("p_u", [128, TB])
    p_d = [ps("p_dA", [128, TB]), ps("p_dB", [128, TB])]
    p_wb = ps("p_wb", [128, 2 * TB])
    p_r = ps("p_r", [128, 8])
    ntt = nto // 128

    f.dma("gpsimd", rw_sb[:], T["r_wT"].rearrange("(kc p) m -> p kc m", p=128),
          writes=("rw_sb",), tag="p5:ldrw")
    f.op("tensor", lambda t: t.matmul(p_r[:], cst["onesr_f"][:], cst["r_b"][:],
                                      start=True, stop=True),
         reads=(), writes=("p_r",), tag="p5:rbb")
    f.op("vector", lambda v: v.tensor_copy(rbb[:], p_r[:]),
         reads=("p_r",), writes=("rbb",), tag="p5:rbbev")

    for tt in range(ntt):
        tsl = slice(tt * 128, tt * 128 + 128)
        def rmm(t, tsl=tsl):
            last = None
            for kc in range(8):
                last = t.matmul(p_r[:], xm[:, kc, tsl], rw_sb[:, kc, :],
                                start=(kc == 0), stop=(kc == 7))
            return last
        f.op("tensor", rmm, reads=("xm", "rw_sb", "rbb"), writes=("p_r",),
             tag=f"p5:rmm{tt}")
        f.op("vector", lambda v, tt=tt: v.tensor_tensor(lg[:, tt, :], p_r[:], rbb[:],
                                                        ALU.add),
             reads=("p_r", "rbb"), writes=("lg",), tag=f"p5:lgev{tt}")

    f.op("scalar", lambda s: s.activation(lg[:], lg[:], AF.Tanh, scale=0.5),
         reads=("lg",), writes=("lg",), tag="p5:tanh")
    f.op("vector", lambda v: v.tensor_scalar(out=lg[:], in0=lg[:], scalar1=0.5,
                                             scalar2=0.5, op0=ALU.mult, op1=ALU.add),
         reads=("lg",), writes=("lg",), tag="p5:probs")

    OF_M1, OF_I1, OF_M2, OF_I2, OF_W1, OF_W2, OF_T = (i * ntt for i in range(7))
    iob = cst["iota7"][:, 0:8].unsqueeze(1).broadcast_to([128, ntt, 8])
    bcc = lambda off: stt5[:, off:off + ntt].unsqueeze(2).broadcast_to([128, ntt, 8])

    def top2(v):
        v.tensor_reduce(stt5[:, OF_M1:OF_M1 + ntt], lg[:], axis=AX.X, op=ALU.max)
        v.drain()
        v.tensor_tensor(g1[:], lg[:], bcc(OF_M1), ALU.is_ge)
        v.drain()
        v.tensor_scalar(out=q3[:], in0=g1[:], scalar1=-BIG, scalar2=BIG,
                        op0=ALU.mult, op1=ALU.add)
        v.drain()
        v.tensor_tensor(q3[:], q3[:], iob, ALU.add)
        v.drain()
        v.tensor_reduce(stt5[:, OF_I1:OF_I1 + ntt], q3[:], axis=AX.X, op=ALU.min)
        v.drain()
        v.tensor_tensor(g1[:], iob, bcc(OF_I1), ALU.is_equal)
        v.drain()
        v.scalar_tensor_tensor(out=q3[:], in0=g1[:], scalar=-BIG, in1=lg[:],
                               op0=ALU.mult, op1=ALU.add)
        v.drain()
        v.tensor_reduce(stt5[:, OF_M2:OF_M2 + ntt], q3[:], axis=AX.X, op=ALU.max)
        v.drain()
        v.tensor_tensor(g2[:], q3[:], bcc(OF_M2), ALU.is_ge)
        v.drain()
        v.tensor_scalar(out=q3[:], in0=g2[:], scalar1=-BIG, scalar2=BIG,
                        op0=ALU.mult, op1=ALU.add)
        v.drain()
        v.tensor_tensor(q3[:], q3[:], iob, ALU.add)
        v.drain()
        v.tensor_reduce(stt5[:, OF_I2:OF_I2 + ntt], q3[:], axis=AX.X, op=ALU.min)
        v.drain()
        v.tensor_tensor(g2[:], iob, bcc(OF_I2), ALU.is_equal)
        v.drain()
        v.tensor_tensor(stt5[:, OF_T:OF_T + ntt], stt5[:, OF_M1:OF_M1 + ntt],
                        stt5[:, OF_M2:OF_M2 + ntt], ALU.add)
        v.drain()
        v.reciprocal(stt5[:, OF_T:OF_T + ntt], stt5[:, OF_T:OF_T + ntt])
        v.drain()
        v.tensor_tensor(stt5[:, OF_W1:OF_W1 + ntt], stt5[:, OF_M1:OF_M1 + ntt],
                        stt5[:, OF_T:OF_T + ntt], ALU.mult)
        v.tensor_tensor(stt5[:, OF_W2:OF_W2 + ntt], stt5[:, OF_M2:OF_M2 + ntt],
                        stt5[:, OF_T:OF_T + ntt], ALU.mult)
        v.drain()
        v.tensor_tensor(g1[:], g1[:], bcc(OF_W1), ALU.mult)
        v.tensor_tensor(g2[:], g2[:], bcc(OF_W2), ALU.mult)
        v.drain()
        v.tensor_tensor(g1[:], g1[:], g2[:], ALU.add)
        v.drain()
        return v.nop()
    f.op("vector", top2, reads=("lg",), writes=("g1", "g2", "q3", "stt5"),
         tag="p5:top2")

    for half in range((ntt + 7) // 8):
        tts = range(half * 8, min(ntt, half * 8 + 8))
        def wtr(t, tts=tts, half=half):
            last = None
            for tt in tts:
                last = t.transpose(p_wb[0:8, (tt - half * 8) * 128:(tt - half * 8) * 128 + 128],
                                   g1[:, tt, 0:8], cst["ident_f"][:])
            return last
        f.op("tensor", wtr, reads=("g1", "p_wb"), writes=("p_wb",), tag=f"p5:wtr{half}")
        n0 = half * 1024
        n1 = min(nto, n0 + 1024)
        f.op("vector", lambda v, n0=n0, n1=n1: v.tensor_copy(
            wgt_fmb[0:8, n0:n1], p_wb[0:8, 0:n1 - n0]),
            reads=("p_wb",), writes=("wgt_fmb",), tag=f"p5:wev{half}")

    NH2 = nto // (2 * TB) if nto >= 2 * TB else 1
    HLEN = min(2 * TB, nto)
    ntc2 = HLEN // TB
    for th in range(NH2):
        h0 = th * HLEN
        for e in range(NE):
            ke = f"p5.{th}.{e}"
            for ib in range(4):
                io = ib * TB
                for sub in range(4):
                    f.dma("gpsimd", wg_sb[:, :, 128 * sub:128 * sub + 128],
                          T["moe_gT"][e].rearrange("(kc p) m -> p kc m", p=128)
                          [:, :, io + 128 * sub:io + 128 * sub + 128],
                          writes=(f"wg_sb{sub}",), tag=ke + f":ldg{ib}.{sub}")
                for sub in range(4):
                    f.dma("gpsimd", wu_sb[:, :, 128 * sub:128 * sub + 128],
                          T["moe_uT"][e].rearrange("(kc p) m -> p kc m", p=128)
                          [:, :, io + 128 * sub:io + 128 * sub + 128],
                          writes=(f"wu_sb{sub}",), tag=ke + f":ldu{ib}.{sub}")
                for tc2 in range(ntc2):
                    tsl = slice(h0 + tc2 * TB, h0 + tc2 * TB + TB)
                    for sub in range(4):
                        ic = 4 * ib + sub
                        def gmm(t, sub=sub, tsl=tsl):
                            last = None
                            for kc in range(8):
                                last = t.matmul(p_g[:],
                                                wg_sb[:, kc, 128 * sub:128 * sub + 128],
                                                xm[:, kc, tsl],
                                                start=(kc == 0), stop=(kc == 7))
                            return last
                        f.op("tensor", gmm, reads=(f"wg_sb{sub}", "xm"), writes=("p_g",),
                             tag=ke + f":g{ib}.{tc2}.{sub}")
                        def umm(t, sub=sub, tsl=tsl):
                            last = None
                            for kc in range(8):
                                last = t.matmul(p_u[:],
                                                wu_sb[:, kc, 128 * sub:128 * sub + 128],
                                                xm[:, kc, tsl],
                                                start=(kc == 0), stop=(kc == 7))
                            return last
                        f.op("tensor", umm, reads=(f"wu_sb{sub}", "xm"), writes=("p_u",),
                             tag=ke + f":u{ib}.{tc2}.{sub}")
                        f.op("scalar", lambda s: s.activation(sg[:], p_g[:], AF.Silu),
                             reads=("p_g",), writes=("sg",), tag=ke + f":si{ib}.{tc2}.{sub}")
                        f.op("vector", lambda v, ic=ic, tc2=tc2: v.tensor_tensor(
                            act_sb[:, ic, tc2 * TB:tc2 * TB + TB], p_u[:], sg[:], ALU.mult),
                            reads=("p_u", "sg"), writes=("act_sb",),
                            tag=ke + f":am{ib}.{tc2}.{sub}")
            for icg in range(4):
                f.dma("gpsimd", dn_sb[:, 4 * icg:4 * icg + 4, :],
                      T["moe_dT"][e].rearrange("(ic p) m -> p ic m", p=128)
                      [:, 4 * icg:4 * icg + 4, :],
                      writes=(f"dn_sb{icg}",), tag=ke + f":ldd{icg}")
            if e > 0:
                def wbc(t, e=e, h0=h0):
                    last = None
                    for tc2 in range(ntc2):
                        last = t.matmul(p_wb[:, tc2 * TB:tc2 * TB + TB],
                                        cst["esel"][:, 128 * (e - 1):128 * (e - 1) + 128],
                                        wgt_fmb[0:8, h0 + tc2 * TB:h0 + tc2 * TB + TB],
                                        start=True, stop=True)
                    return last
                f.op("tensor", wbc, reads=("wgt_fmb", "p_wb"), writes=("p_wb",),
                     tag=ke + ":wbc")
                f.op("vector", lambda v: v.tensor_copy(wbt[:, 0:ntc2 * TB],
                                                       p_wb[:, 0:ntc2 * TB]),
                     reads=("p_wb",), writes=("wbt",), tag=ke + ":wbev")
            for tc2 in range(ntc2):
                tsl = slice(h0 + tc2 * TB, h0 + tc2 * TB + TB)
                for hc in range(8):
                    dp = hc % 2
                    def dmm(t, hc=hc, tc2=tc2, dp=dp):
                        last = None
                        for ic in range(16):
                            last = t.matmul(p_d[dp][:], dn_sb[:, ic, 128 * hc:128 * hc + 128],
                                            act_sb[:, ic, tc2 * TB:tc2 * TB + TB],
                                            start=(ic == 0), stop=(ic == 15))
                        return last
                    f.op("tensor", dmm,
                         reads=("dn_sb0", "dn_sb1", "dn_sb2", "dn_sb3", "act_sb"),
                         writes=(f"p_d{dp}",), tag=ke + f":d{tc2}.{hc}")
                    if e == 0:
                        f.op("vector", lambda v, hc=hc, tsl=tsl, dp=dp: v.tensor_copy(
                            accum[:, hc, tsl], p_d[dp][:]),
                            reads=(f"p_d{dp}",), writes=("accum",), tag=ke + f":ac{tc2}.{hc}")
                    else:
                        def acc(v, hc=hc, tsl=tsl, tc2=tc2, dp=dp):
                            v.tensor_tensor(sgd[:], p_d[dp][:], wbt[:, tc2 * TB:tc2 * TB + TB],
                                            ALU.mult)
                            v.drain()
                            v.tensor_tensor(accum[:, hc, tsl], accum[:, hc, tsl],
                                            sgd[:], ALU.add)
                            v.drain()
                            return v.nop()
                        f.op("vector", acc, reads=(f"p_d{dp}", "wbt", "accum"),
                             writes=("accum", "sgd"), tag=ke + f":ac{tc2}.{hc}")

    d = T["dout"]("moeo", [128, 8, nto])
    if d is not None:
        f.dma("sync", d, accum[:], reads=("accum",), tag="dbg:moe")


def emit_p6(f, nc, T, ctx):
    cst = T["cst"]; nto = T["nto"]; ntb = T["ntb_o"]
    sb = lambda n, s, dt=F32: ctx.enter_context(nc.sbuf_tensor(n, s, dt))
    accum = T["accum"]
    y1t = sb("y1t6", [128, 8, TB])
    x2t = sb("x2t6", [128, 8, TB])
    scf = sb("scf6", [128, 8, TB])
    outt = sb("outt", [128, 8, TB])
    ln = LNH(f, nc, cst, ctx, "l6_")

    for tb in range(ntb):
        sl = slice(tb * TB, (tb + 1) * TB)
        k = f"p6.{tb}"
        f.dma("sync", y1t[:], fm3(T["y1_spill"])[:, :, sl], reads=("y1_spill",),
              writes=("y1t6",), tag=k + ":ldy1")
        f.dma("sync", x2t[:], fm3(T["x2_spill"])[:, :, sl], reads=("x2_spill",),
              writes=("x2t6",), tag=k + ":ldx2")

        def y2c(v, sl=sl):
            v.tensor_tensor(y1t[:], y1t[:], accum[:, :, sl], ALU.add)
            v.drain()
            return v.nop()
        f.op("vector", y2c, reads=("y1t6", "accum"), writes=("y1t6",), tag=k + ":y2")

        p_bc = ln.stats(k + ":fl", y1t[:], "y1t6", False, 0)

        def onorm(v):
            v.tensor_tensor(scf[:], y1t[:],
                            p_bc[:, 0:TB].unsqueeze(1).broadcast_to([128, 8, TB]), ALU.subtract)
            v.drain()
            v.tensor_tensor(scf[:], scf[:],
                            p_bc[:, TB:2 * TB].unsqueeze(1).broadcast_to([128, 8, TB]), ALU.mult)
            v.drain()
            v.tensor_tensor(outt[:], scf[:], x2t[:], ALU.add)
            v.drain()
            return v.nop()
        f.op("vector", onorm, reads=("y1t6", "l6_p_bc", "x2t6"),
             writes=("outt", "scf6"), tag=k + ":on")
        f.dma("sync", fm3(T["out"])[:, :, sl], outt[:], reads=("outt",),
              writes=("out",), tag=k + ":st")


def _cc(a):
    return np.ascontiguousarray(a)


def prep_in_maps(inputs, seq=L):
    ii = {k: np.asarray(v) for k, v in inputs.items()}
    x = ii["L0_attn_hidden_states"].astype(np.float32, copy=False)
    nto = seq // 4
    nch = seq // C

    def A(name):
        return ii[name]

    ident_f = np.eye(128, dtype=np.float32)
    import ml_dtypes as _mld
    ident_b = np.eye(128, dtype=_mld.bfloat16)
    triuT_b = np.triu(np.ones((64, 64), np.float32)).astype("bfloat16_np", copy=False) \
        if False else np.triu(np.ones((64, 64))).astype(np.float32)
    import ml_dtypes
    bf = ml_dtypes.bfloat16
    triuT_b = np.triu(np.ones((64, 64))).astype(bf)
    i64z = np.zeros((128, 64), bf); i64z[0:64] = np.eye(64)
    triuTz = np.zeros((128, 64), bf); triuTz[0:64] = np.triu(np.ones((64, 64)))
    ones64 = np.ones((64, 64), bf)
    trimask4 = np.tile(np.triu(np.ones((64, 64), np.float32)), (1, 4))
    e2_b = np.zeros((128, 2), bf); e2_b[0:64, 0] = 1; e2_b[64:128, 1] = 1
    eb2_b = np.zeros((2, 128), bf); eb2_b[0, 0:64] = 1; eb2_b[1, 64:128] = 1
    onesr_b = np.ones((1, 128), bf)
    onesr_f = np.ones((1, 128), np.float32)
    onescol_b = np.ones((128, 1), bf)
    wmask = np.zeros((128, 128), np.float32)
    wmask[0:64, 0:64] = 1; wmask[64:128, 64:128] = 1
    amask = np.zeros((128, 4, TB), bf)
    for o in range(4):
        jj = 128 * o + np.arange(128)[:, None]
        amask[:, o, :] = (jj <= np.arange(TB)[None, :]).astype(bf)
    r4 = np.zeros((32, 32), np.float32)
    for dp in range(16):
        r4[dp + 16, dp] = -1.0
        r4[dp, dp + 16] = 1.0
    r4T = np.zeros((128, 128), np.float32)
    for blkk in range(4):
        r4T[32 * blkk:32 * blkk + 32, 32 * blkk:32 * blkk + 32] = r4
    iota7 = np.broadcast_to(np.arange(8, dtype=np.float32), (128, 8)).copy()
    epsv = np.zeros((64, 8), np.float32)
    epsv[:, 0] = 1e-5; epsv[:, 1] = 1e-6; epsv[:, 2] = 1e-24
    esel = np.zeros((8, 1024), bf)
    for e in range(8):
        esel[e, 128 * e:128 * e + 128] = 1
    inv = 1.0 / 10000.0 ** (np.arange(0, RD, 2, dtype=np.float32) / RD)
    tpos = np.arange(seq, dtype=np.float32)
    fr = tpos[:, None] * inv[None, :]
    emb = np.concatenate([fr, fr], -1)
    cos1 = np.cos(emb).T.astype(np.float32)
    sin1 = np.sin(emb).T.astype(np.float32)
    cos_t = np.tile(cos1, (4, 1)); sin_t = np.tile(sin1, (4, 1))

    woT = _cc(A("L0_attn_wo_w").T)
    wo_b = _cc(A("L0_attn_wo_b").reshape(8, 128).T)
    post_nw = _cc(A("L0_attn_post_norm_w").reshape(8, 128).T)
    post_nb = _cc(A("L0_attn_post_norm_b").reshape(8, 128).T)
    ln1_w = _cc(A("L0_ffn_ln1_w").reshape(8, 128).T)
    ln2_w = _cc(A("L0_ffn_ln2_w").reshape(8, 128).T)
    kv_dT = _cc(A("L0_ffn_kv_d").T); q_dT = _cc(A("L0_ffn_q_d").T)
    v_u = A("L0_ffn_v_u"); k_u = A("L0_ffn_k_u"); q_u = A("L0_ffn_q_u")
    rope_q = A("L0_ffn_rope_q_w"); rope_k = A("L0_ffn_rope_k_w")
    o_wT = _cc(A("L0_ffn_o_w").T)
    r_wT = np.zeros((D, 8), np.float32)
    r_wT[:, 0:7] = A("L0_ffn_router_w").T
    r_b = np.zeros((1, 8), np.float32)
    r_b[0, 0:7] = A("L0_ffn_routing_bias")
    r_b[0, 7] = -1.0e9
    moe_gT = np.zeros((NE, D, EI), np.float32)
    moe_uT = np.zeros((NE, D, EI), np.float32)
    moe_dT = np.zeros((NE, EI, D), np.float32)
    moe_gT[0] = A("L0_ffn_shared_gate")[0].T
    moe_uT[0] = A("L0_ffn_shared_up")[0].T
    moe_dT[0] = A("L0_ffn_shared_down")[0].T
    for e in range(7):
        moe_gT[e + 1] = A("L0_ffn_routed_gate")[e].T
        moe_uT[e + 1] = A("L0_ffn_routed_up")[e].T
        moe_dT[e + 1] = A("L0_ffn_routed_down")[e].T

    wq, wk, wv = A("L0_attn_wq_w"), A("L0_attn_wk_w"), A("L0_attn_wv_w")
    wqb, wkb, wvb = A("L0_attn_wq_b"), A("L0_attn_wk_b"), A("L0_attn_wv_b")
    gw = A("L0_attn_ttt_norm_weight"); gb = A("L0_attn_ttt_norm_bias")
    lrw = A("L0_attn_ttt_lr_weight"); lrb = A("L0_attn_ttt_lr_bias")
    W1 = A("L0_attn_W1"); b1 = A("L0_attn_b1")

    xfm = np.ascontiguousarray(x.transpose(0, 2, 1))

    in_maps = []
    for c in range(8):
        b, q = c // 4, c % 4
        hsl = slice(256 * q, 256 * q + 256)
        hh = slice(4 * q, 4 * q + 4)
        m = dict(
            x_own=_cc(xfm[b][:, nto * q:nto * q + nto]),
            wqT=_cc(wq[hsl].T), wkT=_cc(wk[hsl].T), wvT=_cc(wv[hsl].T),
            qkv_b=_cc(np.stack([wqb[hsl].reshape(2, 128).T,
                                wkb[hsl].reshape(2, 128).T,
                                wvb[hsl].reshape(2, 128).T], axis=-1)),
            lrwT=_cc(lrw[hh, 0, :].T), lr_b=_cc(lrb[hh].reshape(4, 1)),
            ttt_g2=_cc(gw[hh].reshape(256).reshape(2, 128).T),
            ttt_b2=_cc(gb[hh].reshape(256).reshape(2, 128).T),
            ttt_g_tm=np.broadcast_to(gw[hh].reshape(256), (64, 256)).copy(),
            ttt_b_tm=np.broadcast_to(gb[hh].reshape(256), (64, 256)).copy(),
            W2_0=_blkdiag(W1[4 * q], W1[4 * q + 1]),
            W2_1=_blkdiag(W1[4 * q + 2], W1[4 * q + 3]),
            b2_bc=np.broadcast_to(b1[hh].reshape(256), (64, 256)).copy(),
            post_nw=post_nw, post_nb=post_nb, woT=woT, wo_b=wo_b,
            ln1_w=ln1_w, ln2_w=ln2_w, kv_dT=kv_dT, q_dT=q_dT,
            k_uT=_cc(k_u.reshape(H, HD, 256)[hh, 32:64].reshape(128, 256).T),
            q_uT=_cc(q_u.reshape(H, HD, 256)[hh, 32:64].reshape(128, 256).T),
            v_uT=_cc(v_u.reshape(H, HD, 256)[hh].reshape(256, 256).T),
            rope_qT=_cc(rope_q.reshape(H, RD, 256)[hh].reshape(128, 256).T),
            rope_kT=_cc(rope_k.reshape(H, RD, D)[hh].reshape(128, D).T),
            o_wT=o_wT, cos_t=cos_t, sin_t=sin_t, r_wT=r_wT, r_b=r_b,
            moe_gT=moe_gT, moe_uT=moe_uT, moe_dT=moe_dT,
            ident_f=ident_f, ident_b=ident_b, triuT_b=triuT_b, i64z=i64z,
            triuTz=triuTz, ones64=ones64, trimask4=trimask4,
            e2_b=e2_b, eb2_b=eb2_b, onesr_b=onesr_b, onesr_f=onesr_f,
            onescol_b=onescol_b, wmask=wmask, amask=amask, r4T=r4T, iota7=iota7,
            epsv=epsv, esel=esel,
        )
        in_maps.append(m)
    return in_maps


def _blkdiag(a, bmat):
    W = np.zeros((128, 128), np.float32)
    W[0:64, 0:64] = a
    W[64:128, 64:128] = bmat
    return W


_BUILT = {}


def kernel(**inputs):
    seq = int(np.asarray(inputs["L0_attn_hidden_states"]).shape[1])
    key = seq
    if key not in _BUILT:
        _BUILT[key] = build(seq=seq)
    nc, _ = _BUILT[key]
    in_maps = prep_in_maps(inputs, seq=seq)
    res = run_bass_kernel_spmd(nc, in_maps, core_ids=list(range(8)))
    nto = seq // 4
    out = np.zeros((B, seq, D), np.float32)
    for c in range(8):
        b, q = c // 4, c % 4
        out[b, nto * q:nto * q + nto, :] = np.asarray(res.results[c]["out"]).T
    return out
